# version 06
# speedup vs baseline: 1.5170x; 1.0688x over previous
import contextlib
import numpy as np
import ml_dtypes
import concourse.bass as bass
import concourse.mybir as mybir
from concourse.bass_utils import run_bass_kernel_spmd

F32 = mybir.dt.float32
BF16 = mybir.dt.bfloat16
AF = mybir.ActivationFunctionType
ALU = mybir.AluOpType
AX = mybir.AxisListType.X
BF = ml_dtypes.bfloat16

NCORES = 8
B, TF, C = 4, 4096, 1024
TC = 2048
TT = 512
NT = TC // TT
TH = 2560
NTH = TH // TT
ENGS = ("sync", "tensor", "vector", "scalar", "gpsimd")
HALF_PI = float(np.pi / 2)
CSC_SHIFT = float(0.5 * np.log(512.0))
NEG_BIG = -3.0e38


class PB:

    def __init__(self, nc, es):
        self.nc = nc
        self.block = es.enter_context(nc.Block())
        self.sems = {e: es.enter_context(nc.semaphore("pb_" + e)) for e in ENGS}
        self.cnt = {e: 0 for e in ENGS}

    def phase(self, **fns):
        pre = dict(self.cnt)
        for ename, fn in fns.items():
            sem = self.sems[ename]
            added = [0]

            def body(eng, fn=fn, ename=ename, sem=sem, added=added, pre=pre):
                for o in ENGS:
                    if o != ename and pre[o] > 0:
                        eng.wait_ge(self.sems[o], pre[o])

                def inc(inst, n=1, wait=False):
                    inst.then_inc(sem, n)
                    added[0] += n
                    if wait:
                        eng.wait_ge(sem, pre[ename] + added[0])

                fn(eng, inc)

            getattr(self.block, ename)(body)
            assert added[0] > 0, f"phase body on {ename} incremented nothing"
            self.cnt[ename] += added[0]

    def init_hand_sems(self, es):
        self.h = {nm: es.enter_context(self.nc.semaphore("h_" + nm))
                  for nm in ("ld", "mm", "ev", "st", "pv", "chn")}
        self.hc = {nm: 0 for nm in self.h}

    def chain(self, eng, inst):
        self.hc["chn"] += 1
        inst.then_inc(self.h["chn"], 1)
        eng.wait_ge(self.h["chn"], self.hc["chn"])
        return inst

    def pipe2(self, n, mm, ev=None, ld=None, st=None, mm_eng="tensor",
              ev_eng="vector", ld_units=1):
        pre = dict(self.cnt)
        h, hc = self.h, self.hc
        ld0, mm0, ev0, st0 = hc["ld"], hc["mm"], hc["ev"], hc["st"]

        def waits(eng, ename):
            for o in ENGS:
                if o != ename and pre[o] > 0:
                    eng.wait_ge(self.sems[o], pre[o])

        if ld is not None:
            def body_ld(eng):
                waits(eng, "sync")
                for i in range(n):
                    if i >= 2:
                        eng.wait_ge(h["mm"], mm0 + i - 1)
                    ld(eng, i, i % 2).then_inc(h["ld"], 16)
                eng.nop().then_inc(self.sems["sync"], 1)
            self.block.sync(body_ld)
            hc["ld"] += 16 * ld_units * n
            self.cnt["sync"] += 1

        def body_mm(eng):
            waits(eng, mm_eng)
            for i in range(n):
                if ld is not None:
                    eng.wait_ge(h["ld"], ld0 + 16 * ld_units * (i + 1))
                if ev is not None and i >= 2:
                    eng.wait_ge(h["ev"], ev0 + i - 1)
                mm(eng, i, i % 2).then_inc(h["mm"], 1)
        getattr(self.block, mm_eng)(body_mm)
        hc["mm"] += n
        self.cnt[mm_eng] += 0

        if ev is not None:
            def body_ev(eng):
                waits(eng, ev_eng)
                for i in range(n):
                    eng.wait_ge(h["mm"], mm0 + i + 1)
                    if st is not None and i >= 2:
                        eng.wait_ge(h["st"], st0 + 16 * (i - 1))
                    ev(eng, i, i % 2).then_inc(h["ev"], 1)
            getattr(self.block, ev_eng)(body_ev)
            hc["ev"] += n

        if st is not None:
            def body_st(eng):
                waits(eng, "gpsimd")
                for i in range(n):
                    eng.wait_ge(h["ev"], ev0 + i + 1)
                    st(eng, i, i % 2).then_inc(h["st"], 16)
            self.block.gpsimd(body_st)
            hc["st"] += 16 * n

        def mark(ename, sem_val):
            def body(eng):
                eng.wait_ge(self.h[sem_val[0]], sem_val[1])
                eng.nop().then_inc(self.sems[ename], 1)
            getattr(self.block, ename)(body)
            self.cnt[ename] += 1

        mark(mm_eng, ("mm", hc["mm"]))
        if ev is not None:
            mark(ev_eng, ("ev", hc["ev"]))
        if st is not None:
            mark("gpsimd", ("st", hc["st"]))
        if ld is not None:
            pass

    def finish(self):
        pre = dict(self.cnt)

        def body(eng):
            for o in ENGS:
                if pre[o] > 0:
                    eng.wait_ge(self.sems[o], pre[o])

        self.block.sync(body)


def ch(t, c, W):
    return t[:, c * W:(c + 1) * W]


def load_fm(eng, inc, sbuf, dram, nch, W, t0):
    for c in range(nch):
        inc(eng.dma_start(out=ch(sbuf, c, W), in_=dram[c * 128:(c + 1) * 128, t0:t0 + W]), 16)


def store_fm(eng, inc, dram, sbuf, nch, W, t0):
    for c in range(nch):
        inc(eng.dma_start(out=dram[c * 128:(c + 1) * 128, t0:t0 + W], in_=ch(sbuf, c, W)), 16)


class LNKit:

    def __init__(self, nc, es, tag, nch, W):
        self.nch, self.W = nch, W
        self.xbf = es.enter_context(nc.sbuf_tensor(f"{tag}_xbf", [128, nch * W], BF16))
        self.d = es.enter_context(nc.sbuf_tensor(f"{tag}_d", [128, nch * W], F32))
        self.rstd_a = es.enter_context(nc.sbuf_tensor(f"{tag}_ra", [128, W], F32))
        self.rstd = es.enter_context(nc.sbuf_tensor(f"{tag}_rb", [128, W], F32))


def emit_ln(pp, kit, consts, src, pa, pb, out_bf=None, out_f32=None, eps="eps5"):
    nch, W = kit.nch, kit.W
    Cn = nch * 128
    ones_bf = consts["ones_bf"]

    def cast(v, inc):
        for c in range(nch):
            ins = v.tensor_copy(out=ch(kit.xbf, c, W), in_=ch(src, c, W))
        inc(ins)
    pp.phase(vector=cast)

    def mm_mu(t, inc):
        for c in range(nch):
            ins = t.matmul(pa[:, 0:W], ones_bf, ch(kit.xbf, c, W),
                           start=(c == 0), stop=(c == nch - 1))
        inc(ins)
    pp.phase(tensor=mm_mu)

    def sub(v, inc):
        for c in range(nch):
            ins = v.scalar_tensor_tensor(out=ch(kit.d, c, W), in0=pa[:, 0:W],
                                         scalar=-1.0 / Cn, in1=ch(src, c, W),
                                         op0=ALU.mult, op1=ALU.add)
        inc(ins)
    pp.phase(vector=sub)

    def sq(s, inc):
        for c in range(nch):
            ins = s.activation(out=ch(kit.xbf, c, W), in_=ch(kit.d, c, W), func=AF.Square)
        inc(ins)
    pp.phase(scalar=sq)

    def mm_var(t, inc):
        for c in range(nch):
            ins = t.matmul(pb[:, 0:W], ones_bf, ch(kit.xbf, c, W),
                           start=(c == 0), stop=(c == nch - 1))
        inc(ins)
    pp.phase(tensor=mm_var)

    def rstd(s, inc):
        s.activation(out=kit.rstd_a[:], in_=pb[:, 0:W], func=AF.Ln, scale=1.0 / Cn,
                     bias=consts[eps])
        ins = s.activation(out=kit.rstd[:], in_=kit.rstd_a[:], func=AF.Exp, scale=-0.5)
        inc(ins)
    pp.phase(scalar=rstd)

    if out_bf is not None or out_f32 is not None:
        def apply(v, inc):
            for c in range(nch):
                if out_bf is not None:
                    ins = v.tensor_tensor(out=ch(out_bf, c, W), in0=ch(kit.d, c, W),
                                          in1=kit.rstd[:], op=ALU.mult)
                if out_f32 is not None:
                    ins = v.tensor_tensor(out=ch(out_f32, c, W), in0=ch(kit.d, c, W),
                                          in1=kit.rstd[:], op=ALU.mult)
            inc(ins)
        pp.phase(vector=apply)


def make_consts(nc, es, pp, specs):
    consts = {}
    vals = {}
    for name, v in specs.items():
        t = es.enter_context(nc.sbuf_tensor("c_" + name, [128, 1], F32))
        consts[name] = t[:]
        vals[name] = v
    ones_bf = es.enter_context(nc.sbuf_tensor("c_onesbf", [128, 128], BF16))
    ones_row = es.enter_context(nc.sbuf_tensor("c_onesrow", [1, 128], F32))
    consts["ones_bf"] = ones_bf[:]
    consts["ones_row"] = ones_row[:]

    def setc(v, inc):
        for name, val in vals.items():
            v.memset(consts[name], float(val))
        v.memset(ones_bf[:], 1.0)
        ins = v.memset(ones_row[:], 1.0)
        inc(ins)
    pp.phase(vector=setc)
    return consts


def cast_weight(pp, dram, rows, cols, stage, out_bf):
    nk = rows // 128

    def load(e, inc):
        for kc in range(nk):
            inc(e.dma_start(out=stage[:, kc * cols:(kc + 1) * cols],
                            in_=dram[kc * 128:(kc + 1) * 128, :]), 16)
    pp.phase(sync=load)

    def cast(v, inc):
        ins = v.tensor_copy(out=out_bf[:, 0:nk * cols], in_=stage[:, 0:nk * cols])
        inc(ins)
    pp.phase(vector=cast)



def build_l1_gates():
    nc = bass.Bass("TRN2", target_bir_lowering=False)
    xs = nc.dram_tensor("xs", [C, TC], F32, kind="ExternalInput").ap()
    wiT = nc.dram_tensor("wiT", [C, C], F32, kind="ExternalInput").ap()
    wrT = nc.dram_tensor("wrT", [C, C], F32, kind="ExternalInput").ap()
    bi = nc.dram_tensor("bi", [128, 8], F32, kind="ExternalInput").ap()
    br = nc.dram_tensor("br", [128, 8], F32, kind="ExternalInput").ap()
    lam = nc.dram_tensor("lam", [1, 1], F32, kind="ExternalInput").ap()
    at_o = nc.dram_tensor("at_o", [C, TC], F32, kind="ExternalOutput").ap()
    gx_o = nc.dram_tensor("gx_o", [C, TC], F32, kind="ExternalOutput").ap()

    with contextlib.ExitStack() as es:
        pp = PB(nc, es)
        stage = es.enter_context(nc.sbuf_tensor("stage", [128, 8 * C], F32))
        wi_bf = es.enter_context(nc.sbuf_tensor("wi_bf", [128, 8 * C], BF16))
        wr_bf = es.enter_context(nc.sbuf_tensor("wr_bf", [128, 8 * C], BF16))
        bi_sb = es.enter_context(nc.sbuf_tensor("bi_sb", [128, 8], F32))
        br_sb = es.enter_context(nc.sbuf_tensor("br_sb", [128, 8], F32))
        lam_sb = es.enter_context(nc.sbuf_tensor("lam_sb", [1, 1], F32))
        sp_t = es.enter_context(nc.sbuf_tensor("sp_t", [1, 1], F32))
        sp_t2 = es.enter_context(nc.sbuf_tensor("sp_t2", [1, 1], F32))
        s8_bc = es.enter_context(nc.sbuf_tensor("s8_bc", [128, 1], F32))
        x_t = es.enter_context(nc.sbuf_tensor("x_t", [128, 8 * TT], F32))
        xin_bf = es.enter_context(nc.sbuf_tensor("xin_bf", [128, 8 * TT], BF16))
        it_f = es.enter_context(nc.sbuf_tensor("it_f", [128, 8 * TT], F32))
        at_f = es.enter_context(nc.sbuf_tensor("at_f", [128, 8 * TT], F32))
        gx_f = es.enter_context(nc.sbuf_tensor("gx_f", [128, 8 * TT], F32))
        t1 = es.enter_context(nc.sbuf_tensor("t1", [128, TT], F32))
        t2 = es.enter_context(nc.sbuf_tensor("t2", [128, TT], F32))
        kit = LNKit(nc, es, "ln", 8, TT)
        pa = es.enter_context(nc.psum_tensor("pa", [128, TT], F32))
        pb = es.enter_context(nc.psum_tensor("pb", [128, TT], F32))
        pd = es.enter_context(nc.psum_tensor("pd", [128, 1], F32))

        consts = make_consts(nc, es, pp, {"eps5": 1e-5, "one": 1.0})

        cast_weight(pp, wiT, C, C, stage, wi_bf)
        cast_weight(pp, wrT, C, C, stage, wr_bf)

        def load_small(e, inc):
            inc(e.dma_start(out=bi_sb[:], in_=bi), 16)
            inc(e.dma_start(out=br_sb[:], in_=br), 16)
            inc(e.dma_start(out=lam_sb[:], in_=lam), 16)
        pp.phase(sync=load_small)

        def sp1(s, inc):
            inc(s.activation(out=sp_t[:], in_=lam_sb[:], func=AF.Abs), wait=True)
            inc(s.activation(out=sp_t[:], in_=sp_t[:], func=AF.Exp, scale=-1.0), wait=True)
            inc(s.activation(out=sp_t[:], in_=sp_t[:], func=AF.Ln,
                             bias=consts["one"][0:1, :]), wait=True)
            inc(s.activation(out=sp_t2[:], in_=lam_sb[:], func=AF.Relu))
        pp.phase(scalar=sp1)

        def sp2(v, inc):
            inc(v.tensor_tensor(out=sp_t2[:], in0=sp_t2[:], in1=sp_t[:], op=ALU.add),
                wait=True)
            inc(v.tensor_scalar(out=sp_t2[:], in0=sp_t2[:], scalar1=-8.0,
                                scalar2=None, op0=ALU.mult))
        pp.phase(vector=sp2)

        def sp3(t, inc):
            ins = t.matmul(pd[:, 0:1], consts["ones_row"], sp_t2[:], start=True, stop=True)
            inc(ins)
        pp.phase(tensor=sp3)

        def sp4(v, inc):
            ins = v.tensor_copy(out=s8_bc[:], in_=pd[:, 0:1])
            inc(ins)
        pp.phase(vector=sp4)

        for tt in range(NT):
            t0 = tt * TT

            def load_x(e, inc, t0=t0):
                load_fm(e, inc, x_t, xs, 8, TT, t0)
            pp.phase(sync=load_x)

            emit_ln(pp, kit, consts, x_t, pa, pb, out_bf=xin_bf)

            for which in ("i", "r"):
                w_bf = wi_bf if which == "i" else wr_bf
                b_sb = bi_sb if which == "i" else br_sb
                for dc in range(8):
                    ps = pa if dc % 2 == 0 else pb

                    def mm(t, inc, dc=dc, ps=ps, w_bf=w_bf):
                        for kc in range(8):
                            ins = t.matmul(
                                ps[:, 0:TT],
                                w_bf[:, kc * C + dc * 128: kc * C + (dc + 1) * 128],
                                ch(xin_bf, kc, TT), start=(kc == 0), stop=(kc == 7))
                        inc(ins)
                    pp.phase(tensor=mm)

                    if which == "i":
                        def ev(s, inc, dc=dc, ps=ps, b_sb=b_sb):
                            ins = s.activation(out=ch(it_f, dc, TT), in_=ps[:, 0:TT],
                                               func=AF.Sigmoid, bias=b_sb[:, dc:dc + 1])
                            inc(ins)
                        pp.phase(scalar=ev)
                    else:
                        def ev(s, inc, dc=dc, ps=ps, b_sb=b_sb):
                            s.activation(out=t1[:], in_=ps[:, 0:TT],
                                         func=AF.Sigmoid, bias=b_sb[:, dc:dc + 1])
                            ins = s.activation(out=ch(at_f, dc, TT), in_=t1[:],
                                               func=AF.Exp, scale=s8_bc[:])
                            inc(ins)
                        pp.phase(scalar=ev)

            for dc in range(8):
                def g1(s, inc, dc=dc):
                    s.activation(out=t1[:], in_=ch(at_f, dc, TT), func=AF.Square)
                    s.activation(out=t1[:], in_=t1[:], func=AF.Ln, scale=-1.0,
                                 bias=consts["one"])
                    ins = s.activation(out=t2[:], in_=t1[:], func=AF.Exp, scale=0.5)
                    inc(ins)
                pp.phase(scalar=g1)

                def g2(v, inc, dc=dc):
                    v.tensor_tensor(out=ch(gx_f, dc, TT), in0=ch(it_f, dc, TT),
                                    in1=ch(xin_bf, dc, TT), op=ALU.mult)
                    ins = v.tensor_tensor(out=ch(gx_f, dc, TT), in0=ch(gx_f, dc, TT),
                                          in1=t2[:], op=ALU.mult)
                    inc(ins)
                pp.phase(vector=g2)

            def store(e, inc, t0=t0):
                store_fm(e, inc, at_o, at_f, 8, TT, t0)
                store_fm(e, inc, gx_o, gx_f, 8, TT, t0)
            pp.phase(gpsimd=store)

        pp.finish()
    return nc



def build_l2_scan():
    nc = bass.Bass("TRN2", target_bir_lowering=False)
    at_i = nc.dram_tensor("at_i", [C // 2, TF], F32, kind="ExternalInput").ap()
    gx_i = nc.dram_tensor("gx_i", [C // 2, TF], F32, kind="ExternalInput").ap()
    ys_o = nc.dram_tensor("ys_o", [C // 2, TF], F32, kind="ExternalOutput").ap()
    with contextlib.ExitStack() as es:
        pp = PB(nc, es)
        a_t = es.enter_context(nc.sbuf_tensor("a_t", [128, TF], F32))
        g_t = es.enter_context(nc.sbuf_tensor("g_t", [128, TF], F32))
        y_t = es.enter_context(nc.sbuf_tensor("y_t", [128, TF], F32))
        for c in range(4):
            def load(e, inc, c=c):
                inc(e.dma_start(out=a_t[:], in_=at_i[c * 128:(c + 1) * 128, :]), 16)
                inc(e.dma_start(out=g_t[:], in_=gx_i[c * 128:(c + 1) * 128, :]), 16)
            pp.phase(sync=load)

            def scan(v, inc):
                ins = v.tensor_tensor_scan(out=y_t[:], data0=a_t[:], data1=g_t[:],
                                           initial=0.0, op0=ALU.mult, op1=ALU.add)
                inc(ins)
            pp.phase(vector=scan)

            def store(e, inc, c=c):
                inc(e.dma_start(out=ys_o[c * 128:(c + 1) * 128, :], in_=y_t[:]), 16)
            pp.phase(gpsimd=store)
        pp.finish()
    return nc



def build_l3_ffn0():
    nc = bass.Bass("TRN2", target_bir_lowering=False)
    xs = nc.dram_tensor("xs", [C, TC], F32, kind="ExternalInput").ap()
    ys = nc.dram_tensor("ys", [C, TC], F32, kind="ExternalInput").ap()
    th = nc.dram_tensor("th", [C, C], F32, kind="ExternalInput").ap()
    thn = nc.dram_tensor("thn", [C, C], F32, kind="ExternalInput").ap()
    ph = nc.dram_tensor("ph", [C, C], F32, kind="ExternalInput").ap()
    phn = nc.dram_tensor("phn", [C, C], F32, kind="ExternalInput").ap()
    x2_o = nc.dram_tensor("x2_o", [C, TC], F32, kind="ExternalOutput").ap()
    with contextlib.ExitStack() as es:
        pp = PB(nc, es)
        stage_a = es.enter_context(nc.sbuf_tensor("stage_a", [128, 8 * C], F32))
        stage_b = es.enter_context(nc.sbuf_tensor("stage_b", [128, 8 * C], F32))
        cos_bf = es.enter_context(nc.sbuf_tensor("cos_bf", [128, 8 * C], BF16))
        sin_bf = es.enter_context(nc.sbuf_tensor("sin_bf", [128, 8 * C], BF16))
        x1_t = es.enter_context(nc.sbuf_tensor("x1_t", [128, 8 * TT], F32))
        ys_t = es.enter_context(nc.sbuf_tensor("ys_t", [128, 8 * TT], F32))
        fin_bf = es.enter_context(nc.sbuf_tensor("fin_bf", [128, 8 * TT], BF16))
        r_f = es.enter_context(nc.sbuf_tensor("r_f", [128, 8 * TT], F32))
        sr = es.enter_context(nc.sbuf_tensor("sr", [128, TT], F32))
        si = es.enter_context(nc.sbuf_tensor("si", [128, TT], F32))
        kit = LNKit(nc, es, "ln", 8, TT)
        pa = es.enter_context(nc.psum_tensor("pa", [128, TT], F32))
        pb = es.enter_context(nc.psum_tensor("pb", [128, TT], F32))
        consts = make_consts(nc, es, pp, {"eps5": 1e-5, "one": 1.0, "halfpi": HALF_PI})

        for (wa, wb, out_bf, bias) in ((th, thn, cos_bf, "halfpi"),
                                       (ph, phn, sin_bf, None)):
            def load(e, inc, wa=wa, wb=wb):
                for kc in range(8):
                    inc(e.dma_start(out=stage_a[:, kc * C:(kc + 1) * C],
                                    in_=wa[kc * 128:(kc + 1) * 128, :]), 16)
                    inc(e.dma_start(out=stage_b[:, kc * C:(kc + 1) * C],
                                    in_=wb[kc * 128:(kc + 1) * 128, :]), 16)
            pp.phase(sync=load)

            def add(v, inc):
                ins = v.tensor_tensor(out=stage_a[:], in0=stage_a[:], in1=stage_b[:],
                                      op=ALU.add)
                inc(ins)
            pp.phase(vector=add)

            def trig(s, inc, out_bf=out_bf, bias=bias):
                if bias is None:
                    ins = s.activation(out=out_bf[:], in_=stage_a[:], func=AF.Sin)
                else:
                    ins = s.activation(out=out_bf[:], in_=stage_a[:], func=AF.Sin,
                                       bias=consts[bias])
                inc(ins)
            pp.phase(scalar=trig)

        for tt in range(NT):
            t0 = tt * TT

            def load_x(e, inc, t0=t0):
                load_fm(e, inc, x1_t, xs, 8, TT, t0)
                load_fm(e, inc, ys_t, ys, 8, TT, t0)
            pp.phase(sync=load_x)

            emit_ln(pp, kit, consts, ys_t, pa, pb, out_f32=ys_t)

            def addres(v, inc):
                for c in range(8):
                    ins = v.tensor_tensor(out=ch(x1_t, c, TT), in0=ch(x1_t, c, TT),
                                          in1=ch(ys_t, c, TT), op=ALU.add)
                inc(ins)
            pp.phase(vector=addres)

            emit_ln(pp, kit, consts, x1_t, pa, pb, out_bf=fin_bf)

            for dc in range(8):
                def mm(t, inc, dc=dc):
                    for kc in range(8):
                        t.matmul(pa[:, 0:TT],
                                 cos_bf[:, kc * C + dc * 128: kc * C + (dc + 1) * 128],
                                 ch(fin_bf, kc, TT), start=(kc == 0), stop=(kc == 7))
                    for kc in range(8):
                        ins = t.matmul(
                            pb[:, 0:TT],
                            sin_bf[:, kc * C + dc * 128: kc * C + (dc + 1) * 128],
                            ch(fin_bf, kc, TT), start=(kc == 0), stop=(kc == 7))
                    inc(ins)
                pp.phase(tensor=mm)

                def sq(s, inc):
                    s.activation(out=sr[:], in_=pa[:, 0:TT], func=AF.Square)
                    ins = s.activation(out=si[:], in_=pb[:, 0:TT], func=AF.Square)
                    inc(ins)
                pp.phase(scalar=sq)

                def add2(v, inc):
                    ins = v.tensor_tensor(out=sr[:], in0=sr[:], in1=si[:], op=ALU.add)
                    inc(ins)
                pp.phase(vector=add2)

                def root(s, inc, dc=dc):
                    s.activation(out=sr[:], in_=sr[:], func=AF.Ln)
                    ins = s.activation(out=ch(r_f, dc, TT), in_=sr[:], func=AF.Exp,
                                       scale=0.5)
                    inc(ins)
                pp.phase(scalar=root)

            emit_ln(pp, kit, consts, r_f, pa, pb, out_f32=r_f)

            def addres2(v, inc):
                for c in range(8):
                    ins = v.tensor_tensor(out=ch(x1_t, c, TT), in0=ch(x1_t, c, TT),
                                          in1=ch(r_f, c, TT), op=ALU.add)
                inc(ins)
            pp.phase(vector=addres2)

            def store(e, inc, t0=t0):
                store_fm(e, inc, x2_o, x1_t, 8, TT, t0)
            pp.phase(gpsimd=store)
        pp.finish()
    return nc



def build_l4a_up():
    nc = bass.Bass("TRN2", target_bir_lowering=False)
    x2h = nc.dram_tensor("x2h", [C, TH], F32, kind="ExternalInput").ap()
    wup = nc.dram_tensor("wup", [C, 4096], F32, kind="ExternalInput").ap()
    xml_o = nc.dram_tensor("xml_o", [2048, TH], BF16, kind="ExternalOutput").ap()
    z_o = nc.dram_tensor("z_o", [2048, TC], BF16, kind="ExternalOutput").ap()
    with contextlib.ExitStack() as es:
        pp = PB(nc, es)
        stage = es.enter_context(nc.sbuf_tensor("stage", [128, 8 * 2048], F32))
        wh_bf = es.enter_context(nc.sbuf_tensor("wh_bf", [128, 8 * 2048], BF16))
        x_t = es.enter_context(nc.sbuf_tensor("x_t", [128, 8 * TT], F32))
        fin_bf = es.enter_context(nc.sbuf_tensor("fin_bf", [128, 8 * TH], BF16))
        out_bf = es.enter_context(nc.sbuf_tensor("out_bf", [128, TT], BF16))
        kit = LNKit(nc, es, "ln", 8, TT)
        pa = es.enter_context(nc.psum_tensor("pa", [128, TT], F32))
        pb = es.enter_context(nc.psum_tensor("pb", [128, TT], F32))
        consts = make_consts(nc, es, pp, {"eps5": 1e-5, "one": 1.0})

        for tt in range(NTH):
            t0 = tt * TT

            def load_x(e, inc, t0=t0):
                load_fm(e, inc, x_t, x2h, 8, TT, t0)
            pp.phase(sync=load_x)

            emit_ln(pp, kit, consts, x_t, pa, pb)

            def apply(v, inc, t0=t0):
                for c in range(8):
                    ins = v.tensor_tensor(out=fin_bf[:, c * TH + t0: c * TH + t0 + TT],
                                          in0=ch(kit.d, c, TT), in1=kit.rstd[:],
                                          op=ALU.mult)
                inc(ins)
            pp.phase(vector=apply)

        for half, (dram_out, ntt, toff, wcol0) in enumerate(
                ((z_o, NT, 512, 2048), (xml_o, NTH, 0, 0))):
            def loadw(e, inc, wcol0=wcol0):
                for kc in range(8):
                    inc(e.dma_start(out=stage[:, kc * 2048:(kc + 1) * 2048],
                                    in_=wup[kc * 128:(kc + 1) * 128, wcol0:wcol0 + 2048]), 16)
            pp.phase(sync=loadw)

            def castw(v, inc):
                ins = v.tensor_copy(out=wh_bf[:], in_=stage[:])
                inc(ins)
            pp.phase(vector=castw)

            for tt in range(ntt):
                t0 = toff + tt * TT
                for dc in range(16):
                    ps = pa if dc % 2 == 0 else pb

                    def mm(t, inc, dc=dc, ps=ps, t0=t0):
                        for kc in range(8):
                            ins = t.matmul(
                                ps[:, 0:TT],
                                wh_bf[:, kc * 2048 + dc * 128: kc * 2048 + (dc + 1) * 128],
                                fin_bf[:, kc * TH + t0: kc * TH + t0 + TT],
                                start=(kc == 0), stop=(kc == 7))
                        inc(ins)
                    pp.phase(tensor=mm)

                    def ev(v, inc, ps=ps):
                        ins = v.tensor_copy(out=out_bf[:], in_=ps[:, 0:TT])
                        inc(ins)
                    pp.phase(vector=ev)

                    def st(e, inc, dc=dc, t0=t0, dram_out=dram_out, toff=toff):
                        inc(e.dma_start(out=dram_out[dc * 128:(dc + 1) * 128,
                                                     t0 - toff: t0 - toff + TT],
                                        in_=out_bf[:]), 16)
                    pp.phase(gpsimd=st)
        pp.finish()
    return nc



def build_l4b_qkv():
    nc = bass.Bass("TRN2", target_bir_lowering=False)
    xml = nc.dram_tensor("xml", [2048, TH], BF16, kind="ExternalInput").ap()
    bq = nc.dram_tensor("bq", [16 * 128, 128], F32, kind="ExternalInput").ap()
    bk = nc.dram_tensor("bk", [16 * 128, 128], F32, kind="ExternalInput").ap()
    bv = nc.dram_tensor("bv", [16 * 128, 128], F32, kind="ExternalInput").ap()
    bqT = nc.dram_tensor("bqT", [16 * 128, 128], F32, kind="ExternalInput").ap()
    bkT = nc.dram_tensor("bkT", [16 * 128, 128], F32, kind="ExternalInput").ap()
    bvT = nc.dram_tensor("bvT", [16 * 128, 128], F32, kind="ExternalInput").ap()
    wig_q = nc.dram_tensor("wig_q", [2048, 8], F32, kind="ExternalInput").ap()
    wig_k = nc.dram_tensor("wig_k", [2048, 8], F32, kind="ExternalInput").ap()
    wig_v = nc.dram_tensor("wig_v", [2048, 8], F32, kind="ExternalInput").ap()
    bg = nc.dram_tensor("bg", [8, 1], F32, kind="ExternalInput").ap()
    cw = nc.dram_tensor("cw", [128, 64], F32, kind="ExternalInput").ap()
    cb = nc.dram_tensor("cb", [128, 16], F32, kind="ExternalInput").ap()
    q_o = nc.dram_tensor("q_o", [2048, TC], BF16, kind="ExternalOutput").ap()
    k_o = nc.dram_tensor("k_o", [2048, TC], BF16, kind="ExternalOutput").ap()
    v_o = nc.dram_tensor("v_o", [TC, 2048], BF16, kind="ExternalOutput").ap()
    xact_o = nc.dram_tensor("xact_o", [2048, TC], BF16, kind="ExternalOutput").ap()
    ig_o = nc.dram_tensor("ig_o", [4, TC], F32, kind="ExternalOutput").ap()
    fg_o = nc.dram_tensor("fg_o", [4, TC], F32, kind="ExternalOutput").ap()

    with contextlib.ExitStack() as es:
        pp = PB(nc, es)
        xml_sb = es.enter_context(nc.sbuf_tensor("xml_sb", [128, 16 * TH], BF16))
        xact_sb = es.enter_context(nc.sbuf_tensor("xact_sb", [128, 16 * TC], BF16))
        bq_bf = es.enter_context(nc.sbuf_tensor("bq_bf", [128, 16 * 128], BF16))
        bk_bf = es.enter_context(nc.sbuf_tensor("bk_bf", [128, 16 * 128], BF16))
        bv_bf = es.enter_context(nc.sbuf_tensor("bv_bf", [128, 16 * 128], BF16))
        stage = es.enter_context(nc.sbuf_tensor("stage", [128, 16 * 128], F32))
        stage2 = es.enter_context(nc.sbuf_tensor("stage2", [128, 16 * 128], F32))
        wiga_sb = es.enter_context(nc.sbuf_tensor("wiga_sb", [128, 16 * 8], F32))
        wigb_sb = es.enter_context(nc.sbuf_tensor("wigb_sb", [128, 16 * 8], F32))
        a_act = es.enter_context(nc.sbuf_tensor("a_act", [128, 16 * 8], BF16))
        a_ml = es.enter_context(nc.sbuf_tensor("a_ml", [128, 16 * 8], BF16))
        bg_sb = es.enter_context(nc.sbuf_tensor("bg_sb", [8, 1], F32))
        cw_sb = es.enter_context(nc.sbuf_tensor("cw_sb", [128, 64], F32))
        cb_sb = es.enter_context(nc.sbuf_tensor("cb_sb", [128, 16], F32))
        cv = es.enter_context(nc.sbuf_tensor("cv", [128, TC], F32))
        ev_bf = es.enter_context(nc.sbuf_tensor("ev_bf", [128, TT], BF16))
        g_f = es.enter_context(nc.sbuf_tensor("g_f", [8, TC], F32))
        pa = es.enter_context(nc.psum_tensor("pa", [128, TT], F32))
        pb = es.enter_context(nc.psum_tensor("pb", [128, TT], F32))
        pg = es.enter_context(nc.psum_tensor("pg", [8, TT], F32))
        p128 = es.enter_context(nc.psum_tensor("p128", [128, 128], F32))
        consts = make_consts(nc, es, pp, {"one": 1.0})

        def load_all(e, inc):
            for c in range(16):
                inc(e.dma_start(out=xml_sb[:, c * TH:(c + 1) * TH],
                                in_=xml[c * 128:(c + 1) * 128, :]), 16)
            inc(e.dma_start(out=cw_sb[:], in_=cw), 16)
            inc(e.dma_start(out=cb_sb[:], in_=cb), 16)
            inc(e.dma_start(out=bg_sb[:], in_=bg), 16)
        pp.phase(sync=load_all)

        for src, dst in ((bq, bq_bf), (bk, bk_bf), (bv, bv_bf)):
            def loadw(e, inc, src=src):
                for c in range(16):
                    inc(e.dma_start(out=stage[:, c * 128:(c + 1) * 128],
                                    in_=src[c * 128:(c + 1) * 128, :]), 16)
            pp.phase(sync=loadw)

            def castw(v, inc, dst=dst):
                ins = v.tensor_copy(out=dst[:], in_=stage[:])
                inc(ins)
            pp.phase(vector=castw)

        def load_fold1(e, inc):
            for c in range(16):
                inc(e.dma_start(out=stage[:, c * 128:(c + 1) * 128],
                                in_=bqT[c * 128:(c + 1) * 128, :]), 16)
                inc(e.dma_start(out=stage2[:, c * 128:(c + 1) * 128],
                                in_=bkT[c * 128:(c + 1) * 128, :]), 16)
                inc(e.dma_start(out=wiga_sb[:, c * 8:(c + 1) * 8],
                                in_=wig_q[c * 128:(c + 1) * 128, :]), 16)
                inc(e.dma_start(out=wigb_sb[:, c * 8:(c + 1) * 8],
                                in_=wig_k[c * 128:(c + 1) * 128, :]), 16)
        pp.phase(sync=load_fold1)

        def fold1(t, inc):
            for c in range(16):
                t.matmul(p128[:, c * 8:(c + 1) * 8], stage[:, c * 128:(c + 1) * 128],
                         wiga_sb[:, c * 8:(c + 1) * 8], start=True, stop=False)
                ins = t.matmul(p128[:, c * 8:(c + 1) * 8],
                               stage2[:, c * 128:(c + 1) * 128],
                               wigb_sb[:, c * 8:(c + 1) * 8], start=False, stop=True)
            inc(ins)
        pp.phase(tensor=fold1)

        def evf1(v, inc):
            ins = v.tensor_copy(out=a_act[:], in_=p128[:, 0:128])
            inc(ins)
        pp.phase(vector=evf1)

        def load_fold2(e, inc):
            for c in range(16):
                inc(e.dma_start(out=stage[:, c * 128:(c + 1) * 128],
                                in_=bvT[c * 128:(c + 1) * 128, :]), 16)
                inc(e.dma_start(out=wiga_sb[:, c * 8:(c + 1) * 8],
                                in_=wig_v[c * 128:(c + 1) * 128, :]), 16)
        pp.phase(sync=load_fold2)

        def fold2(t, inc):
            for c in range(16):
                ins = t.matmul(p128[:, c * 8:(c + 1) * 8],
                               stage[:, c * 128:(c + 1) * 128],
                               wiga_sb[:, c * 8:(c + 1) * 8], start=True, stop=True)
            inc(ins)
        pp.phase(tensor=fold2)

        def evf2(v, inc):
            ins = v.tensor_copy(out=a_ml[:], in_=p128[:, 0:128])
            inc(ins)
        pp.phase(vector=evf2)

        for c in range(16):
            def conv(v, inc, c=c):
                base = c * TH + 509
                v.tensor_scalar(out=cv[:], in0=xml_sb[:, base:base + TC],
                                scalar1=cw_sb[:, c * 4:c * 4 + 1], scalar2=None,
                                op0=ALU.mult)
                for i in (1, 2, 3):
                    ins = v.scalar_tensor_tensor(
                        out=cv[:], in0=xml_sb[:, base + i:base + i + TC],
                        scalar=cw_sb[:, c * 4 + i:c * 4 + i + 1], in1=cv[:],
                        op0=ALU.mult, op1=ALU.add)
                inc(ins)
            pp.phase(vector=conv)

            def act(s, inc, c=c):
                ins = s.activation(out=ch(xact_sb, c, TC), in_=cv[:], func=AF.Silu,
                                   bias=cb_sb[:, c:c + 1])
                inc(ins)
            pp.phase(scalar=act)

        def st_act(e, inc):
            for c in range(16):
                inc(e.dma_start(out=xact_o[c * 128:(c + 1) * 128, :],
                                in_=ch(xact_sb, c, TC)), 16)
        pp.phase(gpsimd=st_act)

        for w_bf, dram_out in ((bq_bf, q_o), (bk_bf, k_o)):
            for c in range(16):
                for tt in range(NT):
                    ps = pa if tt % 2 == 0 else pb

                    def mm(t, inc, c=c, tt=tt, ps=ps, w_bf=w_bf):
                        ins = t.matmul(ps[:, 0:TT], w_bf[:, c * 128:(c + 1) * 128],
                                       xact_sb[:, c * TC + tt * TT: c * TC + (tt + 1) * TT],
                                       start=True, stop=True)
                        inc(ins)
                    pp.phase(tensor=mm)

                    def ev(v, inc, ps=ps):
                        ins = v.tensor_copy(out=ev_bf[:], in_=ps[:, 0:TT])
                        inc(ins)
                    pp.phase(vector=ev)

                    def st(e, inc, c=c, tt=tt, dram_out=dram_out):
                        inc(e.dma_start(out=dram_out[c * 128:(c + 1) * 128,
                                                     tt * TT:(tt + 1) * TT],
                                        in_=ev_bf[:]), 16)
                    pp.phase(gpsimd=st)

        for tau in range(16):
            for g in range(4):
                def mm(t, inc, tau=tau, g=g):
                    for j in range(4):
                        c = 4 * g + j
                        ins = t.matmul(pa[:, j * 128:(j + 1) * 128],
                                       xml_sb[:, c * TH + 512 + tau * 128:
                                              c * TH + 512 + (tau + 1) * 128],
                                       bv_bf[:, c * 128:(c + 1) * 128],
                                       start=True, stop=True)
                    inc(ins)
                pp.phase(tensor=mm)

                def ev(v, inc):
                    ins = v.tensor_copy(out=ev_bf[:], in_=pa[:, 0:TT])
                    inc(ins)
                pp.phase(vector=ev)

                def st(e, inc, tau=tau, g=g):
                    inc(e.dma_start(out=v_o[tau * 128:(tau + 1) * 128,
                                           g * 512:(g + 1) * 512],
                                    in_=ev_bf[:]), 16)
                pp.phase(gpsimd=st)

        for tt in range(NT):
            def mm(t, inc, tt=tt):
                for c in range(16):
                    t.matmul(pg[:, 0:TT], a_act[:, c * 8:(c + 1) * 8],
                             xact_sb[:, c * TC + tt * TT: c * TC + (tt + 1) * TT],
                             start=(c == 0), stop=False)
                for c in range(16):
                    ins = t.matmul(pg[:, 0:TT], a_ml[:, c * 8:(c + 1) * 8],
                                   xml_sb[:, c * TH + 512 + tt * TT:
                                          c * TH + 512 + (tt + 1) * TT],
                                   start=False, stop=(c == 15))
                inc(ins)
            pp.phase(tensor=mm)

            def ev(v, inc, tt=tt):
                ins = v.tensor_scalar(out=g_f[:, tt * TT:(tt + 1) * TT],
                                      in0=pg[:, 0:TT], scalar1=bg_sb[:],
                                      scalar2=None, op0=ALU.add)
                inc(ins)
            pp.phase(vector=ev)

        def st_g(e, inc):
            inc(e.dma_start(out=ig_o, in_=g_f[0:4, :]), 16)
            inc(e.dma_start(out=fg_o, in_=g_f[4:8, :]), 16)
        pp.phase(gpsimd=st_g)

        pp.finish()
    return nc


QB = 256
NQB = TF // QB
DH = 512


def build_l5_attn():
    nc = bass.Bass("TRN2", target_bir_lowering=False)
    ins = {}
    for u in range(2):
        ins[f"qf{u}"] = nc.dram_tensor(f"qf{u}", [DH, TF], BF16, kind="ExternalInput").ap()
        ins[f"kf{u}"] = nc.dram_tensor(f"kf{u}", [DH, TF], BF16, kind="ExternalInput").ap()
        ins[f"v1{u}"] = nc.dram_tensor(f"v1{u}", [TF, DH], BF16, kind="ExternalInput").ap()
        ins[f"ig{u}"] = nc.dram_tensor(f"ig{u}", [1, TF], F32, kind="ExternalInput").ap()
        ins[f"fg{u}"] = nc.dram_tensor(f"fg{u}", [1, TF], F32, kind="ExternalInput").ap()
        ins[f"nw{u}"] = nc.dram_tensor(f"nw{u}", [128, DH], F32, kind="ExternalInput").ap()
    ident_in = nc.dram_tensor("ident_in", [128, 128], F32, kind="ExternalInput").ap()
    mbig_in = nc.dram_tensor("mbig_in", [128, 512], BF16, kind="ExternalInput").ap()
    outs = {}
    for u in range(2):
        outs[u] = nc.dram_tensor(f"hn{u}", [TF, DH], BF16, kind="ExternalOutput").ap()
    g3 = nc.dram_tensor("g3", [3, TF], F32, kind="Internal").ap()

    with contextlib.ExitStack() as es:
        pp = PB(nc, es)
        q_sb = es.enter_context(nc.sbuf_tensor("q_sb", [128, 4 * TF], BF16))
        k_sb = es.enter_context(nc.sbuf_tensor("k_sb", [128, 4 * TF], BF16))
        vbuf = es.enter_context(nc.sbuf_tensor("vbuf", [128, 32 * DH], BF16))
        ones_pv = es.enter_context(nc.sbuf_tensor("ones_pv", [128, 1], BF16))
        igr = es.enter_context(nc.sbuf_tensor("igr", [1, TF], F32))
        fgr = es.enter_context(nc.sbuf_tensor("fgr", [1, TF], F32))
        w1r = es.enter_context(nc.sbuf_tensor("w1r", [1, TF], F32))
        w2r = es.enter_context(nc.sbuf_tensor("w2r", [1, TF], F32))
        Fr = es.enter_context(nc.sbuf_tensor("Fr", [1, TF], F32))
        igmF = es.enter_context(nc.sbuf_tensor("igmF", [1, TF], F32))
        rmax = w2r
        nmd = igr
        t32 = es.enter_context(nc.sbuf_tensor("t32", [32, 128], F32))
        ident = es.enter_context(nc.sbuf_tensor("ident_s", [128, 128], F32))
        mbig = es.enter_context(nc.sbuf_tensor("mbig", [128, 512], BF16))
        nw_sb = es.enter_context(nc.sbuf_tensor("nw_sb", [128, DH], F32))
        igmF_tm = es.enter_context(nc.sbuf_tensor("igmF_tm", [128, 32], F32))
        rmax_tm = es.enter_context(nc.sbuf_tensor("rmax_tm", [128, 32], F32))
        encmD_tm = es.enter_context(nc.sbuf_tensor("encmD_tm", [128, 32], F32))
        cq_row = es.enter_context(nc.sbuf_tensor("cq_row", [1, NQB], F32))
        cq_b = es.enter_context(nc.sbuf_tensor("cq_b", [128, NQB], F32))
        ncq3 = es.enter_context(nc.sbuf_tensor("ncq3", [128, NQB], F32))
        colsc = es.enter_context(nc.sbuf_tensor("colsc", [128, 32], F32))
        rf = es.enter_context(nc.sbuf_tensor("rf", [128, 2], F32))
        T_sb = es.enter_context(nc.sbuf_tensor("T_sb", [128, QB], BF16))
        hsb = es.enter_context(nc.sbuf_tensor("hsb", [128, DH], F32))
        dsc = es.enter_context(nc.sbuf_tensor("dsc", [128, 1], F32))
        nrm = es.enter_context(nc.sbuf_tensor("nrm", [128, 1], F32))
        nrm2 = es.enter_context(nc.sbuf_tensor("nrm2", [128, 1], F32))
        mh_mu = es.enter_context(nc.sbuf_tensor("mh_mu", [128, 1], F32))
        mh_d = es.enter_context(nc.sbuf_tensor("mh_d", [128, DH], F32))
        mh_sq = es.enter_context(nc.sbuf_tensor("mh_sq", [128, DH], BF16))
        mh_vs = es.enter_context(nc.sbuf_tensor("mh_vs", [128, 1], F32))
        mh_r = es.enter_context(nc.sbuf_tensor("mh_r", [128, 1], F32))
        hn_bf = es.enter_context(nc.sbuf_tensor("hn_bf", [128, DH], BF16))
        ps_s = es.enter_context(nc.psum_tensor("ps_s", [128, QB], F32))
        ps_a0 = es.enter_context(nc.psum_tensor("ps_a0", [128, DH], F32))
        ps_a1 = es.enter_context(nc.psum_tensor("ps_a1", [128, DH], F32))
        ps_d0 = es.enter_context(nc.psum_tensor("ps_d0", [128, 1], F32))
        ps_d1 = es.enter_context(nc.psum_tensor("ps_d1", [128, 1], F32))
        ps_t = es.enter_context(nc.psum_tensor("ps_t", [128, 32], F32))
        ps_b = es.enter_context(nc.psum_tensor("ps_b", [128, NQB], F32))
        consts = make_consts(nc, es, pp, {"one": 1.0, "eps6": 1e-6, "m1e6": 1e-6})

        def load_const(e, inc):
            inc(e.dma_start(out=ident[:], in_=ident_in), 16)
            inc(e.dma_start(out=mbig[:], in_=mbig_in), 16)
        pp.phase(sync=load_const)

        def mk1(v, inc):
            ins = v.memset(ones_pv[:], 1.0)
            inc(ins)
        pp.phase(vector=mk1)

        for u in range(2):
            def load_u(e, inc, u=u):
                for c in range(4):
                    inc(e.dma_start(out=q_sb[:, c * TF:(c + 1) * TF],
                                    in_=ins[f"qf{u}"][c * 128:(c + 1) * 128, :]), 16)
                    inc(e.dma_start(out=k_sb[:, c * TF:(c + 1) * TF],
                                    in_=ins[f"kf{u}"][c * 128:(c + 1) * 128, :]), 16)
                for j in range(32):
                    inc(e.dma_start(out=vbuf[:, j * DH:(j + 1) * DH],
                                    in_=ins[f"v1{u}"][j * 128:(j + 1) * 128, :]), 16)
                inc(e.dma_start(out=igr[:], in_=ins[f"ig{u}"]), 16)
                inc(e.dma_start(out=fgr[:], in_=ins[f"fg{u}"]), 16)
                inc(e.dma_start(out=nw_sb[:], in_=ins[f"nw{u}"]), 16)
            pp.phase(sync=load_u)

            def lf1(s, inc):
                inc(s.activation(out=w1r[:], in_=fgr[:], func=AF.Abs), wait=True)
                inc(s.activation(out=w1r[:], in_=w1r[:], func=AF.Exp, scale=-1.0), wait=True)
                inc(s.activation(out=w1r[:], in_=w1r[:], func=AF.Ln,
                                 bias=consts["one"][0:1, :]), wait=True)
                inc(s.activation(out=w2r[:], in_=fgr[:], func=AF.Relu))
            pp.phase(scalar=lf1)

            def lf2(v, inc):
                inc(v.tensor_tensor(out=w2r[:], in0=fgr[:], in1=w2r[:], op=ALU.subtract), wait=True)
                inc(v.tensor_tensor(out=w2r[:], in0=w2r[:], in1=w1r[:], op=ALU.subtract), wait=True)
                inc(v.memset(w1r[:], 1.0), wait=True)
                inc(v.tensor_tensor_scan(out=Fr[:], data0=w1r[:], data1=w2r[:],
                                         initial=0.0, op0=ALU.mult, op1=ALU.add), wait=True)
                inc(v.tensor_tensor(out=igmF[:], in0=igr[:], in1=Fr[:], op=ALU.subtract), wait=True)
                inc(v.tensor_tensor_scan(out=rmax[:], data0=igmF[:], data1=igmF[:],
                                         initial=NEG_BIG, op0=ALU.max, op1=ALU.max), wait=True)
                inc(v.tensor_tensor(out=nmd[:], in0=Fr[:], in1=rmax[:], op=ALU.add), wait=True)
                inc(v.tensor_scalar(out=nmd[:], in0=nmd[:], scalar1=-1.0,
                                    scalar2=None, op0=ALU.mult))
            pp.phase(vector=lf2)

            def to_dram(e, inc):
                inc(e.dma_start(out=g3[0:1, :], in_=igmF[:]), 16)
                inc(e.dma_start(out=g3[1:2, :], in_=rmax[:]), 16)
                inc(e.dma_start(out=g3[2:3, :], in_=nmd[:]), 16)
            pp.phase(gpsimd=to_dram)

            for r, dst in ((0, igmF_tm), (1, rmax_tm), (2, encmD_tm)):
                def ld32(e, inc, r=r):
                    inc(e.dma_start(out=t32[:],
                                    in_=g3[r:r + 1, :].rearrange("one (a b) -> (one a) b", b=128)), 16)
                pp.phase(sync=ld32)

                def tp(t, inc):
                    ins = t.transpose(out=ps_t[:], in_=t32[:], identity=ident[0:32, 0:32])
                    inc(ins)
                pp.phase(tensor=tp)

                if r < 2:
                    def cp(v, inc, dst=dst):
                        ins = v.tensor_copy(out=dst[:], in_=ps_t[:])
                        inc(ins)
                    pp.phase(vector=cp)
                else:
                    def cp(s, inc, dst=dst):
                        ins = s.activation(out=dst[:], in_=ps_t[:], func=AF.Exp)
                        inc(ins)
                    pp.phase(scalar=cp)

            def cq1(v, inc):
                ins = v.tensor_copy(out=cq_row[:], in_=rmax[0:1, QB - 1:TF:QB])
                inc(ins)
            pp.phase(vector=cq1)

            def cq2(t, inc):
                ins = t.matmul(ps_b[:, 0:NQB], consts["ones_row"], cq_row[:],
                               start=True, stop=True)
                inc(ins)
            pp.phase(tensor=cq2)

            def cq3(v, inc):
                v.tensor_copy(out=cq_b[:], in_=ps_b[:, 0:NQB])
                ins = v.tensor_scalar(out=ncq3[:], in0=ps_b[:, 0:NQB],
                                      scalar1=CSC_SHIFT, scalar2=-1.0,
                                      op0=ALU.add, op1=ALU.mult)
                inc(ins)
            pp.phase(vector=cq3)

            for qb in range(NQB):
                nk = 2 * qb + 2

                def gsc(s, inc, qb=qb, nk=nk):
                    s.activation(out=colsc[:, 0:nk], in_=igmF_tm[:, 0:nk], func=AF.Exp,
                                 bias=ncq3[:, qb:qb + 1])
                    ins = s.activation(out=rf[:], in_=rmax_tm[:, 2 * qb:2 * qb + 2],
                                       func=AF.Exp, scale=-1.0, bias=cq_b[:, qb:qb + 1])
                    inc(ins)
                pp.phase(scalar=gsc)

                for j in range(nk):
                    def mm_s(t, inc, qb=qb, j=j):
                        for c in range(4):
                            ins = t.matmul(ps_s[:, 0:QB],
                                           k_sb[:, c * TF + j * 128: c * TF + (j + 1) * 128],
                                           q_sb[:, c * TF + qb * QB: c * TF + (qb + 1) * QB],
                                           start=(c == 0), stop=(c == 3))
                        inc(ins)
                    pp.phase(tensor=mm_s)

                    diag_r = j - 2 * qb

                    def evs(v, inc, j=j, diag_r=diag_r):
                        if diag_r in (0, 1):
                            ins = v.scalar_tensor_tensor(
                                out=T_sb[:], in0=ps_s[:, 0:QB],
                                scalar=colsc[:, j:j + 1],
                                in1=mbig[:, 256 - 128 * diag_r: 512 - 128 * diag_r],
                                op0=ALU.mult, op1=ALU.mult)
                        else:
                            ins = v.tensor_scalar(out=T_sb[:], in0=ps_s[:, 0:QB],
                                                  scalar1=colsc[:, j:j + 1],
                                                  scalar2=None, op0=ALU.mult)
                        inc(ins)
                    pp.phase(vector=evs)

                    def mm_v(t, inc, j=j, nk=nk):
                        st_, sp_ = (j == 0), (j == nk - 1)
                        t.matmul(ps_a0[:, 0:DH], T_sb[:, 0:128],
                                 vbuf[:, j * DH:(j + 1) * DH], start=st_, stop=sp_)
                        t.matmul(ps_a1[:, 0:DH], T_sb[:, 128:256],
                                 vbuf[:, j * DH:(j + 1) * DH], start=st_, stop=sp_)
                        t.matmul(ps_d0[:, 0:1], T_sb[:, 0:128], ones_pv[:],
                                 start=st_, stop=sp_)
                        ins = t.matmul(ps_d1[:, 0:1], T_sb[:, 128:256], ones_pv[:],
                                       start=st_, stop=sp_)
                        inc(ins)
                    pp.phase(tensor=mm_v)

                for qc in range(2):
                    ps_acc = ps_a0 if qc == 0 else ps_a1
                    ps_den = ps_d0 if qc == 0 else ps_d1
                    tmc = 2 * qb + qc

                    def ep1(v, inc, ps_acc=ps_acc, ps_den=ps_den, qc=qc):
                        v.tensor_scalar(out=hsb[:], in0=ps_acc[:, 0:DH],
                                        scalar1=rf[:, qc:qc + 1], scalar2=None,
                                        op0=ALU.mult)
                        ins = v.tensor_scalar(out=dsc[:], in0=ps_den[:, 0:1],
                                              scalar1=rf[:, qc:qc + 1], scalar2=None,
                                              op0=ALU.mult)
                        inc(ins)
                    pp.phase(vector=ep1)

                    def ep1b(s, inc):
                        ins = s.activation(out=nrm[:], in_=dsc[:], func=AF.Abs)
                        inc(ins)
                    pp.phase(scalar=ep1b)

                    def ep2(v, inc, tmc=tmc):
                        ins = v.tensor_tensor(out=nrm[:], in0=nrm[:],
                                              in1=encmD_tm[:, tmc:tmc + 1], op=ALU.max)
                        inc(ins)
                    pp.phase(vector=ep2)

                    def ep3(s, inc):
                        inc(s.activation(out=nrm2[:], in_=nrm[:], func=AF.Ln,
                                         bias=consts["m1e6"]), wait=True)
                        inc(s.activation(out=nrm2[:], in_=nrm2[:], func=AF.Exp,
                                         scale=-1.0))
                    pp.phase(scalar=ep3)

                    def ep4a(v, inc):
                        v.tensor_scalar(out=mh_d[:], in0=hsb[:],
                                        scalar1=nrm2[:], scalar2=None, op0=ALU.mult)
                        ins = v.tensor_reduce(out=mh_mu[:], in_=mh_d[:], axis=AX,
                                              op=ALU.add)
                        inc(ins)
                    pp.phase(vector=ep4a)

                    def ep4b(s, inc):
                        ins = s.mul(out=mh_mu[:], in_=mh_mu[:], mul=1.0 / DH)
                        inc(ins)
                    pp.phase(scalar=ep4b)

                    def ep4c(v, inc):
                        ins = v.tensor_scalar(out=mh_d[:], in0=mh_d[:],
                                              scalar1=mh_mu[:], scalar2=None,
                                              op0=ALU.subtract)
                        inc(ins)
                    pp.phase(vector=ep4c)

                    def ep5(s, inc):
                        inc(s.activation(out=mh_sq[:], in_=mh_d[:], func=AF.Square,
                                         accum_out=mh_vs[:]), wait=True)
                        inc(s.activation(out=mh_r[:], in_=mh_vs[:], func=AF.Ln,
                                         scale=1.0 / DH, bias=consts["eps6"]), wait=True)
                        inc(s.activation(out=mh_r[:], in_=mh_r[:], func=AF.Exp,
                                         scale=-0.5))
                    pp.phase(scalar=ep5)

                    def ep6(v, inc):
                        v.tensor_scalar(out=mh_d[:], in0=mh_d[:], scalar1=mh_r[:],
                                        scalar2=None, op0=ALU.mult)
                        ins = v.tensor_tensor(out=hn_bf[:], in0=mh_d[:], in1=nw_sb[:],
                                              op=ALU.mult)
                        inc(ins)
                    pp.phase(vector=ep6)

                    def ep7(e, inc, qb=qb, qc=qc, u=u):
                        r0 = qb * QB + qc * 128
                        inc(e.dma_start(out=outs[u][r0:r0 + 128, :], in_=hn_bf[:]), 16)
                    pp.phase(gpsimd=ep7)

        pp.finish()
    return nc



def build_l6a_post():
    nc = bass.Bass("TRN2", target_bir_lowering=False)
    x2 = nc.dram_tensor("x2", [C, TC], F32, kind="ExternalInput").ap()
    xact = nc.dram_tensor("xact", [2048, TC], BF16, kind="ExternalInput").ap()
    z = nc.dram_tensor("z", [2048, TC], BF16, kind="ExternalInput").ap()
    hn = nc.dram_tensor("hn", [2048, TC], BF16, kind="ExternalInput").ap()
    skip = nc.dram_tensor("skip", [128, 16], F32, kind="ExternalInput").ap()
    wdn = nc.dram_tensor("wdn", [2048, C], F32, kind="ExternalInput").ap()
    x3_o = nc.dram_tensor("x3_o", [C, TC], F32, kind="ExternalOutput").ap()
    with contextlib.ExitStack() as es:
        pp = PB(nc, es)
        stage = es.enter_context(nc.sbuf_tensor("stage", [128, 4 * C], F32))
        w_bf = es.enter_context(nc.sbuf_tensor("w_bf", [128, 16 * C], BF16))
        xa_t = es.enter_context(nc.sbuf_tensor("xa_t", [128, 16 * TT], BF16))
        z_t = es.enter_context(nc.sbuf_tensor("z_t", [128, 16 * TT], BF16))
        hn_t = es.enter_context(nc.sbuf_tensor("hn_t", [128, 16 * TT], BF16))
        x2_t = es.enter_context(nc.sbuf_tensor("x2_t", [128, 8 * TT], F32))
        ao_f = es.enter_context(nc.sbuf_tensor("ao_f", [128, 8 * TT], F32))
        sz = es.enter_context(nc.sbuf_tensor("sz", [128, TT], F32))
        tmf = es.enter_context(nc.sbuf_tensor("tmf", [128, TT], F32))
        skip_sb = es.enter_context(nc.sbuf_tensor("skip_sb", [128, 16], F32))
        kit = LNKit(nc, es, "ln", 8, TT)
        pa = es.enter_context(nc.psum_tensor("pa", [128, TT], F32))
        pb = es.enter_context(nc.psum_tensor("pb", [128, TT], F32))
        consts = make_consts(nc, es, pp, {"eps5": 1e-5, "one": 1.0})

        for rnd in range(4):
            def loadw(e, inc, rnd=rnd):
                for j in range(4):
                    kc = rnd * 4 + j
                    inc(e.dma_start(out=stage[:, j * C:(j + 1) * C],
                                    in_=wdn[kc * 128:(kc + 1) * 128, :]), 16)
            pp.phase(sync=loadw)

            def castw(v, inc, rnd=rnd):
                ins = v.tensor_copy(out=w_bf[:, rnd * 4 * C:(rnd + 1) * 4 * C],
                                    in_=stage[:])
                inc(ins)
            pp.phase(vector=castw)

        def loads(e, inc):
            inc(e.dma_start(out=skip_sb[:], in_=skip), 16)
        pp.phase(sync=loads)

        for tt in range(NT):
            t0 = tt * TT

            def load_t(e, inc, t0=t0):
                load_fm(e, inc, xa_t, xact, 16, TT, t0)
                load_fm(e, inc, z_t, z, 16, TT, t0)
                load_fm(e, inc, hn_t, hn, 16, TT, t0)
                load_fm(e, inc, x2_t, x2, 8, TT, t0)
            pp.phase(sync=load_t)

            for c in range(16):
                def hs1(s, inc, c=c):
                    ins = s.activation(out=sz[:], in_=ch(z_t, c, TT), func=AF.Silu)
                    inc(ins)
                pp.phase(scalar=hs1)

                def hs2(v, inc, c=c):
                    v.scalar_tensor_tensor(out=tmf[:], in0=ch(xa_t, c, TT),
                                           scalar=skip_sb[:, c:c + 1],
                                           in1=ch(hn_t, c, TT),
                                           op0=ALU.mult, op1=ALU.add)
                    ins = v.tensor_tensor(out=ch(hn_t, c, TT), in0=tmf[:], in1=sz[:],
                                          op=ALU.mult)
                    inc(ins)
                pp.phase(vector=hs2)

            for dc in range(8):
                ps = pa if dc % 2 == 0 else pb

                def mm(t, inc, dc=dc, ps=ps):
                    for kc in range(16):
                        ins = t.matmul(ps[:, 0:TT],
                                       w_bf[:, kc * C + dc * 128: kc * C + (dc + 1) * 128],
                                       ch(hn_t, kc, TT), start=(kc == 0), stop=(kc == 15))
                    inc(ins)
                pp.phase(tensor=mm)

                def ev(v, inc, dc=dc, ps=ps):
                    ins = v.tensor_copy(out=ch(ao_f, dc, TT), in_=ps[:, 0:TT])
                    inc(ins)
                pp.phase(vector=ev)

            emit_ln(pp, kit, consts, ao_f, pa, pb, out_f32=ao_f)

            def addres(v, inc):
                for c in range(8):
                    ins = v.tensor_tensor(out=ch(x2_t, c, TT), in0=ch(x2_t, c, TT),
                                          in1=ch(ao_f, c, TT), op=ALU.add)
                inc(ins)
            pp.phase(vector=addres)

            def store(e, inc, t0=t0):
                store_fm(e, inc, x3_o, x2_t, 8, TT, t0)
            pp.phase(gpsimd=store)
        pp.finish()
    return nc


def build_l6b_ffn1():
    nc = bass.Bass("TRN2", target_bir_lowering=False)
    x3 = nc.dram_tensor("x3", [C, TC], F32, kind="ExternalInput").ap()
    w1 = nc.dram_tensor("w1", [C, 4096], F32, kind="ExternalInput").ap()
    b1 = nc.dram_tensor("b1", [128, 32], F32, kind="ExternalInput").ap()
    w2 = nc.dram_tensor("w2", [4096, C], F32, kind="ExternalInput").ap()
    b2 = nc.dram_tensor("b2", [128, 8], F32, kind="ExternalInput").ap()
    x4_o = nc.dram_tensor("x4_o", [C, TC], F32, kind="ExternalOutput").ap()
    w2bf_d = nc.dram_tensor("w2bf_d", [4096, C], BF16, kind="Internal").ap()
    with contextlib.ExitStack() as es:
        pp = PB(nc, es)
        stage = es.enter_context(nc.sbuf_tensor("stage", [128, 4096], F32))
        w1_bf = es.enter_context(nc.sbuf_tensor("w1_bf", [128, 8 * 4096], BF16))
        w2t = es.enter_context(nc.sbuf_tensor("w2t", [128, C], BF16))
        b1_sb = es.enter_context(nc.sbuf_tensor("b1_sb", [128, 32], F32))
        b2_sb = es.enter_context(nc.sbuf_tensor("b2_sb", [128, 8], F32))
        x3_t = es.enter_context(nc.sbuf_tensor("x3_t", [128, 8 * TT], F32))
        fin_bf = es.enter_context(nc.sbuf_tensor("fin_bf", [128, 8 * TT], BF16))
        hh_bf = es.enter_context(nc.sbuf_tensor("hh_bf", [128, 32 * TT], BF16))
        ffn_f = es.enter_context(nc.sbuf_tensor("ffn_f", [128, 8 * TT], F32))
        kit = LNKit(nc, es, "ln", 8, TT)
        pa = es.enter_context(nc.psum_tensor("pa", [128, TT], F32))
        pb = es.enter_context(nc.psum_tensor("pb", [128, TT], F32))
        pc0 = es.enter_context(nc.psum_tensor("pc0", [128, TT], F32))
        pc1 = es.enter_context(nc.psum_tensor("pc1", [128, TT], F32))
        pc2 = es.enter_context(nc.psum_tensor("pc2", [128, TT], F32))
        pc3 = es.enter_context(nc.psum_tensor("pc3", [128, TT], F32))
        pcs = [pc0, pc1, pc2, pc3]
        consts = make_consts(nc, es, pp, {"eps5": 1e-5, "one": 1.0})

        for kc in range(8):
            def loadw1(e, inc, kc=kc):
                inc(e.dma_start(out=stage[:], in_=w1[kc * 128:(kc + 1) * 128, :]), 16)
            pp.phase(sync=loadw1)

            def castw1(v, inc, kc=kc):
                ins = v.tensor_copy(out=w1_bf[:, kc * 4096:(kc + 1) * 4096], in_=stage[:])
                inc(ins)
            pp.phase(vector=castw1)

        def load_b(e, inc):
            inc(e.dma_start(out=b1_sb[:], in_=b1), 16)
            inc(e.dma_start(out=b2_sb[:], in_=b2), 16)
        pp.phase(sync=load_b)

        for kc in range(32):
            def loadw2(e, inc, kc=kc):
                inc(e.dma_start(out=stage[:, 0:C], in_=w2[kc * 128:(kc + 1) * 128, :]), 16)
            pp.phase(sync=loadw2)

            def castw2(v, inc):
                ins = v.tensor_copy(out=w2t[:], in_=stage[:, 0:C])
                inc(ins)
            pp.phase(vector=castw2)

            def stw2(e, inc, kc=kc):
                inc(e.dma_start(out=w2bf_d[kc * 128:(kc + 1) * 128, :], in_=w2t[:]), 16)
            pp.phase(gpsimd=stw2)

        for tt in range(NT):
            t0 = tt * TT

            def load_x(e, inc, t0=t0):
                load_fm(e, inc, x3_t, x3, 8, TT, t0)
            pp.phase(sync=load_x)

            emit_ln(pp, kit, consts, x3_t, pa, pb, out_bf=fin_bf)

            for hcol in range(32):
                ps = pa if hcol % 2 == 0 else pb

                def mm1(t, inc, hcol=hcol, ps=ps):
                    for kc in range(8):
                        ins = t.matmul(
                            ps[:, 0:TT],
                            w1_bf[:, kc * 4096 + hcol * 128: kc * 4096 + (hcol + 1) * 128],
                            ch(fin_bf, kc, TT), start=(kc == 0), stop=(kc == 7))
                    inc(ins)
                pp.phase(tensor=mm1)

                def gl(s, inc, hcol=hcol, ps=ps):
                    ins = s.activation(out=ch(hh_bf, hcol, TT), in_=ps[:, 0:TT],
                                       func=AF.Gelu, bias=b1_sb[:, hcol:hcol + 1])
                    inc(ins)
                pp.phase(scalar=gl)

            for dcg in range(2):
                for kc in range(32):
                    def loadw2t(e, inc, kc=kc):
                        inc(e.dma_start(out=w2t[:],
                                        in_=w2bf_d[kc * 128:(kc + 1) * 128, :]), 16)
                    pp.phase(sync=loadw2t)

                    def mm2(t, inc, kc=kc, dcg=dcg):
                        for j in range(4):
                            dc = dcg * 4 + j
                            ins = t.matmul(pcs[j][:, 0:TT],
                                           w2t[:, dc * 128:(dc + 1) * 128],
                                           ch(hh_bf, kc, TT),
                                           start=(kc == 0), stop=(kc == 31))
                        inc(ins)
                    pp.phase(tensor=mm2)

                def ev2(v, inc, dcg=dcg):
                    for j in range(4):
                        dc = dcg * 4 + j
                        ins = v.tensor_scalar(out=ch(ffn_f, dc, TT),
                                              in0=pcs[j][:, 0:TT],
                                              scalar1=b2_sb[:, dc:dc + 1],
                                              scalar2=None, op0=ALU.add)
                    inc(ins)
                pp.phase(vector=ev2)

            emit_ln(pp, kit, consts, ffn_f, pa, pb, out_f32=ffn_f)

            def addres(v, inc):
                for c in range(8):
                    ins = v.tensor_tensor(out=ch(x3_t, c, TT), in0=ch(x3_t, c, TT),
                                          in1=ch(ffn_f, c, TT), op=ALU.add)
                inc(ins)
            pp.phase(vector=addres)

            def store(e, inc, t0=t0):
                store_fm(e, inc, x4_o, x3_t, 8, TT, t0)
            pp.phase(gpsimd=store)
        pp.finish()
    return nc


QBF = 512
THL = 2051
DBG = False


def build_fused():
    nc = bass.Bass("TRN2", target_bir_lowering=False)
    I = lambda n, s, d=F32: nc.dram_tensor(n, s, d, kind="ExternalInput").ap()
    xs_d = I("xs", [C, TC])
    wiT_d = I("wiT", [C, C]); wrT_d = I("wrT", [C, C])
    bi_d = I("bi", [128, 8]); br_d = I("br", [128, 8]); lam_d = I("lam", [1, 1])
    th_d = I("th", [C, C]); thn_d = I("thn", [C, C])
    ph_d = I("ph", [C, C]); phn_d = I("phn", [C, C])
    wup_d = I("wup", [C, 4096])
    bq_d = I("bq", [2048, 128]); bk_d = I("bk", [2048, 128]); bv_d = I("bv", [2048, 128])
    bqT_d = I("bqT", [2048, 128]); bkT_d = I("bkT", [2048, 128]); bvT_d = I("bvT", [2048, 128])
    wigq_d = I("wig_q", [2048, 8]); wigk_d = I("wig_k", [2048, 8]); wigv_d = I("wig_v", [2048, 8])
    bg_d = I("bg", [8, 1]); cw_d = I("cw", [128, 64]); cb_d = I("cb", [128, 16])
    nw_d = I("nw", [128, 16]); skip_d = I("skip", [128, 16])
    wdn_d = I("wdn", [2048, C])
    w1_d = I("w1", [C, 4096]); b1_d = I("b1", [128, 32])
    w2_d = I("w2", [4096, C]); b2_d = I("b2", [128, 8])
    poison_d = I("poison", [4, TF])
    mbigA_d = I("mbigA", [128, 2 * QBF], BF16)
    mbigB_d = I("mbigB", [128, 2 * QBF], BF16)
    cmask_d = I("cmask", [128, 1]); hm4_d = I("hm4", [4, 1]); hm4c_d = I("hm4c", [4, 1])
    ident_d = I("ident", [128, 128])
    y_d = nc.dram_tensor("y", [C, TC], F32, kind="ExternalOutput").ap()
    dbg = {}
    if DBG:
        for nm in ("dbg_x1", "dbg_x2", "dbg_x3"):
            dbg[nm] = nc.dram_tensor(nm, [C, TC], F32, kind="ExternalOutput").ap()
        dbg["dbg_gown"] = nc.dram_tensor("dbg_gown", [8, TC], F32, kind="ExternalOutput").ap()
        dbg["dbg_hn"] = nc.dram_tensor("dbg_hn", [2048, TC], BF16, kind="ExternalOutput").ap()
        dbg["dbg_grow"] = nc.dram_tensor("dbg_grow", [4, TF], F32, kind="ExternalOutput").ap()

    D = lambda n, s, d=F32: nc.dram_tensor(n, s, d).ap()
    ysb_d = D("ysb", [2048, TC], BF16)
    pbb_d = D("pbb", [2048, TC], BF16)
    carry_c = D("carry_c", [128, 8]); carry_o = D("carry_o", [256, 8])
    tail_c = D("tail_c", [128, 24]); tail_o = D("tail_o", [256, 24])
    xmlb_d = D("xmlb", [2048, THL], BF16)
    kvb_d = D("kvb", [4096, 2048], BF16)
    kv_o = D("kv_o", [8192, 2048], BF16)
    gb_d = D("gb", [8, TC]); g_o = D("g_o", [16, TC])
    qb_d = D("qbd", [2048, TC], BF16)
    xab_d = D("xab", [2048, TC], BF16)
    zb_d = D("zb", [2048, TC], BF16)
    grow_d = D("grow", [4, TF])
    gown_d = D("gown", [8, TC])
    hnb_d = D("hnb", [2048, TC], BF16)
    w1bf_d = D("w1bf", [C, 4096], BF16)
    w2bf_d = D("w2bf", [4096, C], BF16)
    wdbf_d = D("wdbf", [2048, C], BF16)

    with contextlib.ExitStack() as es:
        pp = PB(nc, es)
        pp.init_hand_sems(es)
        xsb = es.enter_context(nc.sbuf_tensor("xsb", [128, 8 * TC], F32))
        kit = LNKit(nc, es, "ln", 8, TT)
        pa = es.enter_context(nc.psum_tensor("pa", [128, TT], F32))
        pb = es.enter_context(nc.psum_tensor("pb", [128, TT], F32))
        consts = make_consts(nc, es, pp, {"eps5": 1e-5, "eps6": 1e-6, "one": 1.0,
                                          "halfpi": HALF_PI, "m1e6": 1e-6})
        cmask = es.enter_context(nc.sbuf_tensor("cmask_s", [128, 1], F32))
        hm4 = es.enter_context(nc.sbuf_tensor("hm4_s", [4, 1], F32))
        hm4c = es.enter_context(nc.sbuf_tensor("hm4c_s", [4, 1], F32))
        ident = es.enter_context(nc.sbuf_tensor("ident_s", [128, 128], F32))
        nw_sb = es.enter_context(nc.sbuf_tensor("nw_s", [128, 16], F32))
        skip_sb = es.enter_context(nc.sbuf_tensor("skip_s", [128, 16], F32))
        halo_sb = es.enter_context(nc.sbuf_tensor("halo_s", [128, 24], F32))

        def sl_t(c, tt):
            return xsb[:, c * TC + tt * TT: c * TC + (tt + 1) * TT]

        def dump_stream(nm):
            if not DBG:
                return
            def dumper(e, inc, nm=nm):
                for c in range(8):
                    for tt in range(NT):
                        inc(e.dma_start(out=dbg[nm][c * 128:(c + 1) * 128,
                                                    tt * TT:(tt + 1) * TT],
                                        in_=sl_t(c, tt)), 16)
            pp.phase(gpsimd=dumper)

        def emit_stream_ln(out_bf=None, out_into=None, tt=0):
            def cast(v, inc, tt=tt):
                for c in range(8):
                    ins = v.tensor_copy(out=ch(kit.xbf, c, TT), in_=sl_t(c, tt))
                inc(ins)
            pp.phase(vector=cast)

            def mm_mu(t, inc):
                for c in range(8):
                    ins = t.matmul(pa[:, 0:TT], consts["ones_bf"], ch(kit.xbf, c, TT),
                                   start=(c == 0), stop=(c == 7))
                inc(ins)
            pp.phase(tensor=mm_mu)

            def sub(v, inc, tt=tt):
                for c in range(8):
                    ins = v.scalar_tensor_tensor(out=ch(kit.d, c, TT), in0=pa[:, 0:TT],
                                                 scalar=-1.0 / C, in1=sl_t(c, tt),
                                                 op0=ALU.mult, op1=ALU.add)
                inc(ins)
            pp.phase(vector=sub)

            def sq(s, inc):
                for c in range(8):
                    ins = s.activation(out=ch(kit.xbf, c, TT), in_=ch(kit.d, c, TT),
                                       func=AF.Square)
                inc(ins)
            pp.phase(scalar=sq)

            def mm_var(t, inc):
                for c in range(8):
                    ins = t.matmul(pb[:, 0:TT], consts["ones_bf"], ch(kit.xbf, c, TT),
                                   start=(c == 0), stop=(c == 7))
                inc(ins)
            pp.phase(tensor=mm_var)

            def rstd(s, inc):
                s.activation(out=kit.rstd_a[:], in_=pb[:, 0:TT], func=AF.Ln,
                             scale=1.0 / C, bias=consts["eps5"])
                ins = s.activation(out=kit.rstd[:], in_=kit.rstd_a[:], func=AF.Exp,
                                   scale=-0.5)
                inc(ins)
            pp.phase(scalar=rstd)

            def apply(v, inc):
                for c in range(8):
                    dst = ch(out_bf, c, TT) if out_bf is not None else out_into(c)
                    ins = v.tensor_tensor(out=dst, in0=ch(kit.d, c, TT),
                                          in1=kit.rstd[:], op=ALU.mult)
                inc(ins)
            pp.phase(vector=apply)

        def s0_load(e, inc):
            for c in range(8):
                inc(e.dma_start(out=xsb[:, c * TC:(c + 1) * TC],
                                in_=xs_d[c * 128:(c + 1) * 128, :]), 16)
            inc(e.dma_start(out=cmask[:], in_=cmask_d), 16)
            inc(e.dma_start(out=hm4[:], in_=hm4_d), 16)
            inc(e.dma_start(out=hm4c[:], in_=hm4c_d), 16)
            inc(e.dma_start(out=ident[:], in_=ident_d), 16)
            inc(e.dma_start(out=nw_sb[:], in_=nw_d), 16)
            inc(e.dma_start(out=skip_sb[:], in_=skip_d), 16)
        pp.phase(sync=s0_load)

        with contextlib.ExitStack() as sw:
            stage = sw.enter_context(nc.sbuf_tensor("sw_stage", [128, 4096], F32))
            wt = sw.enter_context(nc.sbuf_tensor("sw_wt", [128, 4096], BF16))
            for src_d, dst_d, rows, cols in ((w1_d, w1bf_d, 8, 4096),
                                             (w2_d, w2bf_d, 32, C),
                                             (wdn_d, wdbf_d, 16, C)):
                for kc in range(rows):
                    def lw(e, inc, src_d=src_d, kc=kc, cols=cols):
                        inc(e.dma_start(out=stage[:, 0:cols],
                                        in_=src_d[kc * 128:(kc + 1) * 128, :]), 16)
                    pp.phase(sync=lw)

                    def cw_(v, inc, cols=cols):
                        ins = v.tensor_copy(out=wt[:, 0:cols], in_=stage[:, 0:cols])
                        inc(ins)
                    pp.phase(vector=cw_)

                    def sw_(e, inc, dst_d=dst_d, kc=kc, cols=cols):
                        inc(e.dma_start(out=dst_d[kc * 128:(kc + 1) * 128, :],
                                        in_=wt[:, 0:cols]), 16)
                    pp.phase(gpsimd=sw_)

        with contextlib.ExitStack() as s1:
            stage = s1.enter_context(nc.sbuf_tensor("s1_stage", [128, C], F32))
            wi_bf = s1.enter_context(nc.sbuf_tensor("s1_wi", [128, 8 * C], BF16))
            wr_bf = s1.enter_context(nc.sbuf_tensor("s1_wr", [128, 8 * C], BF16))
            bi_sb = s1.enter_context(nc.sbuf_tensor("s1_bi", [128, 8], F32))
            br_sb = s1.enter_context(nc.sbuf_tensor("s1_br", [128, 8], F32))
            lam_sb = s1.enter_context(nc.sbuf_tensor("s1_lam", [1, 1], F32))
            sp_t = s1.enter_context(nc.sbuf_tensor("s1_spt", [1, 1], F32))
            sp_t2 = s1.enter_context(nc.sbuf_tensor("s1_spt2", [1, 1], F32))
            s8_bc = s1.enter_context(nc.sbuf_tensor("s1_s8", [128, 1], F32))
            xin_bf = s1.enter_context(nc.sbuf_tensor("s1_xin", [128, 8 * TT], BF16))
            it_f = s1.enter_context(nc.sbuf_tensor("s1_it", [128, 8 * TT], F32))
            at_f = s1.enter_context(nc.sbuf_tensor("s1_at", [128, 8 * TT], F32))
            gx_f = s1.enter_context(nc.sbuf_tensor("s1_gx", [128, 8 * TT], F32))
            ysp_bf = s1.enter_context(nc.sbuf_tensor("s1_ysbf", [128, 8 * TT], BF16))
            prp_bf = s1.enter_context(nc.sbuf_tensor("s1_prbf", [128, 8 * TT], BF16))
            ones_t = s1.enter_context(nc.sbuf_tensor("s1_ones", [128, TT], F32))
            ys_last = s1.enter_context(nc.sbuf_tensor("s1_ysl", [128, 8], F32))
            pr_last = s1.enter_context(nc.sbuf_tensor("s1_prl", [128, 8], F32))
            t1 = s1.enter_context(nc.sbuf_tensor("s1_t1", [128, TT], F32))
            pd = s1.enter_context(nc.psum_tensor("s1_pd", [128, 1], F32))

            for w_d, w_bf in ((wiT_d, wi_bf), (wrT_d, wr_bf)):
                for kc in range(8):
                    def lw(e, inc, w_d=w_d, kc=kc):
                        inc(e.dma_start(out=stage[:], in_=w_d[kc * 128:(kc + 1) * 128, :]), 16)
                    pp.phase(sync=lw)

                    def cw_(v, inc, w_bf=w_bf, kc=kc):
                        ins = v.tensor_copy(out=w_bf[:, kc * C:(kc + 1) * C], in_=stage[:])
                        inc(ins)
                    pp.phase(vector=cw_)

            def s1_small(e, inc):
                inc(e.dma_start(out=bi_sb[:], in_=bi_d), 16)
                inc(e.dma_start(out=br_sb[:], in_=br_d), 16)
                inc(e.dma_start(out=lam_sb[:], in_=lam_d), 16)
            pp.phase(sync=s1_small)

            def sp1(s, inc):
                inc(s.activation(out=sp_t[:], in_=lam_sb[:], func=AF.Abs), wait=True)
                inc(s.activation(out=sp_t[:], in_=sp_t[:], func=AF.Exp, scale=-1.0), wait=True)
                inc(s.activation(out=sp_t[:], in_=sp_t[:], func=AF.Ln,
                                 bias=consts["one"][0:1, :]), wait=True)
                inc(s.activation(out=sp_t2[:], in_=lam_sb[:], func=AF.Relu))
            pp.phase(scalar=sp1)

            def sp2(v, inc):
                inc(v.tensor_tensor(out=sp_t2[:], in0=sp_t2[:], in1=sp_t[:], op=ALU.add), wait=True)
                inc(v.tensor_scalar(out=sp_t2[:], in0=sp_t2[:], scalar1=-8.0,
                                    scalar2=None, op0=ALU.mult), wait=True)
                inc(v.memset(ones_t[:], 1.0))
            pp.phase(vector=sp2)

            def sp3(t, inc):
                ins = t.matmul(pd[:, 0:1], consts["ones_row"], sp_t2[:], start=True, stop=True)
                inc(ins)
            pp.phase(tensor=sp3)

            def sp4(v, inc):
                ins = v.tensor_copy(out=s8_bc[:], in_=pd[:, 0:1])
                inc(ins)
            pp.phase(vector=sp4)

            for tt in range(NT):
                emit_stream_ln(out_bf=xin_bf, tt=tt)

                def mmi(t, i, s):
                    ps = pa if s == 0 else pb
                    for kc in range(8):
                        ins = t.matmul(ps[:, 0:TT],
                                       wi_bf[:, kc * C + i * 128: kc * C + (i + 1) * 128],
                                       ch(xin_bf, kc, TT), start=(kc == 0), stop=(kc == 7),
                                       skip_group_check=True)
                    return ins

                def evi(sc, i, s):
                    ps = pa if s == 0 else pb
                    return sc.activation(out=ch(it_f, i, TT), in_=ps[:, 0:TT],
                                         func=AF.Sigmoid, bias=bi_sb[:, i:i + 1])

                pp.pipe2(8, mmi, ev=evi, ev_eng="scalar")

                def mmr(t, i, s):
                    ps = pa if s == 0 else pb
                    for kc in range(8):
                        ins = t.matmul(ps[:, 0:TT],
                                       wr_bf[:, kc * C + i * 128: kc * C + (i + 1) * 128],
                                       ch(xin_bf, kc, TT), start=(kc == 0), stop=(kc == 7),
                                       skip_group_check=True)
                    return ins

                def evr(sc, i, s):
                    ps = pa if s == 0 else pb
                    pp.chain(sc, sc.activation(out=t1[:], in_=ps[:, 0:TT],
                                               func=AF.Sigmoid, bias=br_sb[:, i:i + 1]))
                    pp.chain(sc, sc.activation(out=ch(at_f, i, TT), in_=t1[:],
                                               func=AF.Exp, scale=s8_bc[:]))
                    pp.chain(sc, sc.activation(out=t1[:], in_=ch(at_f, i, TT), func=AF.Square))
                    pp.chain(sc, sc.activation(out=t1[:], in_=t1[:], func=AF.Ln, scale=-1.0,
                                               bias=consts["one"]))
                    return sc.activation(out=ch(gx_f, i, TT), in_=t1[:], func=AF.Exp, scale=0.5)

                pp.pipe2(8, mmr, ev=evr, ev_eng="scalar")

                def gg_all(v, inc):
                    for dc in range(8):
                        inc(v.tensor_tensor(out=ch(gx_f, dc, TT), in0=ch(gx_f, dc, TT),
                                            in1=ch(it_f, dc, TT), op=ALU.mult), wait=True)
                        inc(v.tensor_tensor(out=ch(gx_f, dc, TT), in0=ch(gx_f, dc, TT),
                                            in1=ch(xin_bf, dc, TT), op=ALU.mult), wait=True)
                    inc(v.nop())
                pp.phase(vector=gg_all)

                def scans(v, inc, tt=tt):
                    for c in range(8):
                        init_y = 0.0 if tt == 0 else ys_last[:, c:c + 1]
                        init_p = 1.0 if tt == 0 else pr_last[:, c:c + 1]
                        inc(v.tensor_tensor_scan(out=ch(ysp_bf, c, TT), data0=ch(at_f, c, TT),
                                                 data1=ch(gx_f, c, TT), initial=init_y,
                                                 op0=ALU.mult, op1=ALU.add), wait=True)
                        inc(v.tensor_tensor_scan(out=ch(prp_bf, c, TT), data0=ch(at_f, c, TT),
                                                 data1=ones_t[:], initial=init_p,
                                                 op0=ALU.mult, op1=ALU.mult), wait=True)
                        inc(v.tensor_copy(out=ys_last[:, c:c + 1],
                                          in_=ysp_bf[:, c * TT + TT - 1: c * TT + TT]), wait=True)
                        inc(v.tensor_copy(out=pr_last[:, c:c + 1],
                                          in_=prp_bf[:, c * TT + TT - 1: c * TT + TT]), wait=True)
                pp.phase(vector=scans)

                def st_ys(e, inc, tt=tt):
                    store_fm(e, inc, ysb_d, ysp_bf, 8, TT, tt * TT)
                    store_fm(e, inc, pbb_d, prp_bf, 8, TT, tt * TT)
                pp.phase(gpsimd=st_ys)

            def st_carry(e, inc):
                inc(e.dma_start(out=carry_c, in_=ys_last[:]), 16)
            pp.phase(gpsimd=st_carry)

            def cc_carry(g, inc):
                ins = g.collective_compute("AllGather", ALU.bypass,
                                           replica_groups=[[0, 1], [2, 3], [4, 5], [6, 7]],
                                           ins=[carry_c.opt()], outs=[carry_o.opt()])
                inc(ins, 1)
            pp.phase(gpsimd=cc_carry)

            def ld_carry(e, inc):
                inc(e.dma_start(out=ys_last[:], in_=carry_o[0:128, :]), 16)
            pp.phase(sync=ld_carry)

            def carry_eff(v, inc):
                ins = v.tensor_scalar(out=ys_last[:], in0=ys_last[:], scalar1=cmask[:],
                                      scalar2=None, op0=ALU.mult)
                inc(ins)
            pp.phase(vector=carry_eff)

            for tt in range(NT):
                def ld_ys(e, inc, tt=tt):
                    load_fm(e, inc, ysp_bf, ysb_d, 8, TT, tt * TT)
                    load_fm(e, inc, prp_bf, pbb_d, 8, TT, tt * TT)
                pp.phase(sync=ld_ys)

                def fix(v, inc):
                    for c in range(8):
                        ins = v.scalar_tensor_tensor(out=ch(gx_f, c, TT), in0=ch(prp_bf, c, TT),
                                                     scalar=ys_last[:, c:c + 1],
                                                     in1=ch(ysp_bf, c, TT),
                                                     op0=ALU.mult, op1=ALU.add)
                    inc(ins)
                pp.phase(vector=fix)

                emit_ln(pp, kit, consts, gx_f, pa, pb, out_f32=gx_f)

                def addres(v, inc, tt=tt):
                    for c in range(8):
                        ins = v.tensor_tensor(out=sl_t(c, tt), in0=sl_t(c, tt),
                                              in1=ch(gx_f, c, TT), op=ALU.add)
                    inc(ins)
                pp.phase(vector=addres)

        dump_stream("dbg_x1")

        with contextlib.ExitStack() as s3:
            stage_a = s3.enter_context(nc.sbuf_tensor("s3_sa", [128, C], F32))
            stage_b = s3.enter_context(nc.sbuf_tensor("s3_sb", [128, C], F32))
            cos_bf = s3.enter_context(nc.sbuf_tensor("s3_cos", [128, 8 * C], BF16))
            sin_bf = s3.enter_context(nc.sbuf_tensor("s3_sin", [128, 8 * C], BF16))
            fin_bf = s3.enter_context(nc.sbuf_tensor("s3_fin", [128, 8 * TT], BF16))
            r_f = s3.enter_context(nc.sbuf_tensor("s3_rf", [128, 8 * TT], F32))
            sr8 = s3.enter_context(nc.sbuf_tensor("s3_sr", [128, 8 * TT], F32))
            si8 = s3.enter_context(nc.sbuf_tensor("s3_si", [128, 8 * TT], F32))
            pc3 = s3.enter_context(nc.psum_tensor("s3_pc", [128, TT], F32))
            pd3 = s3.enter_context(nc.psum_tensor("s3_pd", [128, TT], F32))

            for (wa, wb, out_bf, bias) in ((th_d, thn_d, cos_bf, "halfpi"),
                                           (ph_d, phn_d, sin_bf, None)):
                for kc in range(8):
                    def load(e, inc, wa=wa, wb=wb, kc=kc):
                        inc(e.dma_start(out=stage_a[:], in_=wa[kc * 128:(kc + 1) * 128, :]), 16)
                        inc(e.dma_start(out=stage_b[:], in_=wb[kc * 128:(kc + 1) * 128, :]), 16)
                    pp.phase(sync=load)

                    def add(v, inc):
                        ins = v.tensor_tensor(out=stage_a[:], in0=stage_a[:], in1=stage_b[:],
                                              op=ALU.add)
                        inc(ins)
                    pp.phase(vector=add)

                    def trig(s, inc, out_bf=out_bf, bias=bias, kc=kc):
                        if bias is None:
                            ins = s.activation(out=out_bf[:, kc * C:(kc + 1) * C],
                                               in_=stage_a[:], func=AF.Sin)
                        else:
                            ins = s.activation(out=out_bf[:, kc * C:(kc + 1) * C],
                                               in_=stage_a[:], func=AF.Sin,
                                               bias=consts[bias])
                        inc(ins)
                    pp.phase(scalar=trig)

            for tt in range(NT):
                emit_stream_ln(out_bf=fin_bf, tt=tt)

                def mm3(t, i, s):
                    psr = pa if s == 0 else pc3
                    psi = pb if s == 0 else pd3
                    for kc in range(8):
                        t.matmul(psr[:, 0:TT],
                                 cos_bf[:, kc * C + i * 128: kc * C + (i + 1) * 128],
                                 ch(fin_bf, kc, TT), start=(kc == 0), stop=(kc == 7),
                                 skip_group_check=True)
                    for kc in range(8):
                        ins = t.matmul(psi[:, 0:TT],
                                       sin_bf[:, kc * C + i * 128: kc * C + (i + 1) * 128],
                                       ch(fin_bf, kc, TT), start=(kc == 0), stop=(kc == 7),
                                       skip_group_check=True)
                    return ins

                def ev3(sc, i, s):
                    psr = pa if s == 0 else pc3
                    psi = pb if s == 0 else pd3
                    sc.activation(out=ch(sr8, i, TT), in_=psr[:, 0:TT], func=AF.Square)
                    return sc.activation(out=ch(si8, i, TT), in_=psi[:, 0:TT], func=AF.Square)

                pp.pipe2(8, mm3, ev=ev3, ev_eng="scalar")

                def add3(v, inc):
                    for dc in range(8):
                        ins = v.tensor_tensor(out=ch(sr8, dc, TT), in0=ch(sr8, dc, TT),
                                              in1=ch(si8, dc, TT), op=ALU.add)
                    inc(ins)
                pp.phase(vector=add3)

                def root3(s, inc):
                    for dc in range(8):
                        inc(s.activation(out=ch(si8, dc, TT), in_=ch(sr8, dc, TT),
                                         func=AF.Ln), wait=True)
                        inc(s.activation(out=ch(r_f, dc, TT), in_=ch(si8, dc, TT),
                                         func=AF.Exp, scale=0.5), wait=True)
                    inc(s.nop())
                pp.phase(scalar=root3)

                emit_ln(pp, kit, consts, r_f, pa, pb, out_f32=r_f)

                def addres2(v, inc, tt=tt):
                    for c in range(8):
                        ins = v.tensor_tensor(out=sl_t(c, tt), in0=sl_t(c, tt),
                                              in1=ch(r_f, c, TT), op=ALU.add)
                    inc(ins)
                pp.phase(vector=addres2)

        dump_stream("dbg_x2")

        with contextlib.ExitStack() as s4:
            def st_tail(e, inc):
                for c in range(8):
                    inc(e.dma_start(out=tail_c[:, c * 3:(c + 1) * 3],
                                    in_=xsb[:, (c + 1) * TC - 3:(c + 1) * TC]), 16)
            pp.phase(gpsimd=st_tail)

            def cc_tail(g, inc):
                ins = g.collective_compute("AllGather", ALU.bypass,
                                           replica_groups=[[0, 1], [2, 3], [4, 5], [6, 7]],
                                           ins=[tail_c.opt()], outs=[tail_o.opt()])
                inc(ins, 1)
            pp.phase(gpsimd=cc_tail)

            def ld_halo(e, inc):
                inc(e.dma_start(out=halo_sb[:], in_=tail_o[0:128, :]), 16)
            pp.phase(sync=ld_halo)

            def msk_halo(v, inc):
                ins = v.tensor_scalar(out=halo_sb[:], in0=halo_sb[:], scalar1=cmask[:],
                                      scalar2=None, op0=ALU.mult)
                inc(ins)
            pp.phase(vector=msk_halo)

            stage = s4.enter_context(nc.sbuf_tensor("s4_stage", [128, 2048], F32))
            wh_bf = s4.enter_context(nc.sbuf_tensor("s4_wh", [128, 8 * 2048], BF16))
            fin_bf = s4.enter_context(nc.sbuf_tensor("s4_fin", [128, 8 * TC], BF16))
            finh_bf = s4.enter_context(nc.sbuf_tensor("s4_finh", [128, 24], BF16))
            out_bf = s4.enter_context(nc.sbuf_tensor("s4_out", [128, 2 * TT], BF16))
            outh_bf = s4.enter_context(nc.sbuf_tensor("s4_outh", [128, 3], BF16))
            hh1 = s4.enter_context(nc.sbuf_tensor("s4_hh1", [128, 24], F32))
            hh2 = s4.enter_context(nc.sbuf_tensor("s4_hh2", [128, 24], BF16))
            hrs = s4.enter_context(nc.sbuf_tensor("s4_hrs", [128, 3], F32))

            for tt in range(NT):
                emit_stream_ln(out_into=lambda c, tt=tt: fin_bf[:, c * TC + tt * TT:
                                                                c * TC + (tt + 1) * TT], tt=tt)

            def hmu(t, inc):
                for c in range(8):
                    ins = t.matmul(pa[:, 0:3], consts["ones_bf"],
                                   halo_sb.rearrange("p (c w) -> p c w", w=3)[:, c, :]
                                   if False else halo_sb[:, c * 3:(c + 1) * 3],
                                   start=(c == 0), stop=(c == 7))
                inc(ins)
            def hcast(v, inc):
                ins = v.tensor_copy(out=hh2[:], in_=halo_sb[:])
                inc(ins)
            pp.phase(vector=hcast)

            def hmu2(t, inc):
                for c in range(8):
                    ins = t.matmul(pa[:, 0:3], consts["ones_bf"], hh2[:, c * 3:(c + 1) * 3],
                                   start=(c == 0), stop=(c == 7))
                inc(ins)
            pp.phase(tensor=hmu2)

            def hsub(v, inc):
                for c in range(8):
                    ins = v.scalar_tensor_tensor(out=hh1[:, c * 3:(c + 1) * 3], in0=pa[:, 0:3],
                                                 scalar=-1.0 / C, in1=halo_sb[:, c * 3:(c + 1) * 3],
                                                 op0=ALU.mult, op1=ALU.add)
                inc(ins)
            pp.phase(vector=hsub)

            def hsq(s, inc):
                ins = s.activation(out=hh2[:], in_=hh1[:], func=AF.Square)
                inc(ins)
            pp.phase(scalar=hsq)

            def hvar(t, inc):
                for c in range(8):
                    ins = t.matmul(pb[:, 0:3], consts["ones_bf"], hh2[:, c * 3:(c + 1) * 3],
                                   start=(c == 0), stop=(c == 7))
                inc(ins)
            pp.phase(tensor=hvar)

            def hrstd(s, inc):
                inc(s.activation(out=hrs[:], in_=pb[:, 0:3], func=AF.Ln,
                                 scale=1.0 / C, bias=consts["eps5"]), wait=True)
                inc(s.activation(out=hrs[:], in_=hrs[:], func=AF.Exp, scale=-0.5))
            pp.phase(scalar=hrstd)

            def happ(v, inc):
                for c in range(8):
                    ins = v.tensor_tensor(out=finh_bf[:, c * 3:(c + 1) * 3],
                                          in0=hh1[:, c * 3:(c + 1) * 3], in1=hrs[:],
                                          op=ALU.mult)
                inc(ins)
            pp.phase(vector=happ)

            for half, (dram_out, wcol0) in enumerate(((zb_d, 2048), (xmlb_d, 0))):
                for kc in range(8):
                    def loadw(e, inc, kc=kc, wcol0=wcol0):
                        inc(e.dma_start(out=stage[:],
                                        in_=wup_d[kc * 128:(kc + 1) * 128, wcol0:wcol0 + 2048]), 16)
                    pp.phase(sync=loadw)

                    def castw(v, inc, kc=kc):
                        ins = v.tensor_copy(out=wh_bf[:, kc * 2048:(kc + 1) * 2048], in_=stage[:])
                        inc(ins)
                    pp.phase(vector=castw)

                items = [(tt, dc) for tt in range(NT) for dc in range(16)]

                def mmu_(t, i, s, items=items):
                    tt, dc = items[i]
                    ps = pa if s == 0 else pb
                    for kc in range(8):
                        ins = t.matmul(
                            ps[:, 0:TT],
                            wh_bf[:, kc * 2048 + dc * 128: kc * 2048 + (dc + 1) * 128],
                            fin_bf[:, kc * TC + tt * TT: kc * TC + (tt + 1) * TT],
                            start=(kc == 0), stop=(kc == 7), skip_group_check=True)
                    return ins

                def evu_(v, i, s):
                    ps = pa if s == 0 else pb
                    return v.tensor_copy(out=out_bf[:, s * TT:(s + 1) * TT], in_=ps[:, 0:TT])

                def stu_(e, i, s, items=items, dram_out=dram_out, half=half):
                    tt, dc = items[i]
                    off = tt * TT if half == 0 else 3 + tt * TT
                    return e.dma_start(out=dram_out[dc * 128:(dc + 1) * 128, off: off + TT],
                                       in_=out_bf[:, s * TT:(s + 1) * TT])

                pp.pipe2(64, mmu_, ev=evu_, st=stu_)

                if half == 1:
                    for dc in range(16):
                        def mmh(t, inc, dc=dc):
                            for kc in range(8):
                                ins = t.matmul(pa[:, 0:3],
                                               wh_bf[:, kc * 2048 + dc * 128: kc * 2048 + (dc + 1) * 128],
                                               finh_bf[:, kc * 3:(kc + 1) * 3],
                                               start=(kc == 0), stop=(kc == 7))
                            inc(ins)
                        pp.phase(tensor=mmh)

                        def evh(v, inc):
                            ins = v.tensor_copy(out=outh_bf[:], in_=pa[:, 0:3])
                            inc(ins)
                        pp.phase(vector=evh)

                        def sth(e, inc, dc=dc):
                            inc(e.dma_start(out=xmlb_d[dc * 128:(dc + 1) * 128, 0:3],
                                            in_=outh_bf[:]), 16)
                        pp.phase(gpsimd=sth)

        with contextlib.ExitStack() as s5:
            bq_bf = s5.enter_context(nc.sbuf_tensor("s5_bq", [128, 16 * 128], BF16))
            bk_bf = s5.enter_context(nc.sbuf_tensor("s5_bk", [128, 16 * 128], BF16))
            bv_bf = s5.enter_context(nc.sbuf_tensor("s5_bv", [128, 16 * 128], BF16))
            stage2 = s5.enter_context(nc.sbuf_tensor("s5_st2", [128, 16 * 128], F32))
            stage3 = s5.enter_context(nc.sbuf_tensor("s5_st3", [128, 16 * 128], F32))
            wiga = s5.enter_context(nc.sbuf_tensor("s5_wiga", [128, 16 * 8], F32))
            wigb = s5.enter_context(nc.sbuf_tensor("s5_wigb", [128, 16 * 8], F32))
            a_act = s5.enter_context(nc.sbuf_tensor("s5_aact", [128, 16 * 8], BF16))
            a_ml = s5.enter_context(nc.sbuf_tensor("s5_aml", [128, 16 * 8], BF16))
            bg_sb = s5.enter_context(nc.sbuf_tensor("s5_bg", [8, 1], F32))
            cw_sb = s5.enter_context(nc.sbuf_tensor("s5_cw", [128, 64], F32))
            cb_sb = s5.enter_context(nc.sbuf_tensor("s5_cb", [128, 16], F32))
            xml_c = s5.enter_context(nc.sbuf_tensor("s5_xmlc", [128, THL], BF16))
            cv = s5.enter_context(nc.sbuf_tensor("s5_cv", [128, TC], F32))
            xact_c = s5.enter_context(nc.sbuf_tensor("s5_xac", [128, TC], BF16))
            ev_bf = s5.enter_context(nc.sbuf_tensor("s5_ev", [128, 2 * TT], BF16))
            g_f = s5.enter_context(nc.sbuf_tensor("s5_gf", [8, TC], F32))
            pg = s5.enter_context(nc.psum_tensor("s5_pg", [8, TC], F32))
            p128 = s5.enter_context(nc.psum_tensor("s5_p128", [128, 128], F32))

            def load_small(e, inc):
                inc(e.dma_start(out=cw_sb[:], in_=cw_d), 16)
                inc(e.dma_start(out=cb_sb[:], in_=cb_d), 16)
                inc(e.dma_start(out=bg_sb[:], in_=bg_d), 16)
            pp.phase(sync=load_small)

            for src_d, dst in ((bq_d, bq_bf), (bk_d, bk_bf), (bv_d, bv_bf)):
                def loadw(e, inc, src_d=src_d):
                    for c in range(16):
                        inc(e.dma_start(out=stage2[:, c * 128:(c + 1) * 128],
                                        in_=src_d[c * 128:(c + 1) * 128, :]), 16)
                pp.phase(sync=loadw)

                def castw(v, inc, dst=dst):
                    ins = v.tensor_copy(out=dst[:], in_=stage2[:])
                    inc(ins)
                pp.phase(vector=castw)

            def load_fold1(e, inc):
                for c in range(16):
                    inc(e.dma_start(out=stage2[:, c * 128:(c + 1) * 128],
                                    in_=bqT_d[c * 128:(c + 1) * 128, :]), 16)
                    inc(e.dma_start(out=stage3[:, c * 128:(c + 1) * 128],
                                    in_=bkT_d[c * 128:(c + 1) * 128, :]), 16)
                    inc(e.dma_start(out=wiga[:, c * 8:(c + 1) * 8],
                                    in_=wigq_d[c * 128:(c + 1) * 128, :]), 16)
                    inc(e.dma_start(out=wigb[:, c * 8:(c + 1) * 8],
                                    in_=wigk_d[c * 128:(c + 1) * 128, :]), 16)
            pp.phase(sync=load_fold1)

            def fold1(t, inc):
                for c in range(16):
                    t.matmul(p128[:, c * 8:(c + 1) * 8], stage2[:, c * 128:(c + 1) * 128],
                             wiga[:, c * 8:(c + 1) * 8], start=True, stop=False)
                    ins = t.matmul(p128[:, c * 8:(c + 1) * 8], stage3[:, c * 128:(c + 1) * 128],
                                   wigb[:, c * 8:(c + 1) * 8], start=False, stop=True)
                inc(ins)
            pp.phase(tensor=fold1)

            def evf1(v, inc):
                ins = v.tensor_copy(out=a_act[:], in_=p128[:, 0:128])
                inc(ins)
            pp.phase(vector=evf1)

            def load_fold2(e, inc):
                for c in range(16):
                    inc(e.dma_start(out=stage2[:, c * 128:(c + 1) * 128],
                                    in_=bvT_d[c * 128:(c + 1) * 128, :]), 16)
                    inc(e.dma_start(out=wiga[:, c * 8:(c + 1) * 8],
                                    in_=wigv_d[c * 128:(c + 1) * 128, :]), 16)
            pp.phase(sync=load_fold2)

            def fold2(t, inc):
                for c in range(16):
                    ins = t.matmul(p128[:, c * 8:(c + 1) * 8], stage2[:, c * 128:(c + 1) * 128],
                                   wiga[:, c * 8:(c + 1) * 8], start=True, stop=True)
                inc(ins)
            pp.phase(tensor=fold2)

            def evf2(v, inc):
                ins = v.tensor_copy(out=a_ml[:], in_=p128[:, 0:128])
                inc(ins)
            pp.phase(vector=evf2)

            for c in range(16):
                def ld_xml(e, inc, c=c):
                    inc(e.dma_start(out=xml_c[:], in_=xmlb_d[c * 128:(c + 1) * 128, :]), 16)
                pp.phase(sync=ld_xml)

                def conv(v, inc, c=c):
                    v.tensor_scalar(out=cv[:], in0=xml_c[:, 0:TC],
                                    scalar1=cw_sb[:, c * 4:c * 4 + 1], scalar2=None, op0=ALU.mult)
                    for i in (1, 2, 3):
                        ins = v.scalar_tensor_tensor(out=cv[:], in0=xml_c[:, i:i + TC],
                                                     scalar=cw_sb[:, c * 4 + i:c * 4 + i + 1],
                                                     in1=cv[:], op0=ALU.mult, op1=ALU.add)
                    inc(ins)
                pp.phase(vector=conv)

                def act(s, inc, c=c):
                    ins = s.activation(out=xact_c[:], in_=cv[:], func=AF.Silu,
                                       bias=cb_sb[:, c:c + 1])
                    inc(ins)
                pp.phase(scalar=act)

                def st_xa(e, inc, c=c):
                    inc(e.dma_start(out=xab_d[c * 128:(c + 1) * 128, :], in_=xact_c[:]), 16)
                pp.phase(gpsimd=st_xa)

                qk_items = [(w, tt) for w in range(2) for tt in range(NT)]

                def mmqk(t, i, s, c=c):
                    w, tt = qk_items[i]
                    w_bf = bq_bf if w == 0 else bk_bf
                    ps = pa if s == 0 else pb
                    return t.matmul(ps[:, 0:TT], w_bf[:, c * 128:(c + 1) * 128],
                                    xact_c[:, tt * TT:(tt + 1) * TT],
                                    start=True, stop=True, skip_group_check=True)

                def evqk(v, i, s):
                    ps = pa if s == 0 else pb
                    return v.tensor_copy(out=ev_bf[:, s * TT:(s + 1) * TT], in_=ps[:, 0:TT])

                def stqk(e, i, s, c=c):
                    w, tt = qk_items[i]
                    dram_out = qb_d if w == 0 else kvb_d
                    return e.dma_start(out=dram_out[c * 128:(c + 1) * 128,
                                                    tt * TT:(tt + 1) * TT],
                                       in_=ev_bf[:, s * TT:(s + 1) * TT])

                pp.pipe2(8, mmqk, ev=evqk, st=stqk)

                def mmg(t, inc, c=c):
                    for tt in range(NT):
                        t.matmul(pg[:, tt * TT:(tt + 1) * TT], a_act[:, c * 8:(c + 1) * 8],
                                 xact_c[:, tt * TT:(tt + 1) * TT],
                                 start=(c == 0), stop=False, skip_group_check=True)
                        ins = t.matmul(pg[:, tt * TT:(tt + 1) * TT], a_ml[:, c * 8:(c + 1) * 8],
                                       xml_c[:, 3 + tt * TT: 3 + (tt + 1) * TT],
                                       start=False, stop=(c == 15), skip_group_check=True)
                    inc(ins)
                pp.phase(tensor=mmg)

            def evg(v, inc):
                ins = v.tensor_scalar(out=g_f[:], in0=pg[:, 0:TC], scalar1=bg_sb[:],
                                      scalar2=None, op0=ALU.add)
                inc(ins)
            pp.phase(vector=evg)

            def st_g(e, inc):
                inc(e.dma_start(out=gb_d, in_=g_f[:]), 16)
            pp.phase(gpsimd=st_g)

            for tau in range(16):
                def ld_tau(e, inc, tau=tau):
                    for c in range(16):
                        inc(e.dma_start(out=xml_c[:, c * 128:(c + 1) * 128],
                                        in_=xmlb_d[c * 128:(c + 1) * 128,
                                                   3 + tau * 128: 3 + (tau + 1) * 128]), 16)
                pp.phase(sync=ld_tau)

                def mmv_(t, i, s, tau=tau):
                    for j in range(4):
                        c = 4 * i + j
                        ps = pa if s == 0 else pb
                        ins = t.matmul(ps[:, j * 128:(j + 1) * 128],
                                       xml_c[:, c * 128:(c + 1) * 128],
                                       bv_bf[:, c * 128:(c + 1) * 128],
                                       start=True, stop=True, skip_group_check=True)
                    return ins

                def evv_(v, i, s):
                    ps = pa if s == 0 else pb
                    return v.tensor_copy(out=ev_bf[:, s * TT:(s + 1) * TT], in_=ps[:, 0:TT])

                def stv_(e, i, s, tau=tau):
                    return e.dma_start(out=kvb_d[2048 + tau * 128: 2048 + (tau + 1) * 128,
                                              i * 512:(i + 1) * 512],
                                       in_=ev_bf[:, s * TT:(s + 1) * TT])

                pp.pipe2(4, mmv_, ev=evv_, st=stv_)

            def cc_kv(g, inc):
                g.collective_compute("AllGather", ALU.bypass,
                                     replica_groups=[[0, 1], [2, 3], [4, 5], [6, 7]],
                                     ins=[gb_d.opt()], outs=[g_o.opt()]).then_inc(pp.h["pv"], 1)
                g.collective_compute("AllGather", ALU.bypass,
                                     replica_groups=[[0, 1], [2, 3], [4, 5], [6, 7]],
                                     ins=[kvb_d.opt()], outs=[kv_o.opt()]).then_inc(pp.h["pv"], 1)
                inc(g.nop(), 1)
            pp.phase(gpsimd=cc_kv)
            cc_base = pp.hc["pv"]
            pp.hc["pv"] += 2

        with contextlib.ExitStack() as s6:
            igf = s6.enter_context(nc.sbuf_tensor("s6_ig", [4, TF], F32))
            fgf = s6.enter_context(nc.sbuf_tensor("s6_fg", [4, TF], F32))
            Ff = s6.enter_context(nc.sbuf_tensor("s6_F", [4, TF], F32))
            pois = s6.enter_context(nc.sbuf_tensor("s6_po", [4, TF], F32))
            own_a = s6.enter_context(nc.sbuf_tensor("s6_oa", [4, TC], F32))
            own_b = s6.enter_context(nc.sbuf_tensor("s6_ob", [4, TC], F32))
            own_c = s6.enter_context(nc.sbuf_tensor("s6_oc", [4, TC], F32))

            def ld_g(e, inc):
                e.wait_ge(pp.h["pv"], cc_base + 1)
                inc(e.dma_start(out=igf[:, 0:TC], in_=g_o[0:4, :]), 16)
                inc(e.dma_start(out=igf[:, TC:TF], in_=gb_d[0:4, :]), 16)
                inc(e.dma_start(out=fgf[:, 0:TC], in_=g_o[4:8, :]), 16)
                inc(e.dma_start(out=fgf[:, TC:TF], in_=gb_d[4:8, :]), 16)
            pp.phase(sync=ld_g)

            def lf1(s, inc):
                inc(s.activation(out=Ff[:], in_=fgf[:], func=AF.Abs), wait=True)
                inc(s.activation(out=Ff[:], in_=Ff[:], func=AF.Exp, scale=-1.0), wait=True)
                inc(s.activation(out=Ff[:], in_=Ff[:], func=AF.Ln,
                                 bias=consts["one"][0:4, :]), wait=True)
                inc(s.activation(out=own_a[:], in_=fgf[:, 0:TC], func=AF.Relu), wait=True)
                inc(s.activation(out=own_b[:], in_=fgf[:, TC:TF], func=AF.Relu))
            pp.phase(scalar=lf1)

            def lf2(v, inc):
                inc(v.tensor_tensor(out=fgf[:, 0:TC], in0=fgf[:, 0:TC], in1=own_a[:],
                                    op=ALU.subtract), wait=True)
                inc(v.tensor_tensor(out=fgf[:, TC:TF], in0=fgf[:, TC:TF], in1=own_b[:],
                                    op=ALU.subtract), wait=True)
                inc(v.tensor_tensor(out=fgf[:], in0=fgf[:], in1=Ff[:], op=ALU.subtract), wait=True)
                inc(v.memset(pois[:], 1.0), wait=True)
                inc(v.tensor_tensor_scan(out=Ff[:], data0=pois[:], data1=fgf[:],
                                         initial=0.0, op0=ALU.mult, op1=ALU.add), wait=True)
                inc(v.tensor_tensor(out=igf[:], in0=igf[:], in1=Ff[:], op=ALU.subtract))
            pp.phase(vector=lf2)

            def ld_pois(e, inc):
                inc(e.dma_start(out=pois[:], in_=poison_d), 16)
            pp.phase(sync=ld_pois)

            def lf3(v, inc):
                inc(v.tensor_tensor(out=igf[:], in0=igf[:], in1=pois[:], op=ALU.add), wait=True)
                inc(v.tensor_tensor_scan(out=fgf[:], data0=igf[:], data1=igf[:],
                                         initial=NEG_BIG, op0=ALU.max, op1=ALU.max), wait=True)
                inc(v.tensor_scalar(out=own_a[:], in0=Ff[:, 0:TC], scalar1=hm4c[:],
                                    scalar2=None, op0=ALU.mult), wait=True)
                inc(v.scalar_tensor_tensor(out=own_a[:], in0=Ff[:, TC:TF], scalar=hm4[:],
                                           in1=own_a[:], op0=ALU.mult, op1=ALU.add), wait=True)
                inc(v.tensor_scalar(out=own_b[:], in0=fgf[:, 0:TC], scalar1=hm4c[:],
                                    scalar2=None, op0=ALU.mult), wait=True)
                inc(v.scalar_tensor_tensor(out=own_b[:], in0=fgf[:, TC:TF], scalar=hm4[:],
                                           in1=own_b[:], op0=ALU.mult, op1=ALU.add), wait=True)
                inc(v.tensor_tensor(out=own_c[:], in0=own_a[:], in1=own_b[:], op=ALU.add), wait=True)
                inc(v.tensor_scalar(out=own_c[:], in0=own_c[:], scalar1=-1.0,
                                    scalar2=None, op0=ALU.mult))
            pp.phase(vector=lf3)

            def st_rows(e, inc):
                inc(e.dma_start(out=grow_d, in_=igf[:]), 16)
                inc(e.dma_start(out=gown_d[0:4, :], in_=own_b[:]), 16)
                inc(e.dma_start(out=gown_d[4:8, :], in_=own_c[:]), 16)
            pp.phase(gpsimd=st_rows)

        with contextlib.ExitStack() as s7:
            class _Kit4:
                pass
            kit4 = _Kit4()
            kit4.nch, kit4.W = 4, QBF
            kit4.xbf = s7.enter_context(nc.sbuf_tensor("ln4_xbf", [128, 4 * QBF], BF16))
            kit4.d = s7.enter_context(nc.sbuf_tensor("ln4_d", [128, 4 * QBF], F32))
            kit4.rstd = s7.enter_context(nc.sbuf_tensor("ln4_rb", [128, QBF], F32))
            kit4.rstd_a = kit4.rstd
            khead = s7.enter_context(nc.sbuf_tensor("s7_kh", [128, 4 * TF], BF16))
            vhead = s7.enter_context(nc.sbuf_tensor("s7_vh", [128, 32 * 512], BF16))
            q_t = s7.enter_context(nc.sbuf_tensor("s7_q", [128, 4 * QBF], BF16))
            t32 = s7.enter_context(nc.sbuf_tensor("s7_t32", [32, 128], F32))
            igmF_tm = s7.enter_context(nc.sbuf_tensor("s7_itm", [128, 32], F32))
            rm_qb = s7.enter_context(nc.sbuf_tensor("s7_rm", [1, QBF], F32))
            en_qb = s7.enter_context(nc.sbuf_tensor("s7_en", [1, QBF], F32))
            cq_row = s7.enter_context(nc.sbuf_tensor("s7_cqr", [1, 4], F32))
            cq_b = s7.enter_context(nc.sbuf_tensor("s7_cqb", [128, 4], F32))
            ncq3 = s7.enter_context(nc.sbuf_tensor("s7_ncq", [128, 4], F32))
            colsc = s7.enter_context(nc.sbuf_tensor("s7_cs", [128, 36], F32))
            rf_row = s7.enter_context(nc.sbuf_tensor("s7_rfr", [1, QBF], F32))
            den_r = s7.enter_context(nc.sbuf_tensor("s7_den", [1, QBF], F32))
            nrm_r = den_r
            sc_row = s7.enter_context(nc.sbuf_tensor("s7_scr", [1, QBF], F32))
            screp = s7.enter_context(nc.sbuf_tensor("s7_srep", [128, QBF], F32))
            T_sb = s7.enter_context(nc.sbuf_tensor("s7_T", [128, QBF], BF16))
            T_sbB = s7.enter_context(nc.sbuf_tensor("s7_TB", [128, QBF], BF16))
            T2 = [T_sb, T_sbB]
            hfm = s7.enter_context(nc.sbuf_tensor("s7_hfm", [128, 4 * QBF], F32))
            hn4 = s7.enter_context(nc.sbuf_tensor("s7_hn", [128, 4 * QBF], BF16))
            mbigA = s7.enter_context(nc.sbuf_tensor("s7_mA", [128, 2 * QBF], BF16))
            mbigB = s7.enter_context(nc.sbuf_tensor("s7_mB", [128, 2 * QBF], BF16))
            ones_pv = s7.enter_context(nc.sbuf_tensor("s7_1pv", [128, 1], BF16))
            ps_den = s7.enter_context(nc.psum_tensor("s7_psd", [1, QBF], F32))
            ps_acc = [s7.enter_context(nc.psum_tensor(f"s7_pa{i}", [128, QBF], F32))
                      for i in range(4)]
            ps_s = pa
            ps_bc = pb
            PS2 = [pa, pb]

            def ld_masks(e, inc):
                inc(e.dma_start(out=mbigA[:], in_=mbigA_d), 16)
                inc(e.dma_start(out=mbigB[:], in_=mbigB_d), 16)
            pp.phase(sync=ld_masks)

            def mk1(v, inc):
                ins = v.memset(ones_pv[:], 1.0)
                inc(ins)
            pp.phase(vector=mk1)

            for n in range(4):
                def ld_head(e, inc, n=n):
                    e.wait_ge(pp.h["pv"], cc_base + 2)
                    for c in range(4):
                        r0 = n * 512 + c * 128
                        inc(e.dma_start(out=khead[:, c * TF: c * TF + TC],
                                        in_=kv_o[r0:r0 + 128, :]), 16)
                        inc(e.dma_start(out=khead[:, c * TF + TC: (c + 1) * TF],
                                        in_=kvb_d[r0:r0 + 128, :]), 16)
                    for j in range(16):
                        inc(e.dma_start(out=vhead[:, j * 512:(j + 1) * 512],
                                        in_=kv_o[2048 + j * 128: 2048 + (j + 1) * 128,
                                                 n * 512:(n + 1) * 512]), 16)
                    for j in range(16):
                        inc(e.dma_start(out=vhead[:, (16 + j) * 512:(17 + j) * 512],
                                        in_=kvb_d[2048 + j * 128: 2048 + (j + 1) * 128,
                                                  n * 512:(n + 1) * 512]), 16)
                    inc(e.dma_start(out=t32[:],
                                    in_=grow_d[n:n + 1, :].rearrange("one (a b) -> (one a) b", b=128)), 16)
                    inc(e.dma_start(out=cq_row[:],
                                    in_=gown_d[n:n + 1, QBF - 1:TC:QBF],
                                    allow_slow_non_contiguous=True), 16)
                pp.phase(sync=ld_head)

                def tp(t, inc):
                    ins = t.transpose(out=ps_bc[:, 0:32], in_=t32[:], identity=ident[0:32, 0:32])
                    inc(ins)
                pp.phase(tensor=tp)

                def cp_tm(v, inc):
                    ins = v.tensor_copy(out=igmF_tm[:], in_=ps_bc[:, 0:32])
                    inc(ins)
                pp.phase(vector=cp_tm)

                def cq2(t, inc):
                    ins = t.matmul(ps_bc[:, 0:4], consts["ones_row"], cq_row[:],
                                   start=True, stop=True)
                    inc(ins)
                pp.phase(tensor=cq2)

                def cq3(v, inc):
                    inc(v.tensor_copy(out=cq_b[:], in_=ps_bc[:, 0:4]), wait=True)
                    inc(v.tensor_scalar(out=ncq3[:], in0=cq_b[:], scalar1=CSC_SHIFT,
                                        scalar2=-1.0, op0=ALU.add, op1=ALU.mult))
                pp.phase(vector=cq3)

                for qb in range(4):
                    nk = 16 + 4 * qb + 4

                    def ld_qbrows(e, inc, n=n, qb=qb):
                        inc(e.dma_start(out=rm_qb[:],
                                        in_=gown_d[n:n + 1, qb * QBF:(qb + 1) * QBF]), 16)
                        inc(e.dma_start(out=en_qb[:],
                                        in_=gown_d[4 + n:5 + n, qb * QBF:(qb + 1) * QBF]), 16)
                    pp.phase(sync=ld_qbrows)

                    def gsc(s, inc, qb=qb, nk=nk):
                        s.activation(out=colsc[:, 0:nk], in_=igmF_tm[:, 0:nk], func=AF.Exp,
                                     bias=ncq3[:, qb:qb + 1])
                        inc(s.activation(out=en_qb[:], in_=en_qb[:], func=AF.Exp), wait=True)
                        ins = s.activation(out=rf_row[:], in_=rm_qb[:],
                                           func=AF.Exp, scale=-1.0,
                                           bias=cq_b[0:1, qb:qb + 1])
                        inc(ins)
                    pp.phase(scalar=gsc)

                    if 4 * qb + 4 < 16:
                        def gz(v, inc, qb=qb):
                            ins = v.tensor_scalar(out=colsc[:, 4 * qb + 4:16],
                                                  in0=colsc[:, 4 * qb + 4:16],
                                                  scalar1=cmask[:], scalar2=None,
                                                  op0=ALU.mult)
                            inc(ins)
                        pp.phase(vector=gz)

                    def ld_q(e, inc, n=n, qb=qb):
                        for c in range(4):
                            inc(e.dma_start(out=q_t[:, c * QBF:(c + 1) * QBF],
                                            in_=qb_d[n * 512 + c * 128: n * 512 + (c + 1) * 128,
                                                     qb * QBF:(qb + 1) * QBF]), 16)
                    pp.phase(sync=ld_q)

                    h, hc = pp.h, pp.hc
                    pre = dict(pp.cnt)
                    sc0, ev0, pv0 = hc["mm"], hc["ev"], hc["pv"]
                    jorder = [nk - 1 - k for k in range(nk)]

                    def body_pe(t, qb=qb, nk=nk, pre=pre, sc0=sc0, ev0=ev0, pv0=pv0,
                                jorder=jorder):
                        for o in ENGS:
                            if o != "tensor" and pre[o] > 0:
                                t.wait_ge(pp.sems[o], pre[o])

                        def pv(k, last):
                            j = jorder[k]
                            st_, sp_ = (k == 0), last
                            for fc in range(4):
                                t.matmul(ps_acc[fc][:, 0:QBF],
                                         vhead[:, j * 512 + fc * 128: j * 512 + (fc + 1) * 128],
                                         T2[k % 2][:], start=st_, stop=sp_,
                                         skip_group_check=True)
                            t.matmul(ps_den[:, 0:QBF], ones_pv[:], T2[k % 2][:],
                                     start=st_, stop=sp_,
                                     skip_group_check=True).then_inc(h["pv"], 1)

                        for k in range(nk):
                            j = jorder[k]
                            if k >= 2:
                                t.wait_ge(h["ev"], ev0 + k - 1)
                            for c in range(4):
                                ins = t.matmul(PS2[k % 2][:, 0:QBF],
                                               khead[:, c * TF + j * 128: c * TF + (j + 1) * 128],
                                               q_t[:, c * QBF:(c + 1) * QBF],
                                               start=(c == 0), stop=(c == 3),
                                               skip_group_check=True)
                            ins.then_inc(h["mm"], 1)
                            if k >= 1:
                                t.wait_ge(h["ev"], ev0 + k)
                                pv(k - 1, last=False)
                        t.wait_ge(h["ev"], ev0 + nk)
                        pv(nk - 1, last=True)
                    pp.block.tensor(body_pe)
                    hc["mm"] += nk
                    hc["pv"] += nk

                    def body_ve(v, qb=qb, nk=nk, pre=pre, sc0=sc0, pv0=pv0,
                                jorder=jorder):
                        for o in ENGS:
                            if o != "vector" and pre[o] > 0:
                                v.wait_ge(pp.sems[o], pre[o])
                        for k in range(nk):
                            j = jorder[k]
                            v.wait_ge(h["mm"], sc0 + k + 1)
                            if k >= 2:
                                v.wait_ge(h["pv"], pv0 + k - 1)
                            rA = j - 4 * qb
                            rB = j - 16 - 4 * qb
                            if 0 <= rA <= 3:
                                ins = v.scalar_tensor_tensor(
                                    out=T2[k % 2][:], in0=PS2[k % 2][:, 0:QBF],
                                    scalar=colsc[:, j:j + 1],
                                    in1=mbigA[:, QBF - 128 * rA: 2 * QBF - 128 * rA],
                                    op0=ALU.mult, op1=ALU.mult)
                            elif 0 <= rB <= 3:
                                ins = v.scalar_tensor_tensor(
                                    out=T2[k % 2][:], in0=PS2[k % 2][:, 0:QBF],
                                    scalar=colsc[:, j:j + 1],
                                    in1=mbigB[:, QBF - 128 * rB: 2 * QBF - 128 * rB],
                                    op0=ALU.mult, op1=ALU.mult)
                            else:
                                ins = v.tensor_scalar(out=T2[k % 2][:], in0=PS2[k % 2][:, 0:QBF],
                                                      scalar1=colsc[:, j:j + 1],
                                                      scalar2=None, op0=ALU.mult)
                            ins.then_inc(h["ev"], 1)
                    pp.block.vector(body_ve)
                    hc["ev"] += nk

                    def mark_t(t, hcpv=hc["pv"]):
                        t.wait_ge(h["pv"], hcpv)
                        t.nop().then_inc(pp.sems["tensor"], 1)
                    pp.block.tensor(mark_t)
                    pp.cnt["tensor"] += 1

                    def mark_v(v, hcev=hc["ev"]):
                        v.wait_ge(h["ev"], hcev)
                        v.nop().then_inc(pp.sems["vector"], 1)
                    pp.block.vector(mark_v)
                    pp.cnt["vector"] += 1

                    def ep1(v, inc, qb=qb):
                        inc(v.tensor_tensor(out=den_r[:], in0=ps_den[:, 0:QBF], in1=rf_row[:],
                                            op=ALU.mult))
                    pp.phase(vector=ep1)

                    def ep1b(s, inc):
                        inc(s.activation(out=nrm_r[:], in_=den_r[:], func=AF.Abs))
                    pp.phase(scalar=ep1b)

                    def ep1c(v, inc):
                        inc(v.tensor_tensor(out=nrm_r[:], in0=nrm_r[:],
                                            in1=en_qb[:], op=ALU.max))
                    pp.phase(vector=ep1c)

                    def ep2(s, inc):
                        inc(s.activation(out=nrm_r[:], in_=nrm_r[:], func=AF.Ln,
                                         bias=consts["m1e6"][0:1, :]), wait=True)
                        inc(s.activation(out=nrm_r[:], in_=nrm_r[:], func=AF.Exp, scale=-1.0))
                    pp.phase(scalar=ep2)

                    def ep3(v, inc):
                        inc(v.tensor_tensor(out=sc_row[:], in0=rf_row[:], in1=nrm_r[:],
                                            op=ALU.mult))
                    pp.phase(vector=ep3)

                    def ep4(t, inc):
                        ins = t.matmul(ps_bc[:, 0:QBF], consts["ones_row"], sc_row[:],
                                       start=True, stop=True)
                        inc(ins)
                    pp.phase(tensor=ep4)

                    def ep5(v, inc):
                        inc(v.tensor_copy(out=screp[:], in_=ps_bc[:, 0:QBF]), wait=True)
                        for fc in range(4):
                            inc(v.tensor_tensor(out=ch(hfm, fc, QBF), in0=ps_acc[fc][:, 0:QBF],
                                                in1=screp[:], op=ALU.mult), wait=True)
                        for fc in range(4):
                            ins = v.tensor_copy(out=ch(kit4.xbf, fc, QBF), in_=ch(hfm, fc, QBF))
                        inc(ins)
                    pp.phase(vector=ep5)

                    def mmu(t, inc):
                        for fc in range(4):
                            ins = t.matmul(ps_s[:, 0:QBF], consts["ones_bf"], ch(kit4.xbf, fc, QBF),
                                           start=(fc == 0), stop=(fc == 3))
                        inc(ins)
                    pp.phase(tensor=mmu)

                    def msub(v, inc):
                        for fc in range(4):
                            ins = v.scalar_tensor_tensor(out=ch(kit4.d, fc, QBF), in0=ps_s[:, 0:QBF],
                                                         scalar=-1.0 / DH, in1=ch(hfm, fc, QBF),
                                                         op0=ALU.mult, op1=ALU.add)
                        inc(ins)
                    pp.phase(vector=msub)

                    def msq(s, inc):
                        for fc in range(4):
                            ins = s.activation(out=ch(kit4.xbf, fc, QBF), in_=ch(kit4.d, fc, QBF),
                                               func=AF.Square)
                        inc(ins)
                    pp.phase(scalar=msq)

                    def mvar(t, inc):
                        for fc in range(4):
                            ins = t.matmul(ps_bc[:, 0:QBF], consts["ones_bf"], ch(kit4.xbf, fc, QBF),
                                           start=(fc == 0), stop=(fc == 3))
                        inc(ins)
                    pp.phase(tensor=mvar)

                    def mrstd(s, inc):
                        inc(s.activation(out=kit4.rstd[:], in_=ps_bc[:, 0:QBF], func=AF.Ln,
                                         scale=1.0 / DH, bias=consts["eps6"]), wait=True)
                        inc(s.activation(out=kit4.rstd[:], in_=kit4.rstd[:], func=AF.Exp,
                                         scale=-0.5))
                    pp.phase(scalar=mrstd)

                    def mout(v, inc, n=n):
                        for fc in range(4):
                            ins = v.scalar_tensor_tensor(out=hn4[:, fc * QBF:(fc + 1) * QBF],
                                                         in0=ch(kit4.d, fc, QBF),
                                                         scalar=nw_sb[:, n * 4 + fc:n * 4 + fc + 1],
                                                         in1=kit4.rstd[:],
                                                         op0=ALU.mult, op1=ALU.mult)
                        inc(ins)
                    pp.phase(vector=mout)

                    def mst(e, inc, n=n, qb=qb):
                        for fc in range(4):
                            inc(e.dma_start(out=hnb_d[n * 512 + fc * 128: n * 512 + (fc + 1) * 128,
                                                      qb * QBF:(qb + 1) * QBF],
                                            in_=hn4[:, fc * QBF:(fc + 1) * QBF]), 16)
                    pp.phase(gpsimd=mst)

        with contextlib.ExitStack() as s8:
            w_t = s8.enter_context(nc.sbuf_tensor("s8_wt", [128, 2 * 16 * 128], BF16))
            xa_t = s8.enter_context(nc.sbuf_tensor("s8_xa", [128, 16 * TT], BF16))
            z_t = s8.enter_context(nc.sbuf_tensor("s8_z", [128, 16 * TT], BF16))
            hn_t = s8.enter_context(nc.sbuf_tensor("s8_hn", [128, 16 * TT], BF16))
            ao_f = s8.enter_context(nc.sbuf_tensor("s8_ao", [128, 8 * TT], F32))
            sz = s8.enter_context(nc.sbuf_tensor("s8_sz", [128, TT], F32))
            tmf = s8.enter_context(nc.sbuf_tensor("s8_tmf", [128, TT], F32))

            for tt in range(NT):
                def load_t(e, inc, tt=tt):
                    load_fm(e, inc, xa_t, xab_d, 16, TT, tt * TT)
                    load_fm(e, inc, z_t, zb_d, 16, TT, tt * TT)
                    load_fm(e, inc, hn_t, hnb_d, 16, TT, tt * TT)
                pp.phase(sync=load_t)

                for c in range(16):
                    def hs1(s, inc, c=c):
                        ins = s.activation(out=sz[:], in_=ch(z_t, c, TT), func=AF.Silu)
                        inc(ins)
                    pp.phase(scalar=hs1)

                    def hs2(v, inc, c=c):
                        inc(v.scalar_tensor_tensor(out=tmf[:], in0=ch(xa_t, c, TT),
                                                   scalar=skip_sb[:, c:c + 1],
                                                   in1=ch(hn_t, c, TT),
                                                   op0=ALU.mult, op1=ALU.add), wait=True)
                        inc(v.tensor_tensor(out=ch(hn_t, c, TT), in0=tmf[:], in1=sz[:],
                                            op=ALU.mult))
                    pp.phase(vector=hs2)

                def ldd(e, i, s):
                    for kc in range(16):
                        ins = e.dma_start(out=w_t[:, s * 2048 + kc * 128: s * 2048 + (kc + 1) * 128],
                                          in_=wdbf_d[kc * 128:(kc + 1) * 128,
                                                     i * 128:(i + 1) * 128])
                        if kc < 15:
                            ins.then_inc(pp.h["ld"], 16)
                    return ins

                def mmd(t, i, s):
                    ps = pa if s == 0 else pb
                    for kc in range(16):
                        ins = t.matmul(ps[:, 0:TT],
                                       w_t[:, s * 2048 + kc * 128: s * 2048 + (kc + 1) * 128],
                                       ch(hn_t, kc, TT), start=(kc == 0), stop=(kc == 15),
                                       skip_group_check=True)
                    return ins

                def evd(v, i, s):
                    ps = pa if s == 0 else pb
                    return v.tensor_copy(out=ch(ao_f, i, TT), in_=ps[:, 0:TT])

                pp.pipe2(8, mmd, ev=evd, ld=ldd, ld_units=16)

                emit_ln(pp, kit, consts, ao_f, pa, pb, out_f32=ao_f)

                def addres(v, inc, tt=tt):
                    for c in range(8):
                        ins = v.tensor_tensor(out=sl_t(c, tt), in0=sl_t(c, tt),
                                              in1=ch(ao_f, c, TT), op=ALU.add)
                    inc(ins)
                pp.phase(vector=addres)

        dump_stream("dbg_x3")
        if DBG:
            def dump_rows(e, inc):
                inc(e.dma_start(out=dbg["dbg_gown"], in_=gown_d), 16)
                inc(e.dma_start(out=dbg["dbg_grow"], in_=grow_d), 16)
                for r0 in range(0, 2048, 128):
                    inc(e.dma_start(out=dbg["dbg_hn"][r0:r0 + 128, :],
                                    in_=hnb_d[r0:r0 + 128, :]), 16)
            pp.phase(gpsimd=dump_rows)

        with contextlib.ExitStack() as s9:
            w1t = s9.enter_context(nc.sbuf_tensor("s9_w1t", [128, 2 * 1024], BF16))
            w2t = s9.enter_context(nc.sbuf_tensor("s9_w2t", [128, 2 * C], BF16))
            b1_sb = s9.enter_context(nc.sbuf_tensor("s9_b1", [128, 32], F32))
            b2_sb = s9.enter_context(nc.sbuf_tensor("s9_b2", [128, 8], F32))
            fin_bf = s9.enter_context(nc.sbuf_tensor("s9_fin", [128, 8 * TT], BF16))
            hh_bf = s9.enter_context(nc.sbuf_tensor("s9_hh", [128, 32 * TT], BF16))
            ffn_f = s9.enter_context(nc.sbuf_tensor("s9_ffn", [128, 8 * TT], F32))
            pcs = [s9.enter_context(nc.psum_tensor(f"s9_pc{i}", [128, TT], F32))
                   for i in range(4)]

            def load_b(e, inc):
                inc(e.dma_start(out=b1_sb[:], in_=b1_d), 16)
                inc(e.dma_start(out=b2_sb[:], in_=b2_d), 16)
            pp.phase(sync=load_b)

            for tt in range(NT):
                emit_stream_ln(out_bf=fin_bf, tt=tt)

                def ld1(e, i, s):
                    for kc in range(8):
                        ins = e.dma_start(out=w1t[:, s * 1024 + kc * 128: s * 1024 + (kc + 1) * 128],
                                          in_=w1bf_d[kc * 128:(kc + 1) * 128,
                                                     i * 128:(i + 1) * 128])
                        if kc < 7:
                            ins.then_inc(pp.h["ld"], 16)
                    return ins

                def mm1(t, i, s):
                    ps = pa if s == 0 else pb
                    for kc in range(8):
                        ins = t.matmul(ps[:, 0:TT],
                                       w1t[:, s * 1024 + kc * 128: s * 1024 + (kc + 1) * 128],
                                       ch(fin_bf, kc, TT), start=(kc == 0), stop=(kc == 7),
                                       skip_group_check=True)
                    return ins

                def gl1(sc, i, s):
                    ps = pa if s == 0 else pb
                    return sc.activation(out=ch(hh_bf, i, TT), in_=ps[:, 0:TT],
                                         func=AF.Gelu, bias=b1_sb[:, i:i + 1])

                pp.pipe2(32, mm1, ev=gl1, ld=ld1, ev_eng="scalar", ld_units=8)

                for dcg in range(2):
                    def ld2(e, i, s):
                        return e.dma_start(out=w2t[:, s * C:(s + 1) * C],
                                           in_=w2bf_d[i * 128:(i + 1) * 128, :])

                    def mm2(t, i, s, dcg=dcg):
                        for jj in range(4):
                            dc = dcg * 4 + jj
                            ins = t.matmul(pcs[jj][:, 0:TT],
                                           w2t[:, s * C + dc * 128: s * C + (dc + 1) * 128],
                                           ch(hh_bf, i, TT),
                                           start=(i == 0), stop=(i == 31),
                                           skip_group_check=True)
                        return ins

                    pp.pipe2(32, mm2, ld=ld2)

                    def ev2(v, inc, dcg=dcg):
                        for jj in range(4):
                            dc = dcg * 4 + jj
                            ins = v.tensor_scalar(out=ch(ffn_f, dc, TT), in0=pcs[jj][:, 0:TT],
                                                  scalar1=b2_sb[:, dc:dc + 1],
                                                  scalar2=None, op0=ALU.add)
                        inc(ins)
                    pp.phase(vector=ev2)

                emit_ln(pp, kit, consts, ffn_f, pa, pb, out_f32=ffn_f)

                def addres(v, inc, tt=tt):
                    for c in range(8):
                        ins = v.tensor_tensor(out=sl_t(c, tt), in0=sl_t(c, tt),
                                              in1=ch(ffn_f, c, TT), op=ALU.add)
                    inc(ins)
                pp.phase(vector=addres)

                def store(e, inc, tt=tt):
                    for c in range(8):
                        inc(e.dma_start(out=y_d[c * 128:(c + 1) * 128, tt * TT:(tt + 1) * TT],
                                        in_=sl_t(c, tt)), 16)
                pp.phase(gpsimd=store)

        pp.finish()
    return nc



def _asc(a):
    return np.ascontiguousarray(a)


def _blockdiag(W):
    blocks = W.reshape(16, 32, 4, 4)
    Bd = np.zeros((16, 128, 128), np.float32)
    BdT = np.zeros((16, 128, 128), np.float32)
    for a in range(32):
        Bd[:, 4 * a:4 * a + 4, 4 * a:4 * a + 4] = blocks[:, a].transpose(0, 2, 1)
        BdT[:, 4 * a:4 * a + 4, 4 * a:4 * a + 4] = blocks[:, a]
    return Bd.reshape(16 * 128, 128), BdT.reshape(16 * 128, 128)


def _cols(vec, nch):
    return _asc(np.asarray(vec, np.float32).reshape(nch, 128).T)


_CACHE = {}


def _get(name, builder):
    if name not in _CACHE:
        _CACHE[name] = builder()
    return _CACHE[name]


def kernel_unfused(**inputs):
    f32 = np.float32
    x0 = np.asarray(inputs["L0_attn_x"], f32)
    cores = [(b, h) for b in range(B) for h in range(2)]

    wiT = _asc(np.asarray(inputs["L0_attn_Wi"], f32).T)
    wrT = _asc(np.asarray(inputs["L0_attn_Wr"], f32).T)
    bi = _cols(inputs["L0_attn_bi"], 8)
    br = _cols(inputs["L0_attn_br"], 8)
    lam = np.asarray(inputs["L0_attn_cap_lambda"], f32).reshape(1, 1)
    im1 = []
    for (b, h) in cores:
        xs = _asc(x0[b, h * TC:(h + 1) * TC, :].T)
        im1.append({"xs": xs, "wiT": wiT, "wrT": wrT, "bi": bi, "br": br, "lam": lam})
    r1 = run_bass_kernel_spmd(_get("l1", build_l1_gates), im1,
                              core_ids=list(range(NCORES))).results

    at_fm = [np.concatenate([r1[2 * b]["at_o"], r1[2 * b + 1]["at_o"]], axis=1)
             for b in range(B)]
    gx_fm = [np.concatenate([r1[2 * b]["gx_o"], r1[2 * b + 1]["gx_o"]], axis=1)
             for b in range(B)]

    im2 = []
    for b in range(B):
        for f in range(2):
            im2.append({"at_i": _asc(at_fm[b][f * 512:(f + 1) * 512]),
                        "gx_i": _asc(gx_fm[b][f * 512:(f + 1) * 512])})
    r2 = run_bass_kernel_spmd(_get("l2", build_l2_scan), im2,
                              core_ids=list(range(NCORES))).results
    ys_fm = [np.concatenate([r2[2 * b]["ys_o"], r2[2 * b + 1]["ys_o"]], axis=0)
             for b in range(B)]

    th = _asc(np.asarray(inputs["L0_ffn_theta"], f32))
    thn = _asc(np.asarray(inputs["L0_ffn_theta_noise"], f32))
    phm = _asc(np.asarray(inputs["L0_ffn_phi"], f32))
    phn = _asc(np.asarray(inputs["L0_ffn_phi_noise"], f32))
    im3 = []
    for (b, h) in cores:
        im3.append({"xs": _asc(x0[b, h * TC:(h + 1) * TC, :].T),
                    "ys": _asc(ys_fm[b][:, h * TC:(h + 1) * TC]),
                    "th": th, "thn": thn, "ph": phm, "phn": phn})
    r3 = run_bass_kernel_spmd(_get("l3", build_l3_ffn0), im3,
                              core_ids=list(range(NCORES))).results
    x2 = [r3[i]["x2_o"] for i in range(NCORES)]

    wup = _asc(np.asarray(inputs["L1_attn_W_up"], f32))
    im4a = []
    for i, (b, h) in enumerate(cores):
        x2h = np.zeros((C, TH), f32)
        x2h[:, 512:] = x2[i]
        if h == 1:
            x2h[:, 509:512] = x2[2 * b][:, TC - 3:]
        im4a.append({"x2h": x2h, "wup": wup})
    r4a = run_bass_kernel_spmd(_get("l4a", build_l4a_up), im4a,
                               core_ids=list(range(NCORES))).results

    bq, bqT = _blockdiag(np.asarray(inputs["L1_attn_Wq"], f32))
    bk, bkT = _blockdiag(np.asarray(inputs["L1_attn_Wk"], f32))
    bv, bvT = _blockdiag(np.asarray(inputs["L1_attn_Wv"], f32))
    W_ig = np.asarray(inputs["L1_attn_W_ig"], f32)
    W_fg = np.asarray(inputs["L1_attn_W_fg"], f32)
    wig_q = _asc(np.concatenate([W_ig[0:2048], W_fg[0:2048]], axis=1))
    wig_k = _asc(np.concatenate([W_ig[2048:4096], W_fg[2048:4096]], axis=1))
    wig_v = _asc(np.concatenate([W_ig[4096:6144], W_fg[4096:6144]], axis=1))
    bg = _asc(np.concatenate([np.asarray(inputs["L1_attn_b_ig"], f32),
                              np.asarray(inputs["L1_attn_b_fg"], f32)]).reshape(8, 1))
    cw = _asc(np.asarray(inputs["L1_attn_conv_w"], f32)
              .reshape(4, 16, 128).transpose(2, 1, 0).reshape(128, 64))
    cb = _cols(inputs["L1_attn_conv_b"], 16)
    im4b = []
    for i in range(NCORES):
        im4b.append({"xml": r4a[i]["xml_o"], "bq": bq, "bk": bk, "bv": bv,
                     "bqT": bqT, "bkT": bkT, "bvT": bvT,
                     "wig_q": wig_q, "wig_k": wig_k, "wig_v": wig_v,
                     "bg": bg, "cw": cw, "cb": cb})
    r4b = run_bass_kernel_spmd(_get("l4b", build_l4b_qkv), im4b,
                               core_ids=list(range(NCORES))).results

    norm_w = np.asarray(inputs["L1_attn_norm_w"], f32)
    ident = np.eye(128, dtype=f32)
    mbig = (np.arange(512)[None, :] >= (np.arange(128)[:, None] + 256)).astype(BF)
    im5 = []
    for i in range(NCORES):
        b, npair = i // 2, i % 2
        m = {"ident_in": ident, "mbig_in": mbig}
        for u in range(2):
            n = 2 * npair + u
            c0, c1 = 2 * b, 2 * b + 1
            m[f"qf{u}"] = _asc(np.concatenate(
                [r4b[c0]["q_o"][512 * n:512 * (n + 1)],
                 r4b[c1]["q_o"][512 * n:512 * (n + 1)]], axis=1))
            m[f"kf{u}"] = _asc(np.concatenate(
                [r4b[c0]["k_o"][512 * n:512 * (n + 1)],
                 r4b[c1]["k_o"][512 * n:512 * (n + 1)]], axis=1))
            vfull = np.concatenate([r4b[c0]["v_o"], r4b[c1]["v_o"]], axis=0)
            m[f"v1{u}"] = _asc(vfull[:, 512 * n:512 * (n + 1)])
            m[f"ig{u}"] = _asc(np.concatenate(
                [r4b[c0]["ig_o"][n:n + 1], r4b[c1]["ig_o"][n:n + 1]], axis=1))
            m[f"fg{u}"] = _asc(np.concatenate(
                [r4b[c0]["fg_o"][n:n + 1], r4b[c1]["fg_o"][n:n + 1]], axis=1))
            m[f"nw{u}"] = _asc(np.tile(norm_w[None, 512 * n:512 * (n + 1)], (128, 1)))
        im5.append(m)
    r5 = run_bass_kernel_spmd(_get("l5", build_l5_attn), im5,
                              core_ids=list(range(NCORES))).results

    skip_sb = _cols(inputs["L1_attn_skip"], 16)
    wdn = _asc(np.asarray(inputs["L1_attn_W_down"], f32))
    im6a = []
    for i, (b, h) in enumerate(cores):
        hn_parts = []
        for n in range(4):
            core5 = 2 * b + n // 2
            u = n % 2
            hn_parts.append(r5[core5][f"hn{u}"][h * TC:(h + 1) * TC, :])
        hn_tm = np.concatenate(hn_parts, axis=1)
        im6a.append({"x2": x2[i], "xact": r4b[i]["xact_o"], "z": r4a[i]["z_o"],
                     "hn": _asc(hn_tm.T), "skip": skip_sb, "wdn": wdn})
    r6a = run_bass_kernel_spmd(_get("l6a", build_l6a_post), im6a,
                               core_ids=list(range(NCORES))).results

    w1 = _asc(np.asarray(inputs["L1_ffn_w1"], f32))
    b1c = _cols(inputs["L1_ffn_b1"], 32)
    w2 = _asc(np.asarray(inputs["L1_ffn_w2"], f32))
    b2c = _cols(inputs["L1_ffn_b2"], 8)
    im6b = [{"x3": r6a[i]["x3_o"], "w1": w1, "b1": b1c, "w2": w2, "b2": b2c}
            for i in range(NCORES)]
    r6b = run_bass_kernel_spmd(_get("l6b", build_l6b_ffn1), im6b,
                               core_ids=list(range(NCORES))).results

    out = np.empty((B, TF, C), f32)
    for i, (b, h) in enumerate(cores):
        out[b, h * TC:(h + 1) * TC, :] = r6b[i]["x4_o"].T
    return out


def kernel(**inputs):
    f32 = np.float32
    x0 = np.asarray(inputs["L0_attn_x"], f32)
    cores = [(b, h) for b in range(B) for h in range(2)]

    wiT = _asc(np.asarray(inputs["L0_attn_Wi"], f32).T)
    wrT = _asc(np.asarray(inputs["L0_attn_Wr"], f32).T)
    bi = _cols(inputs["L0_attn_bi"], 8)
    br = _cols(inputs["L0_attn_br"], 8)
    lam = np.asarray(inputs["L0_attn_cap_lambda"], f32).reshape(1, 1)
    th = _asc(np.asarray(inputs["L0_ffn_theta"], f32))
    thn = _asc(np.asarray(inputs["L0_ffn_theta_noise"], f32))
    phm = _asc(np.asarray(inputs["L0_ffn_phi"], f32))
    phn = _asc(np.asarray(inputs["L0_ffn_phi_noise"], f32))
    wup = _asc(np.asarray(inputs["L1_attn_W_up"], f32))
    bq, bqT = _blockdiag(np.asarray(inputs["L1_attn_Wq"], f32))
    bk, bkT = _blockdiag(np.asarray(inputs["L1_attn_Wk"], f32))
    bv, bvT = _blockdiag(np.asarray(inputs["L1_attn_Wv"], f32))
    W_ig = np.asarray(inputs["L1_attn_W_ig"], f32)
    W_fg = np.asarray(inputs["L1_attn_W_fg"], f32)
    wig_q = _asc(np.concatenate([W_ig[0:2048], W_fg[0:2048]], axis=1))
    wig_k = _asc(np.concatenate([W_ig[2048:4096], W_fg[2048:4096]], axis=1))
    wig_v = _asc(np.concatenate([W_ig[4096:6144], W_fg[4096:6144]], axis=1))
    bg = _asc(np.concatenate([np.asarray(inputs["L1_attn_b_ig"], f32),
                              np.asarray(inputs["L1_attn_b_fg"], f32)]).reshape(8, 1))
    cw = _asc(np.asarray(inputs["L1_attn_conv_w"], f32)
              .reshape(4, 16, 128).transpose(2, 1, 0).reshape(128, 64))
    cb = _cols(inputs["L1_attn_conv_b"], 16)
    nw = _cols(inputs["L1_attn_norm_w"], 16)
    skip = _cols(inputs["L1_attn_skip"], 16)
    wdn = _asc(np.asarray(inputs["L1_attn_W_down"], f32))
    w1 = _asc(np.asarray(inputs["L1_ffn_w1"], f32))
    b1c = _cols(inputs["L1_ffn_b1"], 32)
    w2 = _asc(np.asarray(inputs["L1_ffn_w2"], f32))
    b2c = _cols(inputs["L1_ffn_b2"], 8)
    ident = np.eye(128, dtype=f32)
    mwin = (np.arange(2 * QBF)[None, :] >= (np.arange(128)[:, None] + QBF)).astype(BF)
    ones_w = np.ones((128, 2 * QBF), BF)
    pois_h0 = np.zeros((4, TF), f32); pois_h0[:, TC:] = -1.0e30
    pois_h1 = np.zeros((4, TF), f32)

    im = []
    for (b, h) in cores:
        m = dict(xs=_asc(x0[b, h * TC:(h + 1) * TC, :].T), wiT=wiT, wrT=wrT,
                 bi=bi, br=br, lam=lam, th=th, thn=thn, ph=phm, phn=phn, wup=wup,
                 bq=bq, bk=bk, bv=bv, bqT=bqT, bkT=bkT, bvT=bvT,
                 wig_q=wig_q, wig_k=wig_k, wig_v=wig_v, bg=bg, cw=cw, cb=cb,
                 nw=nw, skip=skip, wdn=wdn, w1=w1, b1=b1c, w2=w2, b2=b2c,
                 ident=ident)
        if h == 0:
            m["poison"] = pois_h0
            m["mbigA"] = mwin
            m["mbigB"] = ones_w
            m["cmask"] = np.zeros((128, 1), f32)
            m["hm4"] = np.zeros((4, 1), f32)
            m["hm4c"] = np.ones((4, 1), f32)
        else:
            m["poison"] = pois_h1
            m["mbigA"] = ones_w
            m["mbigB"] = mwin
            m["cmask"] = np.ones((128, 1), f32)
            m["hm4"] = np.ones((4, 1), f32)
            m["hm4c"] = np.zeros((4, 1), f32)
        im.append(m)

    res = run_bass_kernel_spmd(_get("fused", build_fused), im,
                               core_ids=list(range(NCORES))).results
    out = np.empty((B, TF, C), f32)
    for i, (b, h) in enumerate(cores):
        out[b, h * TC:(h + 1) * TC, :] = res[i]["y"].T
    return out


# version 08 (final)
# speedup vs baseline: 1.6536x; 1.0900x over previous
import contextlib
import numpy as np
import ml_dtypes
import concourse.bass as bass
import concourse.mybir as mybir
from concourse.bass_utils import run_bass_kernel_spmd

F32 = mybir.dt.float32
BF16 = mybir.dt.bfloat16
AF = mybir.ActivationFunctionType
ALU = mybir.AluOpType
AX = mybir.AxisListType.X
BF = ml_dtypes.bfloat16

NCORES = 8
B, TF, C = 4, 4096, 1024
TC = 2048
TT = 512
NT = TC // TT
TH = 2560
NTH = TH // TT
ENGS = ("sync", "tensor", "vector", "scalar", "gpsimd")
HALF_PI = float(np.pi / 2)
CSC_SHIFT = float(0.5 * np.log(512.0))
NEG_BIG = -3.0e38


class PB:

    def __init__(self, nc, es):
        self.nc = nc
        self.block = es.enter_context(nc.Block())
        self.sems = {e: es.enter_context(nc.semaphore("pb_" + e)) for e in ENGS}
        self.cnt = {e: 0 for e in ENGS}

    def phase(self, **fns):
        pre = dict(self.cnt)
        for ename, fn in fns.items():
            sem = self.sems[ename]
            added = [0]

            def body(eng, fn=fn, ename=ename, sem=sem, added=added, pre=pre):
                for o in ENGS:
                    if o != ename and pre[o] > 0:
                        eng.wait_ge(self.sems[o], pre[o])

                def inc(inst, n=1, wait=False):
                    inst.then_inc(sem, n)
                    added[0] += n
                    if wait:
                        eng.wait_ge(sem, pre[ename] + added[0])

                fn(eng, inc)

            getattr(self.block, ename)(body)
            assert added[0] > 0, f"phase body on {ename} incremented nothing"
            self.cnt[ename] += added[0]

    def init_hand_sems(self, es):
        self.h = {nm: es.enter_context(self.nc.semaphore("h_" + nm))
                  for nm in ("ld", "mm", "ev", "st", "pv", "chn",
                             "cc1", "cc2", "cc3")}
        self.hc = {nm: 0 for nm in self.h}

    def chain(self, eng, inst):
        self.hc["chn"] += 1
        inst.then_inc(self.h["chn"], 1)
        eng.wait_ge(self.h["chn"], self.hc["chn"])
        return inst

    def pipe2(self, n, mm, ev=None, ld=None, st=None, mm_eng="tensor",
              ev_eng="vector", ld_units=1):
        pre = dict(self.cnt)
        h, hc = self.h, self.hc
        ld0, mm0, ev0, st0 = hc["ld"], hc["mm"], hc["ev"], hc["st"]

        def waits(eng, ename):
            for o in ENGS:
                if o != ename and pre[o] > 0:
                    eng.wait_ge(self.sems[o], pre[o])

        if ld is not None:
            def body_ld(eng):
                waits(eng, "sync")
                for i in range(n):
                    if i >= 2:
                        eng.wait_ge(h["mm"], mm0 + i - 1)
                    ld(eng, i, i % 2).then_inc(h["ld"], 16)
                eng.nop().then_inc(self.sems["sync"], 1)
            self.block.sync(body_ld)
            hc["ld"] += 16 * ld_units * n
            self.cnt["sync"] += 1

        def body_mm(eng):
            waits(eng, mm_eng)
            for i in range(n):
                if ld is not None:
                    eng.wait_ge(h["ld"], ld0 + 16 * ld_units * (i + 1))
                if ev is not None and i >= 2:
                    eng.wait_ge(h["ev"], ev0 + i - 1)
                if ev is None and st is not None and i >= 2:
                    eng.wait_ge(h["st"], st0 + 16 * (i - 1))
                mm(eng, i, i % 2).then_inc(h["mm"], 1)
        getattr(self.block, mm_eng)(body_mm)
        hc["mm"] += n
        self.cnt[mm_eng] += 0

        if ev is not None:
            def body_ev(eng):
                waits(eng, ev_eng)
                for i in range(n):
                    eng.wait_ge(h["mm"], mm0 + i + 1)
                    if st is not None and i >= 2:
                        eng.wait_ge(h["st"], st0 + 16 * (i - 1))
                    ev(eng, i, i % 2).then_inc(h["ev"], 1)
            getattr(self.block, ev_eng)(body_ev)
            hc["ev"] += n

        if st is not None:
            def body_st(eng):
                waits(eng, "gpsimd")
                for i in range(n):
                    if ev is not None:
                        eng.wait_ge(h["ev"], ev0 + i + 1)
                    else:
                        eng.wait_ge(h["mm"], mm0 + i + 1)
                    st(eng, i, i % 2).then_inc(h["st"], 16)
            self.block.gpsimd(body_st)
            hc["st"] += 16 * n

        def mark(ename, sem_val):
            def body(eng):
                eng.wait_ge(self.h[sem_val[0]], sem_val[1])
                eng.nop().then_inc(self.sems[ename], 1)
            getattr(self.block, ename)(body)
            self.cnt[ename] += 1

        mark(mm_eng, ("mm", hc["mm"]))
        if ev is not None:
            mark(ev_eng, ("ev", hc["ev"]))
        if st is not None:
            mark("gpsimd", ("st", hc["st"]))
        if ld is not None:
            pass

    def finish(self):
        pre = dict(self.cnt)

        def body(eng):
            for o in ENGS:
                if pre[o] > 0:
                    eng.wait_ge(self.sems[o], pre[o])

        self.block.sync(body)


def ch(t, c, W):
    return t[:, c * W:(c + 1) * W]


def load_fm(eng, inc, sbuf, dram, nch, W, t0):
    for c in range(nch):
        inc(eng.dma_start(out=ch(sbuf, c, W), in_=dram[c * 128:(c + 1) * 128, t0:t0 + W]), 16)


def store_fm(eng, inc, dram, sbuf, nch, W, t0):
    for c in range(nch):
        inc(eng.dma_start(out=dram[c * 128:(c + 1) * 128, t0:t0 + W], in_=ch(sbuf, c, W)), 16)


class LNKit:

    def __init__(self, nc, es, tag, nch, W):
        self.nch, self.W = nch, W
        self.xbf = es.enter_context(nc.sbuf_tensor(f"{tag}_xbf", [128, nch * W], BF16))
        self.d = es.enter_context(nc.sbuf_tensor(f"{tag}_d", [128, nch * W], F32))
        self.rstd_a = es.enter_context(nc.sbuf_tensor(f"{tag}_ra", [128, W], F32))
        self.rstd = es.enter_context(nc.sbuf_tensor(f"{tag}_rb", [128, W], F32))


def emit_ln(pp, kit, consts, src, pa, pb, out_bf=None, out_f32=None, eps="eps5"):
    nch, W = kit.nch, kit.W
    Cn = nch * 128
    ones_bf = consts["ones_bf"]

    def cast(v, inc):
        for c in range(nch):
            ins = v.tensor_copy(out=ch(kit.xbf, c, W), in_=ch(src, c, W))
        inc(ins)
    pp.phase(vector=cast)

    def mm_mu(t, inc):
        for c in range(nch):
            ins = t.matmul(pa[:, 0:W], ones_bf, ch(kit.xbf, c, W),
                           start=(c == 0), stop=(c == nch - 1))
        inc(ins)
    pp.phase(tensor=mm_mu)

    def sub(v, inc):
        for c in range(nch):
            ins = v.scalar_tensor_tensor(out=ch(kit.d, c, W), in0=pa[:, 0:W],
                                         scalar=-1.0 / Cn, in1=ch(src, c, W),
                                         op0=ALU.mult, op1=ALU.add)
        inc(ins)
    pp.phase(vector=sub)

    def sq(s, inc):
        for c in range(nch):
            ins = s.activation(out=ch(kit.xbf, c, W), in_=ch(kit.d, c, W), func=AF.Square)
        inc(ins)
    pp.phase(scalar=sq)

    def mm_var(t, inc):
        for c in range(nch):
            ins = t.matmul(pb[:, 0:W], ones_bf, ch(kit.xbf, c, W),
                           start=(c == 0), stop=(c == nch - 1))
        inc(ins)
    pp.phase(tensor=mm_var)

    def rstd(s, inc):
        s.activation(out=kit.rstd_a[:], in_=pb[:, 0:W], func=AF.Ln, scale=1.0 / Cn,
                     bias=consts[eps])
        ins = s.activation(out=kit.rstd[:], in_=kit.rstd_a[:], func=AF.Exp, scale=-0.5)
        inc(ins)
    pp.phase(scalar=rstd)

    if out_bf is not None or out_f32 is not None:
        def apply(v, inc):
            for c in range(nch):
                if out_bf is not None:
                    ins = v.tensor_tensor(out=ch(out_bf, c, W), in0=ch(kit.d, c, W),
                                          in1=kit.rstd[:], op=ALU.mult)
                if out_f32 is not None:
                    ins = v.tensor_tensor(out=ch(out_f32, c, W), in0=ch(kit.d, c, W),
                                          in1=kit.rstd[:], op=ALU.mult)
            inc(ins)
        pp.phase(vector=apply)


def make_consts(nc, es, pp, specs):
    consts = {}
    vals = {}
    for name, v in specs.items():
        t = es.enter_context(nc.sbuf_tensor("c_" + name, [128, 1], F32))
        consts[name] = t[:]
        vals[name] = v
    ones_bf = es.enter_context(nc.sbuf_tensor("c_onesbf", [128, 128], BF16))
    ones_row = es.enter_context(nc.sbuf_tensor("c_onesrow", [1, 128], F32))
    consts["ones_bf"] = ones_bf[:]
    consts["ones_row"] = ones_row[:]

    def setc(v, inc):
        for name, val in vals.items():
            v.memset(consts[name], float(val))
        v.memset(ones_bf[:], 1.0)
        ins = v.memset(ones_row[:], 1.0)
        inc(ins)
    pp.phase(vector=setc)
    return consts


def cast_weight(pp, dram, rows, cols, stage, out_bf):
    nk = rows // 128

    def load(e, inc):
        for kc in range(nk):
            inc(e.dma_start(out=stage[:, kc * cols:(kc + 1) * cols],
                            in_=dram[kc * 128:(kc + 1) * 128, :]), 16)
    pp.phase(sync=load)

    def cast(v, inc):
        ins = v.tensor_copy(out=out_bf[:, 0:nk * cols], in_=stage[:, 0:nk * cols])
        inc(ins)
    pp.phase(vector=cast)



def build_l1_gates():
    nc = bass.Bass("TRN2", target_bir_lowering=False)
    xs = nc.dram_tensor("xs", [C, TC], F32, kind="ExternalInput").ap()
    wiT = nc.dram_tensor("wiT", [C, C], F32, kind="ExternalInput").ap()
    wrT = nc.dram_tensor("wrT", [C, C], F32, kind="ExternalInput").ap()
    bi = nc.dram_tensor("bi", [128, 8], F32, kind="ExternalInput").ap()
    br = nc.dram_tensor("br", [128, 8], F32, kind="ExternalInput").ap()
    lam = nc.dram_tensor("lam", [1, 1], F32, kind="ExternalInput").ap()
    at_o = nc.dram_tensor("at_o", [C, TC], F32, kind="ExternalOutput").ap()
    gx_o = nc.dram_tensor("gx_o", [C, TC], F32, kind="ExternalOutput").ap()

    with contextlib.ExitStack() as es:
        pp = PB(nc, es)
        stage = es.enter_context(nc.sbuf_tensor("stage", [128, 8 * C], F32))
        wi_bf = es.enter_context(nc.sbuf_tensor("wi_bf", [128, 8 * C], BF16))
        wr_bf = es.enter_context(nc.sbuf_tensor("wr_bf", [128, 8 * C], BF16))
        bi_sb = es.enter_context(nc.sbuf_tensor("bi_sb", [128, 8], F32))
        br_sb = es.enter_context(nc.sbuf_tensor("br_sb", [128, 8], F32))
        lam_sb = es.enter_context(nc.sbuf_tensor("lam_sb", [1, 1], F32))
        sp_t = es.enter_context(nc.sbuf_tensor("sp_t", [1, 1], F32))
        sp_t2 = es.enter_context(nc.sbuf_tensor("sp_t2", [1, 1], F32))
        s8_bc = es.enter_context(nc.sbuf_tensor("s8_bc", [128, 1], F32))
        x_t = es.enter_context(nc.sbuf_tensor("x_t", [128, 8 * TT], F32))
        xin_bf = es.enter_context(nc.sbuf_tensor("xin_bf", [128, 8 * TT], BF16))
        it_f = es.enter_context(nc.sbuf_tensor("it_f", [128, 8 * TT], F32))
        at_f = es.enter_context(nc.sbuf_tensor("at_f", [128, 8 * TT], F32))
        gx_f = es.enter_context(nc.sbuf_tensor("gx_f", [128, 8 * TT], F32))
        t1 = es.enter_context(nc.sbuf_tensor("t1", [128, TT], F32))
        t2 = es.enter_context(nc.sbuf_tensor("t2", [128, TT], F32))
        kit = LNKit(nc, es, "ln", 8, TT)
        pa = es.enter_context(nc.psum_tensor("pa", [128, TT], F32))
        pb = es.enter_context(nc.psum_tensor("pb", [128, TT], F32))
        pd = es.enter_context(nc.psum_tensor("pd", [128, 1], F32))

        consts = make_consts(nc, es, pp, {"eps5": 1e-5, "one": 1.0})

        cast_weight(pp, wiT, C, C, stage, wi_bf)
        cast_weight(pp, wrT, C, C, stage, wr_bf)

        def load_small(e, inc):
            inc(e.dma_start(out=bi_sb[:], in_=bi), 16)
            inc(e.dma_start(out=br_sb[:], in_=br), 16)
            inc(e.dma_start(out=lam_sb[:], in_=lam), 16)
        pp.phase(sync=load_small)

        def sp1(s, inc):
            inc(s.activation(out=sp_t[:], in_=lam_sb[:], func=AF.Abs), wait=True)
            inc(s.activation(out=sp_t[:], in_=sp_t[:], func=AF.Exp, scale=-1.0), wait=True)
            inc(s.activation(out=sp_t[:], in_=sp_t[:], func=AF.Ln,
                             bias=consts["one"][0:1, :]), wait=True)
            inc(s.activation(out=sp_t2[:], in_=lam_sb[:], func=AF.Relu))
        pp.phase(scalar=sp1)

        def sp2(v, inc):
            inc(v.tensor_tensor(out=sp_t2[:], in0=sp_t2[:], in1=sp_t[:], op=ALU.add),
                wait=True)
            inc(v.tensor_scalar(out=sp_t2[:], in0=sp_t2[:], scalar1=-8.0,
                                scalar2=None, op0=ALU.mult))
        pp.phase(vector=sp2)

        def sp3(t, inc):
            ins = t.matmul(pd[:, 0:1], consts["ones_row"], sp_t2[:], start=True, stop=True)
            inc(ins)
        pp.phase(tensor=sp3)

        def sp4(v, inc):
            ins = v.tensor_copy(out=s8_bc[:], in_=pd[:, 0:1])
            inc(ins)
        pp.phase(vector=sp4)

        for tt in range(NT):
            t0 = tt * TT

            def load_x(e, inc, t0=t0):
                load_fm(e, inc, x_t, xs, 8, TT, t0)
            pp.phase(sync=load_x)

            emit_ln(pp, kit, consts, x_t, pa, pb, out_bf=xin_bf)

            for which in ("i", "r"):
                w_bf = wi_bf if which == "i" else wr_bf
                b_sb = bi_sb if which == "i" else br_sb
                for dc in range(8):
                    ps = pa if dc % 2 == 0 else pb

                    def mm(t, inc, dc=dc, ps=ps, w_bf=w_bf):
                        for kc in range(8):
                            ins = t.matmul(
                                ps[:, 0:TT],
                                w_bf[:, kc * C + dc * 128: kc * C + (dc + 1) * 128],
                                ch(xin_bf, kc, TT), start=(kc == 0), stop=(kc == 7))
                        inc(ins)
                    pp.phase(tensor=mm)

                    if which == "i":
                        def ev(s, inc, dc=dc, ps=ps, b_sb=b_sb):
                            ins = s.activation(out=ch(it_f, dc, TT), in_=ps[:, 0:TT],
                                               func=AF.Sigmoid, bias=b_sb[:, dc:dc + 1])
                            inc(ins)
                        pp.phase(scalar=ev)
                    else:
                        def ev(s, inc, dc=dc, ps=ps, b_sb=b_sb):
                            s.activation(out=t1[:], in_=ps[:, 0:TT],
                                         func=AF.Sigmoid, bias=b_sb[:, dc:dc + 1])
                            ins = s.activation(out=ch(at_f, dc, TT), in_=t1[:],
                                               func=AF.Exp, scale=s8_bc[:])
                            inc(ins)
                        pp.phase(scalar=ev)

            for dc in range(8):
                def g1(s, inc, dc=dc):
                    s.activation(out=t1[:], in_=ch(at_f, dc, TT), func=AF.Square)
                    s.activation(out=t1[:], in_=t1[:], func=AF.Ln, scale=-1.0,
                                 bias=consts["one"])
                    ins = s.activation(out=t2[:], in_=t1[:], func=AF.Exp, scale=0.5)
                    inc(ins)
                pp.phase(scalar=g1)

                def g2(v, inc, dc=dc):
                    v.tensor_tensor(out=ch(gx_f, dc, TT), in0=ch(it_f, dc, TT),
                                    in1=ch(xin_bf, dc, TT), op=ALU.mult)
                    ins = v.tensor_tensor(out=ch(gx_f, dc, TT), in0=ch(gx_f, dc, TT),
                                          in1=t2[:], op=ALU.mult)
                    inc(ins)
                pp.phase(vector=g2)

            def store(e, inc, t0=t0):
                store_fm(e, inc, at_o, at_f, 8, TT, t0)
                store_fm(e, inc, gx_o, gx_f, 8, TT, t0)
            pp.phase(gpsimd=store)

        pp.finish()
    return nc



def build_l2_scan():
    nc = bass.Bass("TRN2", target_bir_lowering=False)
    at_i = nc.dram_tensor("at_i", [C // 2, TF], F32, kind="ExternalInput").ap()
    gx_i = nc.dram_tensor("gx_i", [C // 2, TF], F32, kind="ExternalInput").ap()
    ys_o = nc.dram_tensor("ys_o", [C // 2, TF], F32, kind="ExternalOutput").ap()
    with contextlib.ExitStack() as es:
        pp = PB(nc, es)
        a_t = es.enter_context(nc.sbuf_tensor("a_t", [128, TF], F32))
        g_t = es.enter_context(nc.sbuf_tensor("g_t", [128, TF], F32))
        y_t = es.enter_context(nc.sbuf_tensor("y_t", [128, TF], F32))
        for c in range(4):
            def load(e, inc, c=c):
                inc(e.dma_start(out=a_t[:], in_=at_i[c * 128:(c + 1) * 128, :]), 16)
                inc(e.dma_start(out=g_t[:], in_=gx_i[c * 128:(c + 1) * 128, :]), 16)
            pp.phase(sync=load)

            def scan(v, inc):
                ins = v.tensor_tensor_scan(out=y_t[:], data0=a_t[:], data1=g_t[:],
                                           initial=0.0, op0=ALU.mult, op1=ALU.add)
                inc(ins)
            pp.phase(vector=scan)

            def store(e, inc, c=c):
                inc(e.dma_start(out=ys_o[c * 128:(c + 1) * 128, :], in_=y_t[:]), 16)
            pp.phase(gpsimd=store)
        pp.finish()
    return nc



def build_l3_ffn0():
    nc = bass.Bass("TRN2", target_bir_lowering=False)
    xs = nc.dram_tensor("xs", [C, TC], F32, kind="ExternalInput").ap()
    ys = nc.dram_tensor("ys", [C, TC], F32, kind="ExternalInput").ap()
    th = nc.dram_tensor("th", [C, C], F32, kind="ExternalInput").ap()
    thn = nc.dram_tensor("thn", [C, C], F32, kind="ExternalInput").ap()
    ph = nc.dram_tensor("ph", [C, C], F32, kind="ExternalInput").ap()
    phn = nc.dram_tensor("phn", [C, C], F32, kind="ExternalInput").ap()
    x2_o = nc.dram_tensor("x2_o", [C, TC], F32, kind="ExternalOutput").ap()
    with contextlib.ExitStack() as es:
        pp = PB(nc, es)
        stage_a = es.enter_context(nc.sbuf_tensor("stage_a", [128, 8 * C], F32))
        stage_b = es.enter_context(nc.sbuf_tensor("stage_b", [128, 8 * C], F32))
        cos_bf = es.enter_context(nc.sbuf_tensor("cos_bf", [128, 8 * C], BF16))
        sin_bf = es.enter_context(nc.sbuf_tensor("sin_bf", [128, 8 * C], BF16))
        x1_t = es.enter_context(nc.sbuf_tensor("x1_t", [128, 8 * TT], F32))
        ys_t = es.enter_context(nc.sbuf_tensor("ys_t", [128, 8 * TT], F32))
        fin_bf = es.enter_context(nc.sbuf_tensor("fin_bf", [128, 8 * TT], BF16))
        r_f = es.enter_context(nc.sbuf_tensor("r_f", [128, 8 * TT], F32))
        sr = es.enter_context(nc.sbuf_tensor("sr", [128, TT], F32))
        si = es.enter_context(nc.sbuf_tensor("si", [128, TT], F32))
        kit = LNKit(nc, es, "ln", 8, TT)
        pa = es.enter_context(nc.psum_tensor("pa", [128, TT], F32))
        pb = es.enter_context(nc.psum_tensor("pb", [128, TT], F32))
        consts = make_consts(nc, es, pp, {"eps5": 1e-5, "one": 1.0, "halfpi": HALF_PI})

        for (wa, wb, out_bf, bias) in ((th, thn, cos_bf, "halfpi"),
                                       (ph, phn, sin_bf, None)):
            def load(e, inc, wa=wa, wb=wb):
                for kc in range(8):
                    inc(e.dma_start(out=stage_a[:, kc * C:(kc + 1) * C],
                                    in_=wa[kc * 128:(kc + 1) * 128, :]), 16)
                    inc(e.dma_start(out=stage_b[:, kc * C:(kc + 1) * C],
                                    in_=wb[kc * 128:(kc + 1) * 128, :]), 16)
            pp.phase(sync=load)

            def add(v, inc):
                ins = v.tensor_tensor(out=stage_a[:], in0=stage_a[:], in1=stage_b[:],
                                      op=ALU.add)
                inc(ins)
            pp.phase(vector=add)

            def trig(s, inc, out_bf=out_bf, bias=bias):
                if bias is None:
                    ins = s.activation(out=out_bf[:], in_=stage_a[:], func=AF.Sin)
                else:
                    ins = s.activation(out=out_bf[:], in_=stage_a[:], func=AF.Sin,
                                       bias=consts[bias])
                inc(ins)
            pp.phase(scalar=trig)

        for tt in range(NT):
            t0 = tt * TT

            def load_x(e, inc, t0=t0):
                load_fm(e, inc, x1_t, xs, 8, TT, t0)
                load_fm(e, inc, ys_t, ys, 8, TT, t0)
            pp.phase(sync=load_x)

            emit_ln(pp, kit, consts, ys_t, pa, pb, out_f32=ys_t)

            def addres(v, inc):
                for c in range(8):
                    ins = v.tensor_tensor(out=ch(x1_t, c, TT), in0=ch(x1_t, c, TT),
                                          in1=ch(ys_t, c, TT), op=ALU.add)
                inc(ins)
            pp.phase(vector=addres)

            emit_ln(pp, kit, consts, x1_t, pa, pb, out_bf=fin_bf)

            for dc in range(8):
                def mm(t, inc, dc=dc):
                    for kc in range(8):
                        t.matmul(pa[:, 0:TT],
                                 cos_bf[:, kc * C + dc * 128: kc * C + (dc + 1) * 128],
                                 ch(fin_bf, kc, TT), start=(kc == 0), stop=(kc == 7))
                    for kc in range(8):
                        ins = t.matmul(
                            pb[:, 0:TT],
                            sin_bf[:, kc * C + dc * 128: kc * C + (dc + 1) * 128],
                            ch(fin_bf, kc, TT), start=(kc == 0), stop=(kc == 7))
                    inc(ins)
                pp.phase(tensor=mm)

                def sq(s, inc):
                    s.activation(out=sr[:], in_=pa[:, 0:TT], func=AF.Square)
                    ins = s.activation(out=si[:], in_=pb[:, 0:TT], func=AF.Square)
                    inc(ins)
                pp.phase(scalar=sq)

                def add2(v, inc):
                    ins = v.tensor_tensor(out=sr[:], in0=sr[:], in1=si[:], op=ALU.add)
                    inc(ins)
                pp.phase(vector=add2)

                def root(s, inc, dc=dc):
                    s.activation(out=sr[:], in_=sr[:], func=AF.Ln)
                    ins = s.activation(out=ch(r_f, dc, TT), in_=sr[:], func=AF.Exp,
                                       scale=0.5)
                    inc(ins)
                pp.phase(scalar=root)

            emit_ln(pp, kit, consts, r_f, pa, pb, out_f32=r_f)

            def addres2(v, inc):
                for c in range(8):
                    ins = v.tensor_tensor(out=ch(x1_t, c, TT), in0=ch(x1_t, c, TT),
                                          in1=ch(r_f, c, TT), op=ALU.add)
                inc(ins)
            pp.phase(vector=addres2)

            def store(e, inc, t0=t0):
                store_fm(e, inc, x2_o, x1_t, 8, TT, t0)
            pp.phase(gpsimd=store)
        pp.finish()
    return nc



def build_l4a_up():
    nc = bass.Bass("TRN2", target_bir_lowering=False)
    x2h = nc.dram_tensor("x2h", [C, TH], F32, kind="ExternalInput").ap()
    wup = nc.dram_tensor("wup", [C, 4096], F32, kind="ExternalInput").ap()
    xml_o = nc.dram_tensor("xml_o", [2048, TH], BF16, kind="ExternalOutput").ap()
    z_o = nc.dram_tensor("z_o", [2048, TC], BF16, kind="ExternalOutput").ap()
    with contextlib.ExitStack() as es:
        pp = PB(nc, es)
        stage = es.enter_context(nc.sbuf_tensor("stage", [128, 8 * 2048], F32))
        wh_bf = es.enter_context(nc.sbuf_tensor("wh_bf", [128, 8 * 2048], BF16))
        x_t = es.enter_context(nc.sbuf_tensor("x_t", [128, 8 * TT], F32))
        fin_bf = es.enter_context(nc.sbuf_tensor("fin_bf", [128, 8 * TH], BF16))
        out_bf = es.enter_context(nc.sbuf_tensor("out_bf", [128, TT], BF16))
        kit = LNKit(nc, es, "ln", 8, TT)
        pa = es.enter_context(nc.psum_tensor("pa", [128, TT], F32))
        pb = es.enter_context(nc.psum_tensor("pb", [128, TT], F32))
        consts = make_consts(nc, es, pp, {"eps5": 1e-5, "one": 1.0})

        for tt in range(NTH):
            t0 = tt * TT

            def load_x(e, inc, t0=t0):
                load_fm(e, inc, x_t, x2h, 8, TT, t0)
            pp.phase(sync=load_x)

            emit_ln(pp, kit, consts, x_t, pa, pb)

            def apply(v, inc, t0=t0):
                for c in range(8):
                    ins = v.tensor_tensor(out=fin_bf[:, c * TH + t0: c * TH + t0 + TT],
                                          in0=ch(kit.d, c, TT), in1=kit.rstd[:],
                                          op=ALU.mult)
                inc(ins)
            pp.phase(vector=apply)

        for half, (dram_out, ntt, toff, wcol0) in enumerate(
                ((z_o, NT, 512, 2048), (xml_o, NTH, 0, 0))):
            def loadw(e, inc, wcol0=wcol0):
                for kc in range(8):
                    inc(e.dma_start(out=stage[:, kc * 2048:(kc + 1) * 2048],
                                    in_=wup[kc * 128:(kc + 1) * 128, wcol0:wcol0 + 2048]), 16)
            pp.phase(sync=loadw)

            def castw(v, inc):
                ins = v.tensor_copy(out=wh_bf[:], in_=stage[:])
                inc(ins)
            pp.phase(vector=castw)

            for tt in range(ntt):
                t0 = toff + tt * TT
                for dc in range(16):
                    ps = pa if dc % 2 == 0 else pb

                    def mm(t, inc, dc=dc, ps=ps, t0=t0):
                        for kc in range(8):
                            ins = t.matmul(
                                ps[:, 0:TT],
                                wh_bf[:, kc * 2048 + dc * 128: kc * 2048 + (dc + 1) * 128],
                                fin_bf[:, kc * TH + t0: kc * TH + t0 + TT],
                                start=(kc == 0), stop=(kc == 7))
                        inc(ins)
                    pp.phase(tensor=mm)

                    def ev(v, inc, ps=ps):
                        ins = v.tensor_copy(out=out_bf[:], in_=ps[:, 0:TT])
                        inc(ins)
                    pp.phase(vector=ev)

                    def st(e, inc, dc=dc, t0=t0, dram_out=dram_out, toff=toff):
                        inc(e.dma_start(out=dram_out[dc * 128:(dc + 1) * 128,
                                                     t0 - toff: t0 - toff + TT],
                                        in_=out_bf[:]), 16)
                    pp.phase(gpsimd=st)
        pp.finish()
    return nc



def build_l4b_qkv():
    nc = bass.Bass("TRN2", target_bir_lowering=False)
    xml = nc.dram_tensor("xml", [2048, TH], BF16, kind="ExternalInput").ap()
    bq = nc.dram_tensor("bq", [16 * 128, 128], F32, kind="ExternalInput").ap()
    bk = nc.dram_tensor("bk", [16 * 128, 128], F32, kind="ExternalInput").ap()
    bv = nc.dram_tensor("bv", [16 * 128, 128], F32, kind="ExternalInput").ap()
    bqT = nc.dram_tensor("bqT", [16 * 128, 128], F32, kind="ExternalInput").ap()
    bkT = nc.dram_tensor("bkT", [16 * 128, 128], F32, kind="ExternalInput").ap()
    bvT = nc.dram_tensor("bvT", [16 * 128, 128], F32, kind="ExternalInput").ap()
    wig_q = nc.dram_tensor("wig_q", [2048, 8], F32, kind="ExternalInput").ap()
    wig_k = nc.dram_tensor("wig_k", [2048, 8], F32, kind="ExternalInput").ap()
    wig_v = nc.dram_tensor("wig_v", [2048, 8], F32, kind="ExternalInput").ap()
    bg = nc.dram_tensor("bg", [8, 1], F32, kind="ExternalInput").ap()
    cw = nc.dram_tensor("cw", [128, 64], F32, kind="ExternalInput").ap()
    cb = nc.dram_tensor("cb", [128, 16], F32, kind="ExternalInput").ap()
    q_o = nc.dram_tensor("q_o", [2048, TC], BF16, kind="ExternalOutput").ap()
    k_o = nc.dram_tensor("k_o", [2048, TC], BF16, kind="ExternalOutput").ap()
    v_o = nc.dram_tensor("v_o", [TC, 2048], BF16, kind="ExternalOutput").ap()
    xact_o = nc.dram_tensor("xact_o", [2048, TC], BF16, kind="ExternalOutput").ap()
    ig_o = nc.dram_tensor("ig_o", [4, TC], F32, kind="ExternalOutput").ap()
    fg_o = nc.dram_tensor("fg_o", [4, TC], F32, kind="ExternalOutput").ap()

    with contextlib.ExitStack() as es:
        pp = PB(nc, es)
        xml_sb = es.enter_context(nc.sbuf_tensor("xml_sb", [128, 16 * TH], BF16))
        xact_sb = es.enter_context(nc.sbuf_tensor("xact_sb", [128, 16 * TC], BF16))
        bq_bf = es.enter_context(nc.sbuf_tensor("bq_bf", [128, 16 * 128], BF16))
        bk_bf = es.enter_context(nc.sbuf_tensor("bk_bf", [128, 16 * 128], BF16))
        bv_bf = es.enter_context(nc.sbuf_tensor("bv_bf", [128, 16 * 128], BF16))
        stage = es.enter_context(nc.sbuf_tensor("stage", [128, 16 * 128], F32))
        stage2 = es.enter_context(nc.sbuf_tensor("stage2", [128, 16 * 128], F32))
        wiga_sb = es.enter_context(nc.sbuf_tensor("wiga_sb", [128, 16 * 8], F32))
        wigb_sb = es.enter_context(nc.sbuf_tensor("wigb_sb", [128, 16 * 8], F32))
        a_act = es.enter_context(nc.sbuf_tensor("a_act", [128, 16 * 8], BF16))
        a_ml = es.enter_context(nc.sbuf_tensor("a_ml", [128, 16 * 8], BF16))
        bg_sb = es.enter_context(nc.sbuf_tensor("bg_sb", [8, 1], F32))
        cw_sb = es.enter_context(nc.sbuf_tensor("cw_sb", [128, 64], F32))
        cb_sb = es.enter_context(nc.sbuf_tensor("cb_sb", [128, 16], F32))
        cv = es.enter_context(nc.sbuf_tensor("cv", [128, TC], F32))
        ev_bf = es.enter_context(nc.sbuf_tensor("ev_bf", [128, TT], BF16))
        g_f = es.enter_context(nc.sbuf_tensor("g_f", [8, TC], F32))
        pa = es.enter_context(nc.psum_tensor("pa", [128, TT], F32))
        pb = es.enter_context(nc.psum_tensor("pb", [128, TT], F32))
        pg = es.enter_context(nc.psum_tensor("pg", [8, TT], F32))
        p128 = es.enter_context(nc.psum_tensor("p128", [128, 128], F32))
        consts = make_consts(nc, es, pp, {"one": 1.0})

        def load_all(e, inc):
            for c in range(16):
                inc(e.dma_start(out=xml_sb[:, c * TH:(c + 1) * TH],
                                in_=xml[c * 128:(c + 1) * 128, :]), 16)
            inc(e.dma_start(out=cw_sb[:], in_=cw), 16)
            inc(e.dma_start(out=cb_sb[:], in_=cb), 16)
            inc(e.dma_start(out=bg_sb[:], in_=bg), 16)
        pp.phase(sync=load_all)

        for src, dst in ((bq, bq_bf), (bk, bk_bf), (bv, bv_bf)):
            def loadw(e, inc, src=src):
                for c in range(16):
                    inc(e.dma_start(out=stage[:, c * 128:(c + 1) * 128],
                                    in_=src[c * 128:(c + 1) * 128, :]), 16)
            pp.phase(sync=loadw)

            def castw(v, inc, dst=dst):
                ins = v.tensor_copy(out=dst[:], in_=stage[:])
                inc(ins)
            pp.phase(vector=castw)

        def load_fold1(e, inc):
            for c in range(16):
                inc(e.dma_start(out=stage[:, c * 128:(c + 1) * 128],
                                in_=bqT[c * 128:(c + 1) * 128, :]), 16)
                inc(e.dma_start(out=stage2[:, c * 128:(c + 1) * 128],
                                in_=bkT[c * 128:(c + 1) * 128, :]), 16)
                inc(e.dma_start(out=wiga_sb[:, c * 8:(c + 1) * 8],
                                in_=wig_q[c * 128:(c + 1) * 128, :]), 16)
                inc(e.dma_start(out=wigb_sb[:, c * 8:(c + 1) * 8],
                                in_=wig_k[c * 128:(c + 1) * 128, :]), 16)
        pp.phase(sync=load_fold1)

        def fold1(t, inc):
            for c in range(16):
                t.matmul(p128[:, c * 8:(c + 1) * 8], stage[:, c * 128:(c + 1) * 128],
                         wiga_sb[:, c * 8:(c + 1) * 8], start=True, stop=False)
                ins = t.matmul(p128[:, c * 8:(c + 1) * 8],
                               stage2[:, c * 128:(c + 1) * 128],
                               wigb_sb[:, c * 8:(c + 1) * 8], start=False, stop=True)
            inc(ins)
        pp.phase(tensor=fold1)

        def evf1(v, inc):
            ins = v.tensor_copy(out=a_act[:], in_=p128[:, 0:128])
            inc(ins)
        pp.phase(vector=evf1)

        def load_fold2(e, inc):
            for c in range(16):
                inc(e.dma_start(out=stage[:, c * 128:(c + 1) * 128],
                                in_=bvT[c * 128:(c + 1) * 128, :]), 16)
                inc(e.dma_start(out=wiga_sb[:, c * 8:(c + 1) * 8],
                                in_=wig_v[c * 128:(c + 1) * 128, :]), 16)
        pp.phase(sync=load_fold2)

        def fold2(t, inc):
            for c in range(16):
                ins = t.matmul(p128[:, c * 8:(c + 1) * 8],
                               stage[:, c * 128:(c + 1) * 128],
                               wiga_sb[:, c * 8:(c + 1) * 8], start=True, stop=True)
            inc(ins)
        pp.phase(tensor=fold2)

        def evf2(v, inc):
            ins = v.tensor_copy(out=a_ml[:], in_=p128[:, 0:128])
            inc(ins)
        pp.phase(vector=evf2)

        for c in range(16):
            def conv(v, inc, c=c):
                base = c * TH + 509
                v.tensor_scalar(out=cv[:], in0=xml_sb[:, base:base + TC],
                                scalar1=cw_sb[:, c * 4:c * 4 + 1], scalar2=None,
                                op0=ALU.mult)
                for i in (1, 2, 3):
                    ins = v.scalar_tensor_tensor(
                        out=cv[:], in0=xml_sb[:, base + i:base + i + TC],
                        scalar=cw_sb[:, c * 4 + i:c * 4 + i + 1], in1=cv[:],
                        op0=ALU.mult, op1=ALU.add)
                inc(ins)
            pp.phase(vector=conv)

            def act(s, inc, c=c):
                ins = s.activation(out=ch(xact_sb, c, TC), in_=cv[:], func=AF.Silu,
                                   bias=cb_sb[:, c:c + 1])
                inc(ins)
            pp.phase(scalar=act)

        def st_act(e, inc):
            for c in range(16):
                inc(e.dma_start(out=xact_o[c * 128:(c + 1) * 128, :],
                                in_=ch(xact_sb, c, TC)), 16)
        pp.phase(gpsimd=st_act)

        for w_bf, dram_out in ((bq_bf, q_o), (bk_bf, k_o)):
            for c in range(16):
                for tt in range(NT):
                    ps = pa if tt % 2 == 0 else pb

                    def mm(t, inc, c=c, tt=tt, ps=ps, w_bf=w_bf):
                        ins = t.matmul(ps[:, 0:TT], w_bf[:, c * 128:(c + 1) * 128],
                                       xact_sb[:, c * TC + tt * TT: c * TC + (tt + 1) * TT],
                                       start=True, stop=True)
                        inc(ins)
                    pp.phase(tensor=mm)

                    def ev(v, inc, ps=ps):
                        ins = v.tensor_copy(out=ev_bf[:], in_=ps[:, 0:TT])
                        inc(ins)
                    pp.phase(vector=ev)

                    def st(e, inc, c=c, tt=tt, dram_out=dram_out):
                        inc(e.dma_start(out=dram_out[c * 128:(c + 1) * 128,
                                                     tt * TT:(tt + 1) * TT],
                                        in_=ev_bf[:]), 16)
                    pp.phase(gpsimd=st)

        for tau in range(16):
            for g in range(4):
                def mm(t, inc, tau=tau, g=g):
                    for j in range(4):
                        c = 4 * g + j
                        ins = t.matmul(pa[:, j * 128:(j + 1) * 128],
                                       xml_sb[:, c * TH + 512 + tau * 128:
                                              c * TH + 512 + (tau + 1) * 128],
                                       bv_bf[:, c * 128:(c + 1) * 128],
                                       start=True, stop=True)
                    inc(ins)
                pp.phase(tensor=mm)

                def ev(v, inc):
                    ins = v.tensor_copy(out=ev_bf[:], in_=pa[:, 0:TT])
                    inc(ins)
                pp.phase(vector=ev)

                def st(e, inc, tau=tau, g=g):
                    inc(e.dma_start(out=v_o[tau * 128:(tau + 1) * 128,
                                           g * 512:(g + 1) * 512],
                                    in_=ev_bf[:]), 16)
                pp.phase(gpsimd=st)

        for tt in range(NT):
            def mm(t, inc, tt=tt):
                for c in range(16):
                    t.matmul(pg[:, 0:TT], a_act[:, c * 8:(c + 1) * 8],
                             xact_sb[:, c * TC + tt * TT: c * TC + (tt + 1) * TT],
                             start=(c == 0), stop=False)
                for c in range(16):
                    ins = t.matmul(pg[:, 0:TT], a_ml[:, c * 8:(c + 1) * 8],
                                   xml_sb[:, c * TH + 512 + tt * TT:
                                          c * TH + 512 + (tt + 1) * TT],
                                   start=False, stop=(c == 15))
                inc(ins)
            pp.phase(tensor=mm)

            def ev(v, inc, tt=tt):
                ins = v.tensor_scalar(out=g_f[:, tt * TT:(tt + 1) * TT],
                                      in0=pg[:, 0:TT], scalar1=bg_sb[:],
                                      scalar2=None, op0=ALU.add)
                inc(ins)
            pp.phase(vector=ev)

        def st_g(e, inc):
            inc(e.dma_start(out=ig_o, in_=g_f[0:4, :]), 16)
            inc(e.dma_start(out=fg_o, in_=g_f[4:8, :]), 16)
        pp.phase(gpsimd=st_g)

        pp.finish()
    return nc


QB = 256
NQB = TF // QB
DH = 512


def build_l5_attn():
    nc = bass.Bass("TRN2", target_bir_lowering=False)
    ins = {}
    for u in range(2):
        ins[f"qf{u}"] = nc.dram_tensor(f"qf{u}", [DH, TF], BF16, kind="ExternalInput").ap()
        ins[f"kf{u}"] = nc.dram_tensor(f"kf{u}", [DH, TF], BF16, kind="ExternalInput").ap()
        ins[f"v1{u}"] = nc.dram_tensor(f"v1{u}", [TF, DH], BF16, kind="ExternalInput").ap()
        ins[f"ig{u}"] = nc.dram_tensor(f"ig{u}", [1, TF], F32, kind="ExternalInput").ap()
        ins[f"fg{u}"] = nc.dram_tensor(f"fg{u}", [1, TF], F32, kind="ExternalInput").ap()
        ins[f"nw{u}"] = nc.dram_tensor(f"nw{u}", [128, DH], F32, kind="ExternalInput").ap()
    ident_in = nc.dram_tensor("ident_in", [128, 128], F32, kind="ExternalInput").ap()
    mbig_in = nc.dram_tensor("mbig_in", [128, 512], BF16, kind="ExternalInput").ap()
    outs = {}
    for u in range(2):
        outs[u] = nc.dram_tensor(f"hn{u}", [TF, DH], BF16, kind="ExternalOutput").ap()
    g3 = nc.dram_tensor("g3", [3, TF], F32, kind="Internal").ap()

    with contextlib.ExitStack() as es:
        pp = PB(nc, es)
        q_sb = es.enter_context(nc.sbuf_tensor("q_sb", [128, 4 * TF], BF16))
        k_sb = es.enter_context(nc.sbuf_tensor("k_sb", [128, 4 * TF], BF16))
        vbuf = es.enter_context(nc.sbuf_tensor("vbuf", [128, 32 * DH], BF16))
        ones_pv = es.enter_context(nc.sbuf_tensor("ones_pv", [128, 1], BF16))
        igr = es.enter_context(nc.sbuf_tensor("igr", [1, TF], F32))
        fgr = es.enter_context(nc.sbuf_tensor("fgr", [1, TF], F32))
        w1r = es.enter_context(nc.sbuf_tensor("w1r", [1, TF], F32))
        w2r = es.enter_context(nc.sbuf_tensor("w2r", [1, TF], F32))
        Fr = es.enter_context(nc.sbuf_tensor("Fr", [1, TF], F32))
        igmF = es.enter_context(nc.sbuf_tensor("igmF", [1, TF], F32))
        rmax = w2r
        nmd = igr
        t32 = es.enter_context(nc.sbuf_tensor("t32", [32, 128], F32))
        ident = es.enter_context(nc.sbuf_tensor("ident_s", [128, 128], F32))
        mbig = es.enter_context(nc.sbuf_tensor("mbig", [128, 512], BF16))
        nw_sb = es.enter_context(nc.sbuf_tensor("nw_sb", [128, DH], F32))
        igmF_tm = es.enter_context(nc.sbuf_tensor("igmF_tm", [128, 32], F32))
        rmax_tm = es.enter_context(nc.sbuf_tensor("rmax_tm", [128, 32], F32))
        encmD_tm = es.enter_context(nc.sbuf_tensor("encmD_tm", [128, 32], F32))
        cq_row = es.enter_context(nc.sbuf_tensor("cq_row", [1, NQB], F32))
        cq_b = es.enter_context(nc.sbuf_tensor("cq_b", [128, NQB], F32))
        ncq3 = es.enter_context(nc.sbuf_tensor("ncq3", [128, NQB], F32))
        colsc = es.enter_context(nc.sbuf_tensor("colsc", [128, 32], F32))
        rf = es.enter_context(nc.sbuf_tensor("rf", [128, 2], F32))
        T_sb = es.enter_context(nc.sbuf_tensor("T_sb", [128, QB], BF16))
        hsb = es.enter_context(nc.sbuf_tensor("hsb", [128, DH], F32))
        dsc = es.enter_context(nc.sbuf_tensor("dsc", [128, 1], F32))
        nrm = es.enter_context(nc.sbuf_tensor("nrm", [128, 1], F32))
        nrm2 = es.enter_context(nc.sbuf_tensor("nrm2", [128, 1], F32))
        mh_mu = es.enter_context(nc.sbuf_tensor("mh_mu", [128, 1], F32))
        mh_d = es.enter_context(nc.sbuf_tensor("mh_d", [128, DH], F32))
        mh_sq = es.enter_context(nc.sbuf_tensor("mh_sq", [128, DH], BF16))
        mh_vs = es.enter_context(nc.sbuf_tensor("mh_vs", [128, 1], F32))
        mh_r = es.enter_context(nc.sbuf_tensor("mh_r", [128, 1], F32))
        hn_bf = es.enter_context(nc.sbuf_tensor("hn_bf", [128, DH], BF16))
        ps_s = es.enter_context(nc.psum_tensor("ps_s", [128, QB], F32))
        ps_a0 = es.enter_context(nc.psum_tensor("ps_a0", [128, DH], F32))
        ps_a1 = es.enter_context(nc.psum_tensor("ps_a1", [128, DH], F32))
        ps_d0 = es.enter_context(nc.psum_tensor("ps_d0", [128, 1], F32))
        ps_d1 = es.enter_context(nc.psum_tensor("ps_d1", [128, 1], F32))
        ps_t = es.enter_context(nc.psum_tensor("ps_t", [128, 32], F32))
        ps_b = es.enter_context(nc.psum_tensor("ps_b", [128, NQB], F32))
        consts = make_consts(nc, es, pp, {"one": 1.0, "eps6": 1e-6, "m1e6": 1e-6})

        def load_const(e, inc):
            inc(e.dma_start(out=ident[:], in_=ident_in), 16)
            inc(e.dma_start(out=mbig[:], in_=mbig_in), 16)
        pp.phase(sync=load_const)

        def mk1(v, inc):
            ins = v.memset(ones_pv[:], 1.0)
            inc(ins)
        pp.phase(vector=mk1)

        for u in range(2):
            def load_u(e, inc, u=u):
                for c in range(4):
                    inc(e.dma_start(out=q_sb[:, c * TF:(c + 1) * TF],
                                    in_=ins[f"qf{u}"][c * 128:(c + 1) * 128, :]), 16)
                    inc(e.dma_start(out=k_sb[:, c * TF:(c + 1) * TF],
                                    in_=ins[f"kf{u}"][c * 128:(c + 1) * 128, :]), 16)
                for j in range(32):
                    inc(e.dma_start(out=vbuf[:, j * DH:(j + 1) * DH],
                                    in_=ins[f"v1{u}"][j * 128:(j + 1) * 128, :]), 16)
                inc(e.dma_start(out=igr[:], in_=ins[f"ig{u}"]), 16)
                inc(e.dma_start(out=fgr[:], in_=ins[f"fg{u}"]), 16)
                inc(e.dma_start(out=nw_sb[:], in_=ins[f"nw{u}"]), 16)
            pp.phase(sync=load_u)

            def lf1(s, inc):
                inc(s.activation(out=w1r[:], in_=fgr[:], func=AF.Abs), wait=True)
                inc(s.activation(out=w1r[:], in_=w1r[:], func=AF.Exp, scale=-1.0), wait=True)
                inc(s.activation(out=w1r[:], in_=w1r[:], func=AF.Ln,
                                 bias=consts["one"][0:1, :]), wait=True)
                inc(s.activation(out=w2r[:], in_=fgr[:], func=AF.Relu))
            pp.phase(scalar=lf1)

            def lf2(v, inc):
                inc(v.tensor_tensor(out=w2r[:], in0=fgr[:], in1=w2r[:], op=ALU.subtract), wait=True)
                inc(v.tensor_tensor(out=w2r[:], in0=w2r[:], in1=w1r[:], op=ALU.subtract), wait=True)
                inc(v.memset(w1r[:], 1.0), wait=True)
                inc(v.tensor_tensor_scan(out=Fr[:], data0=w1r[:], data1=w2r[:],
                                         initial=0.0, op0=ALU.mult, op1=ALU.add), wait=True)
                inc(v.tensor_tensor(out=igmF[:], in0=igr[:], in1=Fr[:], op=ALU.subtract), wait=True)
                inc(v.tensor_tensor_scan(out=rmax[:], data0=igmF[:], data1=igmF[:],
                                         initial=NEG_BIG, op0=ALU.max, op1=ALU.max), wait=True)
                inc(v.tensor_tensor(out=nmd[:], in0=Fr[:], in1=rmax[:], op=ALU.add), wait=True)
                inc(v.tensor_scalar(out=nmd[:], in0=nmd[:], scalar1=-1.0,
                                    scalar2=None, op0=ALU.mult))
            pp.phase(vector=lf2)

            def to_dram(e, inc):
                inc(e.dma_start(out=g3[0:1, :], in_=igmF[:]), 16)
                inc(e.dma_start(out=g3[1:2, :], in_=rmax[:]), 16)
                inc(e.dma_start(out=g3[2:3, :], in_=nmd[:]), 16)
            pp.phase(gpsimd=to_dram)

            for r, dst in ((0, igmF_tm), (1, rmax_tm), (2, encmD_tm)):
                def ld32(e, inc, r=r):
                    inc(e.dma_start(out=t32[:],
                                    in_=g3[r:r + 1, :].rearrange("one (a b) -> (one a) b", b=128)), 16)
                pp.phase(sync=ld32)

                def tp(t, inc):
                    ins = t.transpose(out=ps_t[:], in_=t32[:], identity=ident[0:32, 0:32])
                    inc(ins)
                pp.phase(tensor=tp)

                if r < 2:
                    def cp(v, inc, dst=dst):
                        ins = v.tensor_copy(out=dst[:], in_=ps_t[:])
                        inc(ins)
                    pp.phase(vector=cp)
                else:
                    def cp(s, inc, dst=dst):
                        ins = s.activation(out=dst[:], in_=ps_t[:], func=AF.Exp)
                        inc(ins)
                    pp.phase(scalar=cp)

            def cq1(v, inc):
                ins = v.tensor_copy(out=cq_row[:], in_=rmax[0:1, QB - 1:TF:QB])
                inc(ins)
            pp.phase(vector=cq1)

            def cq2(t, inc):
                ins = t.matmul(ps_b[:, 0:NQB], consts["ones_row"], cq_row[:],
                               start=True, stop=True)
                inc(ins)
            pp.phase(tensor=cq2)

            def cq3(v, inc):
                v.tensor_copy(out=cq_b[:], in_=ps_b[:, 0:NQB])
                ins = v.tensor_scalar(out=ncq3[:], in0=ps_b[:, 0:NQB],
                                      scalar1=CSC_SHIFT, scalar2=-1.0,
                                      op0=ALU.add, op1=ALU.mult)
                inc(ins)
            pp.phase(vector=cq3)

            for qb in range(NQB):
                nk = 2 * qb + 2

                def gsc(s, inc, qb=qb, nk=nk):
                    s.activation(out=colsc[:, 0:nk], in_=igmF_tm[:, 0:nk], func=AF.Exp,
                                 bias=ncq3[:, qb:qb + 1])
                    ins = s.activation(out=rf[:], in_=rmax_tm[:, 2 * qb:2 * qb + 2],
                                       func=AF.Exp, scale=-1.0, bias=cq_b[:, qb:qb + 1])
                    inc(ins)
                pp.phase(scalar=gsc)

                for j in range(nk):
                    def mm_s(t, inc, qb=qb, j=j):
                        for c in range(4):
                            ins = t.matmul(ps_s[:, 0:QB],
                                           k_sb[:, c * TF + j * 128: c * TF + (j + 1) * 128],
                                           q_sb[:, c * TF + qb * QB: c * TF + (qb + 1) * QB],
                                           start=(c == 0), stop=(c == 3))
                        inc(ins)
                    pp.phase(tensor=mm_s)

                    diag_r = j - 2 * qb

                    def evs(v, inc, j=j, diag_r=diag_r):
                        if diag_r in (0, 1):
                            ins = v.scalar_tensor_tensor(
                                out=T_sb[:], in0=ps_s[:, 0:QB],
                                scalar=colsc[:, j:j + 1],
                                in1=mbig[:, 256 - 128 * diag_r: 512 - 128 * diag_r],
                                op0=ALU.mult, op1=ALU.mult)
                        else:
                            ins = v.tensor_scalar(out=T_sb[:], in0=ps_s[:, 0:QB],
                                                  scalar1=colsc[:, j:j + 1],
                                                  scalar2=None, op0=ALU.mult)
                        inc(ins)
                    pp.phase(vector=evs)

                    def mm_v(t, inc, j=j, nk=nk):
                        st_, sp_ = (j == 0), (j == nk - 1)
                        t.matmul(ps_a0[:, 0:DH], T_sb[:, 0:128],
                                 vbuf[:, j * DH:(j + 1) * DH], start=st_, stop=sp_)
                        t.matmul(ps_a1[:, 0:DH], T_sb[:, 128:256],
                                 vbuf[:, j * DH:(j + 1) * DH], start=st_, stop=sp_)
                        t.matmul(ps_d0[:, 0:1], T_sb[:, 0:128], ones_pv[:],
                                 start=st_, stop=sp_)
                        ins = t.matmul(ps_d1[:, 0:1], T_sb[:, 128:256], ones_pv[:],
                                       start=st_, stop=sp_)
                        inc(ins)
                    pp.phase(tensor=mm_v)

                for qc in range(2):
                    ps_acc = ps_a0 if qc == 0 else ps_a1
                    ps_den = ps_d0 if qc == 0 else ps_d1
                    tmc = 2 * qb + qc

                    def ep1(v, inc, ps_acc=ps_acc, ps_den=ps_den, qc=qc):
                        v.tensor_scalar(out=hsb[:], in0=ps_acc[:, 0:DH],
                                        scalar1=rf[:, qc:qc + 1], scalar2=None,
                                        op0=ALU.mult)
                        ins = v.tensor_scalar(out=dsc[:], in0=ps_den[:, 0:1],
                                              scalar1=rf[:, qc:qc + 1], scalar2=None,
                                              op0=ALU.mult)
                        inc(ins)
                    pp.phase(vector=ep1)

                    def ep1b(s, inc):
                        ins = s.activation(out=nrm[:], in_=dsc[:], func=AF.Abs)
                        inc(ins)
                    pp.phase(scalar=ep1b)

                    def ep2(v, inc, tmc=tmc):
                        ins = v.tensor_tensor(out=nrm[:], in0=nrm[:],
                                              in1=encmD_tm[:, tmc:tmc + 1], op=ALU.max)
                        inc(ins)
                    pp.phase(vector=ep2)

                    def ep3(s, inc):
                        inc(s.activation(out=nrm2[:], in_=nrm[:], func=AF.Ln,
                                         bias=consts["m1e6"]), wait=True)
                        inc(s.activation(out=nrm2[:], in_=nrm2[:], func=AF.Exp,
                                         scale=-1.0))
                    pp.phase(scalar=ep3)

                    def ep4a(v, inc):
                        v.tensor_scalar(out=mh_d[:], in0=hsb[:],
                                        scalar1=nrm2[:], scalar2=None, op0=ALU.mult)
                        ins = v.tensor_reduce(out=mh_mu[:], in_=mh_d[:], axis=AX,
                                              op=ALU.add)
                        inc(ins)
                    pp.phase(vector=ep4a)

                    def ep4b(s, inc):
                        ins = s.mul(out=mh_mu[:], in_=mh_mu[:], mul=1.0 / DH)
                        inc(ins)
                    pp.phase(scalar=ep4b)

                    def ep4c(v, inc):
                        ins = v.tensor_scalar(out=mh_d[:], in0=mh_d[:],
                                              scalar1=mh_mu[:], scalar2=None,
                                              op0=ALU.subtract)
                        inc(ins)
                    pp.phase(vector=ep4c)

                    def ep5(s, inc):
                        inc(s.activation(out=mh_sq[:], in_=mh_d[:], func=AF.Square,
                                         accum_out=mh_vs[:]), wait=True)
                        inc(s.activation(out=mh_r[:], in_=mh_vs[:], func=AF.Ln,
                                         scale=1.0 / DH, bias=consts["eps6"]), wait=True)
                        inc(s.activation(out=mh_r[:], in_=mh_r[:], func=AF.Exp,
                                         scale=-0.5))
                    pp.phase(scalar=ep5)

                    def ep6(v, inc):
                        v.tensor_scalar(out=mh_d[:], in0=mh_d[:], scalar1=mh_r[:],
                                        scalar2=None, op0=ALU.mult)
                        ins = v.tensor_tensor(out=hn_bf[:], in0=mh_d[:], in1=nw_sb[:],
                                              op=ALU.mult)
                        inc(ins)
                    pp.phase(vector=ep6)

                    def ep7(e, inc, qb=qb, qc=qc, u=u):
                        r0 = qb * QB + qc * 128
                        inc(e.dma_start(out=outs[u][r0:r0 + 128, :], in_=hn_bf[:]), 16)
                    pp.phase(gpsimd=ep7)

        pp.finish()
    return nc



def build_l6a_post():
    nc = bass.Bass("TRN2", target_bir_lowering=False)
    x2 = nc.dram_tensor("x2", [C, TC], F32, kind="ExternalInput").ap()
    xact = nc.dram_tensor("xact", [2048, TC], BF16, kind="ExternalInput").ap()
    z = nc.dram_tensor("z", [2048, TC], BF16, kind="ExternalInput").ap()
    hn = nc.dram_tensor("hn", [2048, TC], BF16, kind="ExternalInput").ap()
    skip = nc.dram_tensor("skip", [128, 16], F32, kind="ExternalInput").ap()
    wdn = nc.dram_tensor("wdn", [2048, C], F32, kind="ExternalInput").ap()
    x3_o = nc.dram_tensor("x3_o", [C, TC], F32, kind="ExternalOutput").ap()
    with contextlib.ExitStack() as es:
        pp = PB(nc, es)
        stage = es.enter_context(nc.sbuf_tensor("stage", [128, 4 * C], F32))
        w_bf = es.enter_context(nc.sbuf_tensor("w_bf", [128, 16 * C], BF16))
        xa_t = es.enter_context(nc.sbuf_tensor("xa_t", [128, 16 * TT], BF16))
        z_t = es.enter_context(nc.sbuf_tensor("z_t", [128, 16 * TT], BF16))
        hn_t = es.enter_context(nc.sbuf_tensor("hn_t", [128, 16 * TT], BF16))
        x2_t = es.enter_context(nc.sbuf_tensor("x2_t", [128, 8 * TT], F32))
        ao_f = es.enter_context(nc.sbuf_tensor("ao_f", [128, 8 * TT], F32))
        sz = es.enter_context(nc.sbuf_tensor("sz", [128, TT], F32))
        tmf = es.enter_context(nc.sbuf_tensor("tmf", [128, TT], F32))
        skip_sb = es.enter_context(nc.sbuf_tensor("skip_sb", [128, 16], F32))
        kit = LNKit(nc, es, "ln", 8, TT)
        pa = es.enter_context(nc.psum_tensor("pa", [128, TT], F32))
        pb = es.enter_context(nc.psum_tensor("pb", [128, TT], F32))
        consts = make_consts(nc, es, pp, {"eps5": 1e-5, "one": 1.0})

        for rnd in range(4):
            def loadw(e, inc, rnd=rnd):
                for j in range(4):
                    kc = rnd * 4 + j
                    inc(e.dma_start(out=stage[:, j * C:(j + 1) * C],
                                    in_=wdn[kc * 128:(kc + 1) * 128, :]), 16)
            pp.phase(sync=loadw)

            def castw(v, inc, rnd=rnd):
                ins = v.tensor_copy(out=w_bf[:, rnd * 4 * C:(rnd + 1) * 4 * C],
                                    in_=stage[:])
                inc(ins)
            pp.phase(vector=castw)

        def loads(e, inc):
            inc(e.dma_start(out=skip_sb[:], in_=skip), 16)
        pp.phase(sync=loads)

        for tt in range(NT):
            t0 = tt * TT

            def load_t(e, inc, t0=t0):
                load_fm(e, inc, xa_t, xact, 16, TT, t0)
                load_fm(e, inc, z_t, z, 16, TT, t0)
                load_fm(e, inc, hn_t, hn, 16, TT, t0)
                load_fm(e, inc, x2_t, x2, 8, TT, t0)
            pp.phase(sync=load_t)

            for c in range(16):
                def hs1(s, inc, c=c):
                    ins = s.activation(out=sz[:], in_=ch(z_t, c, TT), func=AF.Silu)
                    inc(ins)
                pp.phase(scalar=hs1)

                def hs2(v, inc, c=c):
                    v.scalar_tensor_tensor(out=tmf[:], in0=ch(xa_t, c, TT),
                                           scalar=skip_sb[:, c:c + 1],
                                           in1=ch(hn_t, c, TT),
                                           op0=ALU.mult, op1=ALU.add)
                    ins = v.tensor_tensor(out=ch(hn_t, c, TT), in0=tmf[:], in1=sz[:],
                                          op=ALU.mult)
                    inc(ins)
                pp.phase(vector=hs2)

            for dc in range(8):
                ps = pa if dc % 2 == 0 else pb

                def mm(t, inc, dc=dc, ps=ps):
                    for kc in range(16):
                        ins = t.matmul(ps[:, 0:TT],
                                       w_bf[:, kc * C + dc * 128: kc * C + (dc + 1) * 128],
                                       ch(hn_t, kc, TT), start=(kc == 0), stop=(kc == 15))
                    inc(ins)
                pp.phase(tensor=mm)

                def ev(v, inc, dc=dc, ps=ps):
                    ins = v.tensor_copy(out=ch(ao_f, dc, TT), in_=ps[:, 0:TT])
                    inc(ins)
                pp.phase(vector=ev)

            emit_ln(pp, kit, consts, ao_f, pa, pb, out_f32=ao_f)

            def addres(v, inc):
                for c in range(8):
                    ins = v.tensor_tensor(out=ch(x2_t, c, TT), in0=ch(x2_t, c, TT),
                                          in1=ch(ao_f, c, TT), op=ALU.add)
                inc(ins)
            pp.phase(vector=addres)

            def store(e, inc, t0=t0):
                store_fm(e, inc, x3_o, x2_t, 8, TT, t0)
            pp.phase(gpsimd=store)
        pp.finish()
    return nc


def build_l6b_ffn1():
    nc = bass.Bass("TRN2", target_bir_lowering=False)
    x3 = nc.dram_tensor("x3", [C, TC], F32, kind="ExternalInput").ap()
    w1 = nc.dram_tensor("w1", [C, 4096], F32, kind="ExternalInput").ap()
    b1 = nc.dram_tensor("b1", [128, 32], F32, kind="ExternalInput").ap()
    w2 = nc.dram_tensor("w2", [4096, C], F32, kind="ExternalInput").ap()
    b2 = nc.dram_tensor("b2", [128, 8], F32, kind="ExternalInput").ap()
    x4_o = nc.dram_tensor("x4_o", [C, TC], F32, kind="ExternalOutput").ap()
    w2bf_d = nc.dram_tensor("w2bf_d", [4096, C], BF16, kind="Internal").ap()
    with contextlib.ExitStack() as es:
        pp = PB(nc, es)
        stage = es.enter_context(nc.sbuf_tensor("stage", [128, 4096], F32))
        w1_bf = es.enter_context(nc.sbuf_tensor("w1_bf", [128, 8 * 4096], BF16))
        w2t = es.enter_context(nc.sbuf_tensor("w2t", [128, C], BF16))
        b1_sb = es.enter_context(nc.sbuf_tensor("b1_sb", [128, 32], F32))
        b2_sb = es.enter_context(nc.sbuf_tensor("b2_sb", [128, 8], F32))
        x3_t = es.enter_context(nc.sbuf_tensor("x3_t", [128, 8 * TT], F32))
        fin_bf = es.enter_context(nc.sbuf_tensor("fin_bf", [128, 8 * TT], BF16))
        hh_bf = es.enter_context(nc.sbuf_tensor("hh_bf", [128, 32 * TT], BF16))
        ffn_f = es.enter_context(nc.sbuf_tensor("ffn_f", [128, 8 * TT], F32))
        kit = LNKit(nc, es, "ln", 8, TT)
        pa = es.enter_context(nc.psum_tensor("pa", [128, TT], F32))
        pb = es.enter_context(nc.psum_tensor("pb", [128, TT], F32))
        pc0 = es.enter_context(nc.psum_tensor("pc0", [128, TT], F32))
        pc1 = es.enter_context(nc.psum_tensor("pc1", [128, TT], F32))
        pc2 = es.enter_context(nc.psum_tensor("pc2", [128, TT], F32))
        pc3 = es.enter_context(nc.psum_tensor("pc3", [128, TT], F32))
        pcs = [pc0, pc1, pc2, pc3]
        consts = make_consts(nc, es, pp, {"eps5": 1e-5, "one": 1.0})

        for kc in range(8):
            def loadw1(e, inc, kc=kc):
                inc(e.dma_start(out=stage[:], in_=w1[kc * 128:(kc + 1) * 128, :]), 16)
            pp.phase(sync=loadw1)

            def castw1(v, inc, kc=kc):
                ins = v.tensor_copy(out=w1_bf[:, kc * 4096:(kc + 1) * 4096], in_=stage[:])
                inc(ins)
            pp.phase(vector=castw1)

        def load_b(e, inc):
            inc(e.dma_start(out=b1_sb[:], in_=b1), 16)
            inc(e.dma_start(out=b2_sb[:], in_=b2), 16)
        pp.phase(sync=load_b)

        for kc in range(32):
            def loadw2(e, inc, kc=kc):
                inc(e.dma_start(out=stage[:, 0:C], in_=w2[kc * 128:(kc + 1) * 128, :]), 16)
            pp.phase(sync=loadw2)

            def castw2(v, inc):
                ins = v.tensor_copy(out=w2t[:], in_=stage[:, 0:C])
                inc(ins)
            pp.phase(vector=castw2)

            def stw2(e, inc, kc=kc):
                inc(e.dma_start(out=w2bf_d[kc * 128:(kc + 1) * 128, :], in_=w2t[:]), 16)
            pp.phase(gpsimd=stw2)

        for tt in range(NT):
            t0 = tt * TT

            def load_x(e, inc, t0=t0):
                load_fm(e, inc, x3_t, x3, 8, TT, t0)
            pp.phase(sync=load_x)

            emit_ln(pp, kit, consts, x3_t, pa, pb, out_bf=fin_bf)

            for hcol in range(32):
                ps = pa if hcol % 2 == 0 else pb

                def mm1(t, inc, hcol=hcol, ps=ps):
                    for kc in range(8):
                        ins = t.matmul(
                            ps[:, 0:TT],
                            w1_bf[:, kc * 4096 + hcol * 128: kc * 4096 + (hcol + 1) * 128],
                            ch(fin_bf, kc, TT), start=(kc == 0), stop=(kc == 7))
                    inc(ins)
                pp.phase(tensor=mm1)

                def gl(s, inc, hcol=hcol, ps=ps):
                    ins = s.activation(out=ch(hh_bf, hcol, TT), in_=ps[:, 0:TT],
                                       func=AF.Gelu, bias=b1_sb[:, hcol:hcol + 1])
                    inc(ins)
                pp.phase(scalar=gl)

            for dcg in range(2):
                for kc in range(32):
                    def loadw2t(e, inc, kc=kc):
                        inc(e.dma_start(out=w2t[:],
                                        in_=w2bf_d[kc * 128:(kc + 1) * 128, :]), 16)
                    pp.phase(sync=loadw2t)

                    def mm2(t, inc, kc=kc, dcg=dcg):
                        for j in range(4):
                            dc = dcg * 4 + j
                            ins = t.matmul(pcs[j][:, 0:TT],
                                           w2t[:, dc * 128:(dc + 1) * 128],
                                           ch(hh_bf, kc, TT),
                                           start=(kc == 0), stop=(kc == 31))
                        inc(ins)
                    pp.phase(tensor=mm2)

                def ev2(v, inc, dcg=dcg):
                    for j in range(4):
                        dc = dcg * 4 + j
                        ins = v.tensor_scalar(out=ch(ffn_f, dc, TT),
                                              in0=pcs[j][:, 0:TT],
                                              scalar1=b2_sb[:, dc:dc + 1],
                                              scalar2=None, op0=ALU.add)
                    inc(ins)
                pp.phase(vector=ev2)

            emit_ln(pp, kit, consts, ffn_f, pa, pb, out_f32=ffn_f)

            def addres(v, inc):
                for c in range(8):
                    ins = v.tensor_tensor(out=ch(x3_t, c, TT), in0=ch(x3_t, c, TT),
                                          in1=ch(ffn_f, c, TT), op=ALU.add)
                inc(ins)
            pp.phase(vector=addres)

            def store(e, inc, t0=t0):
                store_fm(e, inc, x4_o, x3_t, 8, TT, t0)
            pp.phase(gpsimd=store)
        pp.finish()
    return nc


QBF = 512
THL = 2051
DBG = False


def build_fused():
    nc = bass.Bass("TRN2", target_bir_lowering=False)
    I = lambda n, s, d=F32: nc.dram_tensor(n, s, d, kind="ExternalInput").ap()
    xs_d = I("xs", [C, TC])
    wiT_d = I("wiT", [C, C]); wrT_d = I("wrT", [C, C])
    bi_d = I("bi", [128, 8]); br_d = I("br", [128, 8]); lam_d = I("lam", [1, 1])
    th_d = I("th", [C, C]); thn_d = I("thn", [C, C])
    ph_d = I("ph", [C, C]); phn_d = I("phn", [C, C])
    wup_d = I("wup", [C, 4096])
    bq_d = I("bq", [2048, 128]); bk_d = I("bk", [2048, 128]); bv_d = I("bv", [2048, 128])
    bqT_d = I("bqT", [2048, 128]); bkT_d = I("bkT", [2048, 128]); bvT_d = I("bvT", [2048, 128])
    wigq_d = I("wig_q", [2048, 8]); wigk_d = I("wig_k", [2048, 8]); wigv_d = I("wig_v", [2048, 8])
    bg_d = I("bg", [8, 1]); cw_d = I("cw", [128, 64]); cb_d = I("cb", [128, 16])
    nw_d = I("nw", [128, 16]); skip_d = I("skip", [128, 16])
    wdn_d = I("wdn", [2048, C])
    w1_d = I("w1", [C, 4096]); b1_d = I("b1", [128, 32])
    w2_d = I("w2", [4096, C]); b2_d = I("b2", [128, 8])
    poison_d = I("poison", [4, TF])
    mbigA_d = I("mbigA", [128, 2 * QBF], BF16)
    mbigB_d = I("mbigB", [128, 2 * QBF], BF16)
    cmask_d = I("cmask", [128, 1]); hm4_d = I("hm4", [4, 1]); hm4c_d = I("hm4c", [4, 1])
    ident_d = I("ident", [128, 128])
    y_d = nc.dram_tensor("y", [C, TC], F32, kind="ExternalOutput").ap()
    dbg = {}
    if DBG:
        for nm in ("dbg_x1", "dbg_x2", "dbg_x3"):
            dbg[nm] = nc.dram_tensor(nm, [C, TC], F32, kind="ExternalOutput").ap()
        dbg["dbg_gown"] = nc.dram_tensor("dbg_gown", [8, TC], F32, kind="ExternalOutput").ap()
        dbg["dbg_hn"] = nc.dram_tensor("dbg_hn", [2048, TC], BF16, kind="ExternalOutput").ap()
        dbg["dbg_gb"] = nc.dram_tensor("dbg_gb", [8, TC], F32, kind="ExternalOutput").ap()
        dbg["dbg_xa"] = nc.dram_tensor("dbg_xa", [2048, TC], BF16, kind="ExternalOutput").ap()
        dbg["dbg_kb"] = nc.dram_tensor("dbg_kb", [2048, TC], BF16, kind="ExternalOutput").ap()
        dbg["dbg_vb"] = nc.dram_tensor("dbg_vb", [2048, TC], BF16, kind="ExternalOutput").ap()
        dbg["dbg_qb"] = nc.dram_tensor("dbg_qb", [2048, TC], BF16, kind="ExternalOutput").ap()
        dbg["dbg_grow"] = nc.dram_tensor("dbg_grow", [4, TF], F32, kind="ExternalOutput").ap()

    D = lambda n, s, d=F32: nc.dram_tensor(n, s, d).ap()
    ysb_d = D("ysb", [2048, TC], BF16)
    pbb_d = D("pbb", [2048, TC], BF16)
    carry_c = D("carry_c", [128, 8]); carry_o = D("carry_o", [256, 8])
    tail_c = D("tail_c", [128, 24]); tail_o = D("tail_o", [256, 24])
    xmlb_d = D("xmlb", [2048, THL], BF16)
    kb_d = D("kb", [2048, 2048], BF16)
    vb_d = D("vb", [2048, 2048], BF16)
    ko_o = D("ko_o", [4096, 2048], BF16)
    vo_o = D("vo_o", [4096, 2048], BF16)
    gb_d = D("gb", [8, TC]); g_o = D("g_o", [16, TC])
    qb_d = D("qbd", [2048, TC], BF16)
    xab_d = D("xab", [2048, TC], BF16)
    zb_d = D("zb", [2048, TC], BF16)
    grow_d = D("grow", [4, TF])
    gown_d = D("gown", [8, TC])
    hnb_d = D("hnb", [2048, TC], BF16)
    w1bf_d = D("w1bf", [C, 4096], BF16)
    w2bf_d = D("w2bf", [4096, C], BF16)
    wdbf_d = D("wdbf", [2048, C], BF16)

    with contextlib.ExitStack() as es:
        pp = PB(nc, es)
        pp.init_hand_sems(es)
        xsb = es.enter_context(nc.sbuf_tensor("xsb", [128, 8 * TC], F32))
        kit = LNKit(nc, es, "ln", 8, TT)
        pa = es.enter_context(nc.psum_tensor("pa", [128, TT], F32))
        pb = es.enter_context(nc.psum_tensor("pb", [128, TT], F32))
        consts = make_consts(nc, es, pp, {"eps5": 1e-5, "eps6": 1e-6, "one": 1.0,
                                          "halfpi": HALF_PI, "m1e6": 1e-6})
        cmask = es.enter_context(nc.sbuf_tensor("cmask_s", [128, 1], F32))
        hm4 = es.enter_context(nc.sbuf_tensor("hm4_s", [4, 1], F32))
        hm4c = es.enter_context(nc.sbuf_tensor("hm4c_s", [4, 1], F32))
        ident = es.enter_context(nc.sbuf_tensor("ident_s", [128, 128], F32))
        nw_sb = es.enter_context(nc.sbuf_tensor("nw_s", [128, 16], F32))
        skip_sb = es.enter_context(nc.sbuf_tensor("skip_s", [128, 16], F32))
        halo_sb = es.enter_context(nc.sbuf_tensor("halo_s", [128, 24], F32))

        def sl_t(c, tt):
            return xsb[:, c * TC + tt * TT: c * TC + (tt + 1) * TT]

        def dump_stream(nm):
            if not DBG:
                return
            def dumper(e, inc, nm=nm):
                for c in range(8):
                    for tt in range(NT):
                        inc(e.dma_start(out=dbg[nm][c * 128:(c + 1) * 128,
                                                    tt * TT:(tt + 1) * TT],
                                        in_=sl_t(c, tt)), 16)
            pp.phase(gpsimd=dumper)

        def emit_stream_ln(out_bf=None, out_into=None, tt=0):
            def cast(v, inc, tt=tt):
                for c in range(8):
                    ins = v.tensor_copy(out=ch(kit.xbf, c, TT), in_=sl_t(c, tt))
                inc(ins)
            pp.phase(vector=cast)

            def mm_mu(t, inc):
                for c in range(8):
                    ins = t.matmul(pa[:, 0:TT], consts["ones_bf"], ch(kit.xbf, c, TT),
                                   start=(c == 0), stop=(c == 7))
                inc(ins)
            pp.phase(tensor=mm_mu)

            def sub(v, inc, tt=tt):
                for c in range(8):
                    ins = v.scalar_tensor_tensor(out=ch(kit.d, c, TT), in0=pa[:, 0:TT],
                                                 scalar=-1.0 / C, in1=sl_t(c, tt),
                                                 op0=ALU.mult, op1=ALU.add)
                inc(ins)
            pp.phase(vector=sub)

            def sq(s, inc):
                for c in range(8):
                    ins = s.activation(out=ch(kit.xbf, c, TT), in_=ch(kit.d, c, TT),
                                       func=AF.Square)
                inc(ins)
            pp.phase(scalar=sq)

            def mm_var(t, inc):
                for c in range(8):
                    ins = t.matmul(pb[:, 0:TT], consts["ones_bf"], ch(kit.xbf, c, TT),
                                   start=(c == 0), stop=(c == 7))
                inc(ins)
            pp.phase(tensor=mm_var)

            def rstd(s, inc):
                s.activation(out=kit.rstd_a[:], in_=pb[:, 0:TT], func=AF.Ln,
                             scale=1.0 / C, bias=consts["eps5"])
                ins = s.activation(out=kit.rstd[:], in_=kit.rstd_a[:], func=AF.Exp,
                                   scale=-0.5)
                inc(ins)
            pp.phase(scalar=rstd)

            def apply(v, inc):
                for c in range(8):
                    dst = ch(out_bf, c, TT) if out_bf is not None else out_into(c)
                    ins = v.tensor_tensor(out=dst, in0=ch(kit.d, c, TT),
                                          in1=kit.rstd[:], op=ALU.mult)
                inc(ins)
            pp.phase(vector=apply)

        def s0_load(e, inc):
            for c in range(8):
                inc(e.dma_start(out=xsb[:, c * TC:(c + 1) * TC],
                                in_=xs_d[c * 128:(c + 1) * 128, :]), 16)
            inc(e.dma_start(out=cmask[:], in_=cmask_d), 16)
            inc(e.dma_start(out=hm4[:], in_=hm4_d), 16)
            inc(e.dma_start(out=hm4c[:], in_=hm4c_d), 16)
            inc(e.dma_start(out=ident[:], in_=ident_d), 16)
            inc(e.dma_start(out=nw_sb[:], in_=nw_d), 16)
            inc(e.dma_start(out=skip_sb[:], in_=skip_d), 16)
        pp.phase(sync=s0_load)

        with contextlib.ExitStack() as sw:
            stage = sw.enter_context(nc.sbuf_tensor("sw_stage", [128, 2 * 4096], F32))
            wt = sw.enter_context(nc.sbuf_tensor("sw_wt", [128, 2 * 4096], BF16))
            pc_items = ([(w1_d, w1bf_d, kc, 4096) for kc in range(8)]
                        + [(w2_d, w2bf_d, kc, C) for kc in range(32)]
                        + [(wdn_d, wdbf_d, kc, C) for kc in range(16)])

            def pc_ld(e, i, s):
                src_d, dst_d, kc, cols = pc_items[i]
                return e.dma_start(out=stage[:, s * 4096: s * 4096 + cols],
                                   in_=src_d[kc * 128:(kc + 1) * 128, :])

            def pc_mm(v, i, s):
                src_d, dst_d, kc, cols = pc_items[i]
                return v.tensor_copy(out=wt[:, s * 4096: s * 4096 + cols],
                                     in_=stage[:, s * 4096: s * 4096 + cols])

            def pc_st(e, i, s):
                src_d, dst_d, kc, cols = pc_items[i]
                return e.dma_start(out=dst_d[kc * 128:(kc + 1) * 128, :],
                                   in_=wt[:, s * 4096: s * 4096 + cols])

            pp.pipe2(len(pc_items), pc_mm, ld=pc_ld, st=pc_st, mm_eng="vector")

        with contextlib.ExitStack() as s1:
            stage = s1.enter_context(nc.sbuf_tensor("s1_stage", [128, C], F32))
            wi_bf = s1.enter_context(nc.sbuf_tensor("s1_wi", [128, 8 * C], BF16))
            wr_bf = s1.enter_context(nc.sbuf_tensor("s1_wr", [128, 8 * C], BF16))
            bi_sb = s1.enter_context(nc.sbuf_tensor("s1_bi", [128, 8], F32))
            br_sb = s1.enter_context(nc.sbuf_tensor("s1_br", [128, 8], F32))
            lam_sb = s1.enter_context(nc.sbuf_tensor("s1_lam", [1, 1], F32))
            sp_t = s1.enter_context(nc.sbuf_tensor("s1_spt", [1, 1], F32))
            sp_t2 = s1.enter_context(nc.sbuf_tensor("s1_spt2", [1, 1], F32))
            s8_bc = s1.enter_context(nc.sbuf_tensor("s1_s8", [128, 1], F32))
            xin_bf = s1.enter_context(nc.sbuf_tensor("s1_xin", [128, 8 * TT], BF16))
            it_f = s1.enter_context(nc.sbuf_tensor("s1_it", [128, 8 * TT], F32))
            at_f = s1.enter_context(nc.sbuf_tensor("s1_at", [128, 8 * TT], F32))
            gx_f = s1.enter_context(nc.sbuf_tensor("s1_gx", [128, 8 * TT], F32))
            ysp_bf = s1.enter_context(nc.sbuf_tensor("s1_ysbf", [128, 8 * TT], BF16))
            prp_bf = s1.enter_context(nc.sbuf_tensor("s1_prbf", [128, 8 * TT], BF16))
            ones_t = s1.enter_context(nc.sbuf_tensor("s1_ones", [128, TT], F32))
            ys_last = s1.enter_context(nc.sbuf_tensor("s1_ysl", [128, 8], F32))
            pr_last = s1.enter_context(nc.sbuf_tensor("s1_prl", [128, 8], F32))
            t1 = s1.enter_context(nc.sbuf_tensor("s1_t1", [128, TT], F32))
            pd = s1.enter_context(nc.psum_tensor("s1_pd", [128, 1], F32))

            for w_d, w_bf in ((wiT_d, wi_bf), (wrT_d, wr_bf)):
                for kc in range(8):
                    def lw(e, inc, w_d=w_d, kc=kc):
                        inc(e.dma_start(out=stage[:], in_=w_d[kc * 128:(kc + 1) * 128, :]), 16)
                    pp.phase(sync=lw)

                    def cw_(v, inc, w_bf=w_bf, kc=kc):
                        ins = v.tensor_copy(out=w_bf[:, kc * C:(kc + 1) * C], in_=stage[:])
                        inc(ins)
                    pp.phase(vector=cw_)

            def s1_small(e, inc):
                inc(e.dma_start(out=bi_sb[:], in_=bi_d), 16)
                inc(e.dma_start(out=br_sb[:], in_=br_d), 16)
                inc(e.dma_start(out=lam_sb[:], in_=lam_d), 16)
            pp.phase(sync=s1_small)

            def sp1(s, inc):
                inc(s.activation(out=sp_t[:], in_=lam_sb[:], func=AF.Abs), wait=True)
                inc(s.activation(out=sp_t[:], in_=sp_t[:], func=AF.Exp, scale=-1.0), wait=True)
                inc(s.activation(out=sp_t[:], in_=sp_t[:], func=AF.Ln,
                                 bias=consts["one"][0:1, :]), wait=True)
                inc(s.activation(out=sp_t2[:], in_=lam_sb[:], func=AF.Relu))
            pp.phase(scalar=sp1)

            def sp2(v, inc):
                inc(v.tensor_tensor(out=sp_t2[:], in0=sp_t2[:], in1=sp_t[:], op=ALU.add), wait=True)
                inc(v.tensor_scalar(out=sp_t2[:], in0=sp_t2[:], scalar1=-8.0,
                                    scalar2=None, op0=ALU.mult), wait=True)
                inc(v.memset(ones_t[:], 1.0))
            pp.phase(vector=sp2)

            def sp3(t, inc):
                ins = t.matmul(pd[:, 0:1], consts["ones_row"], sp_t2[:], start=True, stop=True)
                inc(ins)
            pp.phase(tensor=sp3)

            def sp4(v, inc):
                ins = v.tensor_copy(out=s8_bc[:], in_=pd[:, 0:1])
                inc(ins)
            pp.phase(vector=sp4)

            for tt in range(NT):
                emit_stream_ln(out_bf=xin_bf, tt=tt)

                def mmi(t, i, s):
                    ps = pa if s == 0 else pb
                    for kc in range(8):
                        ins = t.matmul(ps[:, 0:TT],
                                       wi_bf[:, kc * C + i * 128: kc * C + (i + 1) * 128],
                                       ch(xin_bf, kc, TT), start=(kc == 0), stop=(kc == 7),
                                       skip_group_check=True)
                    return ins

                def evi(sc, i, s):
                    ps = pa if s == 0 else pb
                    return sc.activation(out=ch(it_f, i, TT), in_=ps[:, 0:TT],
                                         func=AF.Sigmoid, bias=bi_sb[:, i:i + 1])

                pp.pipe2(8, mmi, ev=evi, ev_eng="scalar")

                def mmr(t, i, s):
                    ps = pa if s == 0 else pb
                    for kc in range(8):
                        ins = t.matmul(ps[:, 0:TT],
                                       wr_bf[:, kc * C + i * 128: kc * C + (i + 1) * 128],
                                       ch(xin_bf, kc, TT), start=(kc == 0), stop=(kc == 7),
                                       skip_group_check=True)
                    return ins

                def evr(sc, i, s):
                    ps = pa if s == 0 else pb
                    pp.chain(sc, sc.activation(out=t1[:], in_=ps[:, 0:TT],
                                               func=AF.Sigmoid, bias=br_sb[:, i:i + 1]))
                    pp.chain(sc, sc.activation(out=ch(at_f, i, TT), in_=t1[:],
                                               func=AF.Exp, scale=s8_bc[:]))
                    pp.chain(sc, sc.activation(out=t1[:], in_=ch(at_f, i, TT), func=AF.Square))
                    pp.chain(sc, sc.activation(out=t1[:], in_=t1[:], func=AF.Ln, scale=-1.0,
                                               bias=consts["one"]))
                    return sc.activation(out=ch(gx_f, i, TT), in_=t1[:], func=AF.Exp, scale=0.5)

                pp.pipe2(8, mmr, ev=evr, ev_eng="scalar")

                def gg_all(v, inc):
                    for dc in range(8):
                        inc(v.tensor_tensor(out=ch(gx_f, dc, TT), in0=ch(gx_f, dc, TT),
                                            in1=ch(it_f, dc, TT), op=ALU.mult), wait=True)
                        inc(v.tensor_tensor(out=ch(gx_f, dc, TT), in0=ch(gx_f, dc, TT),
                                            in1=ch(xin_bf, dc, TT), op=ALU.mult), wait=True)
                    inc(v.nop())
                pp.phase(vector=gg_all)

                def scans(v, inc, tt=tt):
                    for c in range(8):
                        init_y = 0.0 if tt == 0 else ys_last[:, c:c + 1]
                        init_p = 1.0 if tt == 0 else pr_last[:, c:c + 1]
                        inc(v.tensor_tensor_scan(out=ch(ysp_bf, c, TT), data0=ch(at_f, c, TT),
                                                 data1=ch(gx_f, c, TT), initial=init_y,
                                                 op0=ALU.mult, op1=ALU.add), wait=True)
                        inc(v.tensor_tensor_scan(out=ch(prp_bf, c, TT), data0=ch(at_f, c, TT),
                                                 data1=ones_t[:], initial=init_p,
                                                 op0=ALU.mult, op1=ALU.mult), wait=True)
                        inc(v.tensor_copy(out=ys_last[:, c:c + 1],
                                          in_=ysp_bf[:, c * TT + TT - 1: c * TT + TT]), wait=True)
                        inc(v.tensor_copy(out=pr_last[:, c:c + 1],
                                          in_=prp_bf[:, c * TT + TT - 1: c * TT + TT]), wait=True)
                pp.phase(vector=scans)

                def st_ys(e, inc, tt=tt):
                    store_fm(e, inc, ysb_d, ysp_bf, 8, TT, tt * TT)
                    store_fm(e, inc, pbb_d, prp_bf, 8, TT, tt * TT)
                pp.phase(gpsimd=st_ys)

            def st_carry(e, inc):
                inc(e.dma_start(out=carry_c, in_=ys_last[:]), 16)
            pp.phase(gpsimd=st_carry)

            gp_now_carry = pp.cnt["gpsimd"]

            def cc_carry(g, inc):
                g.wait_ge(pp.sems["gpsimd"], gp_now_carry)
                ins = g.collective_compute("AllGather", ALU.bypass,
                                           replica_groups=[[0, 1], [2, 3], [4, 5], [6, 7]],
                                           ins=[carry_c.opt()], outs=[carry_o.opt()])
                inc(ins, 1)
            pp.phase(gpsimd=cc_carry)

            def ld_carry(e, inc):
                inc(e.dma_start(out=ys_last[:], in_=carry_o[0:128, :]), 16)
            pp.phase(sync=ld_carry)

            def carry_eff(v, inc):
                ins = v.tensor_scalar(out=ys_last[:], in0=ys_last[:], scalar1=cmask[:],
                                      scalar2=None, op0=ALU.mult)
                inc(ins)
            pp.phase(vector=carry_eff)

            for tt in range(NT):
                def ld_ys(e, inc, tt=tt):
                    load_fm(e, inc, ysp_bf, ysb_d, 8, TT, tt * TT)
                    load_fm(e, inc, prp_bf, pbb_d, 8, TT, tt * TT)
                pp.phase(sync=ld_ys)

                def fix(v, inc):
                    for c in range(8):
                        ins = v.scalar_tensor_tensor(out=ch(gx_f, c, TT), in0=ch(prp_bf, c, TT),
                                                     scalar=ys_last[:, c:c + 1],
                                                     in1=ch(ysp_bf, c, TT),
                                                     op0=ALU.mult, op1=ALU.add)
                    inc(ins)
                pp.phase(vector=fix)

                emit_ln(pp, kit, consts, gx_f, pa, pb, out_f32=gx_f)

                def addres(v, inc, tt=tt):
                    for c in range(8):
                        ins = v.tensor_tensor(out=sl_t(c, tt), in0=sl_t(c, tt),
                                              in1=ch(gx_f, c, TT), op=ALU.add)
                    inc(ins)
                pp.phase(vector=addres)

        dump_stream("dbg_x1")

        with contextlib.ExitStack() as s3:
            stage_a = s3.enter_context(nc.sbuf_tensor("s3_sa", [128, C], F32))
            stage_b = s3.enter_context(nc.sbuf_tensor("s3_sb", [128, C], F32))
            cos_bf = s3.enter_context(nc.sbuf_tensor("s3_cos", [128, 8 * C], BF16))
            sin_bf = s3.enter_context(nc.sbuf_tensor("s3_sin", [128, 8 * C], BF16))
            fin_bf = s3.enter_context(nc.sbuf_tensor("s3_fin", [128, 8 * TT], BF16))
            r_f = s3.enter_context(nc.sbuf_tensor("s3_rf", [128, 8 * TT], F32))
            sr8 = s3.enter_context(nc.sbuf_tensor("s3_sr", [128, 8 * TT], F32))
            si8 = s3.enter_context(nc.sbuf_tensor("s3_si", [128, 8 * TT], F32))
            pc3 = s3.enter_context(nc.psum_tensor("s3_pc", [128, TT], F32))
            pd3 = s3.enter_context(nc.psum_tensor("s3_pd", [128, TT], F32))

            for (wa, wb, out_bf, bias) in ((th_d, thn_d, cos_bf, "halfpi"),
                                           (ph_d, phn_d, sin_bf, None)):
                for kc in range(8):
                    def load(e, inc, wa=wa, wb=wb, kc=kc):
                        inc(e.dma_start(out=stage_a[:], in_=wa[kc * 128:(kc + 1) * 128, :]), 16)
                        inc(e.dma_start(out=stage_b[:], in_=wb[kc * 128:(kc + 1) * 128, :]), 16)
                    pp.phase(sync=load)

                    def add(v, inc):
                        ins = v.tensor_tensor(out=stage_a[:], in0=stage_a[:], in1=stage_b[:],
                                              op=ALU.add)
                        inc(ins)
                    pp.phase(vector=add)

                    def trig(s, inc, out_bf=out_bf, bias=bias, kc=kc):
                        if bias is None:
                            ins = s.activation(out=out_bf[:, kc * C:(kc + 1) * C],
                                               in_=stage_a[:], func=AF.Sin)
                        else:
                            ins = s.activation(out=out_bf[:, kc * C:(kc + 1) * C],
                                               in_=stage_a[:], func=AF.Sin,
                                               bias=consts[bias])
                        inc(ins)
                    pp.phase(scalar=trig)

            for tt in range(NT):
                emit_stream_ln(out_bf=fin_bf, tt=tt)

                def mm3(t, i, s):
                    psr = pa if s == 0 else pc3
                    psi = pb if s == 0 else pd3
                    for kc in range(8):
                        t.matmul(psr[:, 0:TT],
                                 cos_bf[:, kc * C + i * 128: kc * C + (i + 1) * 128],
                                 ch(fin_bf, kc, TT), start=(kc == 0), stop=(kc == 7),
                                 skip_group_check=True)
                    for kc in range(8):
                        ins = t.matmul(psi[:, 0:TT],
                                       sin_bf[:, kc * C + i * 128: kc * C + (i + 1) * 128],
                                       ch(fin_bf, kc, TT), start=(kc == 0), stop=(kc == 7),
                                       skip_group_check=True)
                    return ins

                def ev3(sc, i, s):
                    psr = pa if s == 0 else pc3
                    psi = pb if s == 0 else pd3
                    sc.activation(out=ch(sr8, i, TT), in_=psr[:, 0:TT], func=AF.Square)
                    return sc.activation(out=ch(si8, i, TT), in_=psi[:, 0:TT], func=AF.Square)

                pp.pipe2(8, mm3, ev=ev3, ev_eng="scalar")

                def add3(v, inc):
                    for dc in range(8):
                        ins = v.tensor_tensor(out=ch(sr8, dc, TT), in0=ch(sr8, dc, TT),
                                              in1=ch(si8, dc, TT), op=ALU.add)
                    inc(ins)
                pp.phase(vector=add3)

                def root3(s, inc):
                    for dc in range(8):
                        inc(s.activation(out=ch(si8, dc, TT), in_=ch(sr8, dc, TT),
                                         func=AF.Ln), wait=True)
                        inc(s.activation(out=ch(r_f, dc, TT), in_=ch(si8, dc, TT),
                                         func=AF.Exp, scale=0.5), wait=True)
                    inc(s.nop())
                pp.phase(scalar=root3)

                emit_ln(pp, kit, consts, r_f, pa, pb, out_f32=r_f)

                def addres2(v, inc, tt=tt):
                    for c in range(8):
                        ins = v.tensor_tensor(out=sl_t(c, tt), in0=sl_t(c, tt),
                                              in1=ch(r_f, c, TT), op=ALU.add)
                    inc(ins)
                pp.phase(vector=addres2)

        dump_stream("dbg_x2")

        with contextlib.ExitStack() as s4:
            def st_tail(e, inc):
                for c in range(8):
                    inc(e.dma_start(out=tail_c[:, c * 3:(c + 1) * 3],
                                    in_=xsb[:, (c + 1) * TC - 3:(c + 1) * TC]), 16)
            pp.phase(gpsimd=st_tail)

            gp_now_tail = pp.cnt["gpsimd"]

            def cc_tail(g, inc):
                g.wait_ge(pp.sems["gpsimd"], gp_now_tail)
                ins = g.collective_compute("AllGather", ALU.bypass,
                                           replica_groups=[[0, 1], [2, 3], [4, 5], [6, 7]],
                                           ins=[tail_c.opt()], outs=[tail_o.opt()])
                inc(ins, 1)
            pp.phase(gpsimd=cc_tail)

            def ld_halo(e, inc):
                inc(e.dma_start(out=halo_sb[:], in_=tail_o[0:128, :]), 16)
            pp.phase(sync=ld_halo)

            def msk_halo(v, inc):
                ins = v.tensor_scalar(out=halo_sb[:], in0=halo_sb[:], scalar1=cmask[:],
                                      scalar2=None, op0=ALU.mult)
                inc(ins)
            pp.phase(vector=msk_halo)

            stage = s4.enter_context(nc.sbuf_tensor("s4_stage", [128, 2048], F32))
            wh_bf = s4.enter_context(nc.sbuf_tensor("s4_wh", [128, 8 * 2048], BF16))
            fin_bf = s4.enter_context(nc.sbuf_tensor("s4_fin", [128, 8 * TC], BF16))
            finh_bf = s4.enter_context(nc.sbuf_tensor("s4_finh", [128, 24], BF16))
            out_bf = s4.enter_context(nc.sbuf_tensor("s4_out", [128, 2 * TT], BF16))
            outh_bf = s4.enter_context(nc.sbuf_tensor("s4_outh", [128, 3], BF16))
            hh1 = s4.enter_context(nc.sbuf_tensor("s4_hh1", [128, 24], F32))
            hh2 = s4.enter_context(nc.sbuf_tensor("s4_hh2", [128, 24], BF16))
            hrs = s4.enter_context(nc.sbuf_tensor("s4_hrs", [128, 3], F32))

            for tt in range(NT):
                emit_stream_ln(out_into=lambda c, tt=tt: fin_bf[:, c * TC + tt * TT:
                                                                c * TC + (tt + 1) * TT], tt=tt)

            def hmu(t, inc):
                for c in range(8):
                    ins = t.matmul(pa[:, 0:3], consts["ones_bf"],
                                   halo_sb.rearrange("p (c w) -> p c w", w=3)[:, c, :]
                                   if False else halo_sb[:, c * 3:(c + 1) * 3],
                                   start=(c == 0), stop=(c == 7))
                inc(ins)
            def hcast(v, inc):
                ins = v.tensor_copy(out=hh2[:], in_=halo_sb[:])
                inc(ins)
            pp.phase(vector=hcast)

            def hmu2(t, inc):
                for c in range(8):
                    ins = t.matmul(pa[:, 0:3], consts["ones_bf"], hh2[:, c * 3:(c + 1) * 3],
                                   start=(c == 0), stop=(c == 7))
                inc(ins)
            pp.phase(tensor=hmu2)

            def hsub(v, inc):
                for c in range(8):
                    ins = v.scalar_tensor_tensor(out=hh1[:, c * 3:(c + 1) * 3], in0=pa[:, 0:3],
                                                 scalar=-1.0 / C, in1=halo_sb[:, c * 3:(c + 1) * 3],
                                                 op0=ALU.mult, op1=ALU.add)
                inc(ins)
            pp.phase(vector=hsub)

            def hsq(s, inc):
                ins = s.activation(out=hh2[:], in_=hh1[:], func=AF.Square)
                inc(ins)
            pp.phase(scalar=hsq)

            def hvar(t, inc):
                for c in range(8):
                    ins = t.matmul(pb[:, 0:3], consts["ones_bf"], hh2[:, c * 3:(c + 1) * 3],
                                   start=(c == 0), stop=(c == 7))
                inc(ins)
            pp.phase(tensor=hvar)

            def hrstd(s, inc):
                inc(s.activation(out=hrs[:], in_=pb[:, 0:3], func=AF.Ln,
                                 scale=1.0 / C, bias=consts["eps5"]), wait=True)
                inc(s.activation(out=hrs[:], in_=hrs[:], func=AF.Exp, scale=-0.5))
            pp.phase(scalar=hrstd)

            def happ(v, inc):
                for c in range(8):
                    ins = v.tensor_tensor(out=finh_bf[:, c * 3:(c + 1) * 3],
                                          in0=hh1[:, c * 3:(c + 1) * 3], in1=hrs[:],
                                          op=ALU.mult)
                inc(ins)
            pp.phase(vector=happ)

            for half, (dram_out, wcol0) in enumerate(((zb_d, 2048), (xmlb_d, 0))):
                for kc in range(8):
                    def loadw(e, inc, kc=kc, wcol0=wcol0):
                        inc(e.dma_start(out=stage[:],
                                        in_=wup_d[kc * 128:(kc + 1) * 128, wcol0:wcol0 + 2048]), 16)
                    pp.phase(sync=loadw)

                    def castw(v, inc, kc=kc):
                        ins = v.tensor_copy(out=wh_bf[:, kc * 2048:(kc + 1) * 2048], in_=stage[:])
                        inc(ins)
                    pp.phase(vector=castw)

                items = [(tt, dc) for tt in range(NT) for dc in range(16)]

                def mmu_(t, i, s, items=items):
                    tt, dc = items[i]
                    ps = pa if s == 0 else pb
                    for kc in range(8):
                        ins = t.matmul(
                            ps[:, 0:TT],
                            wh_bf[:, kc * 2048 + dc * 128: kc * 2048 + (dc + 1) * 128],
                            fin_bf[:, kc * TC + tt * TT: kc * TC + (tt + 1) * TT],
                            start=(kc == 0), stop=(kc == 7), skip_group_check=True)
                    return ins

                def evu_(v, i, s):
                    ps = pa if s == 0 else pb
                    return v.tensor_copy(out=out_bf[:, s * TT:(s + 1) * TT], in_=ps[:, 0:TT])

                def stu_(e, i, s, items=items, dram_out=dram_out, half=half):
                    tt, dc = items[i]
                    off = tt * TT if half == 0 else 3 + tt * TT
                    return e.dma_start(out=dram_out[dc * 128:(dc + 1) * 128, off: off + TT],
                                       in_=out_bf[:, s * TT:(s + 1) * TT])

                pp.pipe2(64, mmu_, ev=evu_, st=stu_)

                if half == 1:
                    for dc in range(16):
                        def mmh(t, inc, dc=dc):
                            for kc in range(8):
                                ins = t.matmul(pa[:, 0:3],
                                               wh_bf[:, kc * 2048 + dc * 128: kc * 2048 + (dc + 1) * 128],
                                               finh_bf[:, kc * 3:(kc + 1) * 3],
                                               start=(kc == 0), stop=(kc == 7))
                            inc(ins)
                        pp.phase(tensor=mmh)

                        def evh(v, inc):
                            ins = v.tensor_copy(out=outh_bf[:], in_=pa[:, 0:3])
                            inc(ins)
                        pp.phase(vector=evh)

                        def sth(e, inc, dc=dc):
                            inc(e.dma_start(out=xmlb_d[dc * 128:(dc + 1) * 128, 0:3],
                                            in_=outh_bf[:]), 16)
                        pp.phase(gpsimd=sth)

        for _cc in ("cc1", "cc2", "cc3"):
            pp.hc[_cc] += 1
        cc1v, cc2v, cc3v = pp.hc["cc1"], pp.hc["cc2"], pp.hc["cc3"]
        with contextlib.ExitStack() as s5:
            bq_bf = s5.enter_context(nc.sbuf_tensor("s5_bq", [128, 16 * 128], BF16))
            bk_bf = s5.enter_context(nc.sbuf_tensor("s5_bk", [128, 16 * 128], BF16))
            bv_bf = s5.enter_context(nc.sbuf_tensor("s5_bv", [128, 16 * 128], BF16))
            stage2 = s5.enter_context(nc.sbuf_tensor("s5_st2", [128, 16 * 128], F32))
            stage3 = s5.enter_context(nc.sbuf_tensor("s5_st3", [128, 16 * 128], F32))
            wiga = s5.enter_context(nc.sbuf_tensor("s5_wiga", [128, 16 * 8], F32))
            wigb = s5.enter_context(nc.sbuf_tensor("s5_wigb", [128, 16 * 8], F32))
            a_act = s5.enter_context(nc.sbuf_tensor("s5_aact", [128, 16 * 8], BF16))
            a_ml = s5.enter_context(nc.sbuf_tensor("s5_aml", [128, 16 * 8], BF16))
            bg_sb = s5.enter_context(nc.sbuf_tensor("s5_bg", [8, 1], F32))
            cw_sb = s5.enter_context(nc.sbuf_tensor("s5_cw", [128, 64], F32))
            cb_sb = s5.enter_context(nc.sbuf_tensor("s5_cb", [128, 16], F32))
            xml_c = s5.enter_context(nc.sbuf_tensor("s5_xmlc", [128, THL], BF16))
            cv = s5.enter_context(nc.sbuf_tensor("s5_cv", [128, TC], F32))
            xact_c = s5.enter_context(nc.sbuf_tensor("s5_xac", [128, TC], BF16))
            ev_bf = s5.enter_context(nc.sbuf_tensor("s5_ev", [128, 2 * TT], BF16))
            g_f = s5.enter_context(nc.sbuf_tensor("s5_gf", [8, TC], F32))
            pg = s5.enter_context(nc.psum_tensor("s5_pg", [8, TC], F32))
            p128 = s5.enter_context(nc.psum_tensor("s5_p128", [128, 128], F32))

            def load_small(e, inc):
                inc(e.dma_start(out=cw_sb[:], in_=cw_d), 16)
                inc(e.dma_start(out=cb_sb[:], in_=cb_d), 16)
                inc(e.dma_start(out=bg_sb[:], in_=bg_d), 16)
            pp.phase(sync=load_small)

            for src_d, dst in ((bq_d, bq_bf), (bk_d, bk_bf), (bv_d, bv_bf)):
                def loadw(e, inc, src_d=src_d):
                    for c in range(16):
                        inc(e.dma_start(out=stage2[:, c * 128:(c + 1) * 128],
                                        in_=src_d[c * 128:(c + 1) * 128, :]), 16)
                pp.phase(sync=loadw)

                def castw(v, inc, dst=dst):
                    ins = v.tensor_copy(out=dst[:], in_=stage2[:])
                    inc(ins)
                pp.phase(vector=castw)

            def load_fold1(e, inc):
                for c in range(16):
                    inc(e.dma_start(out=stage2[:, c * 128:(c + 1) * 128],
                                    in_=bqT_d[c * 128:(c + 1) * 128, :]), 16)
                    inc(e.dma_start(out=stage3[:, c * 128:(c + 1) * 128],
                                    in_=bkT_d[c * 128:(c + 1) * 128, :]), 16)
                    inc(e.dma_start(out=wiga[:, c * 8:(c + 1) * 8],
                                    in_=wigq_d[c * 128:(c + 1) * 128, :]), 16)
                    inc(e.dma_start(out=wigb[:, c * 8:(c + 1) * 8],
                                    in_=wigk_d[c * 128:(c + 1) * 128, :]), 16)
            pp.phase(sync=load_fold1)

            def fold1(t, inc):
                for c in range(16):
                    t.matmul(p128[:, c * 8:(c + 1) * 8], stage2[:, c * 128:(c + 1) * 128],
                             wiga[:, c * 8:(c + 1) * 8], start=True, stop=False)
                    ins = t.matmul(p128[:, c * 8:(c + 1) * 8], stage3[:, c * 128:(c + 1) * 128],
                                   wigb[:, c * 8:(c + 1) * 8], start=False, stop=True)
                inc(ins)
            pp.phase(tensor=fold1)

            def evf1(v, inc):
                ins = v.tensor_copy(out=a_act[:], in_=p128[:, 0:128])
                inc(ins)
            pp.phase(vector=evf1)

            def load_fold2(e, inc):
                for c in range(16):
                    inc(e.dma_start(out=stage2[:, c * 128:(c + 1) * 128],
                                    in_=bvT_d[c * 128:(c + 1) * 128, :]), 16)
                    inc(e.dma_start(out=wiga[:, c * 8:(c + 1) * 8],
                                    in_=wigv_d[c * 128:(c + 1) * 128, :]), 16)
            pp.phase(sync=load_fold2)

            def fold2(t, inc):
                for c in range(16):
                    ins = t.matmul(p128[:, c * 8:(c + 1) * 8], stage2[:, c * 128:(c + 1) * 128],
                                   wiga[:, c * 8:(c + 1) * 8], start=True, stop=True)
                inc(ins)
            pp.phase(tensor=fold2)

            def evf2(v, inc):
                ins = v.tensor_copy(out=a_ml[:], in_=p128[:, 0:128])
                inc(ins)
            pp.phase(vector=evf2)

            for c in range(16):
                def ld_xml(e, inc, c=c):
                    inc(e.dma_start(out=xml_c[:], in_=xmlb_d[c * 128:(c + 1) * 128, :]), 16)
                pp.phase(sync=ld_xml)

                def conv(v, inc, c=c):
                    v.tensor_scalar(out=cv[:], in0=xml_c[:, 0:TC],
                                    scalar1=cw_sb[:, c * 4:c * 4 + 1], scalar2=None, op0=ALU.mult)
                    for i in (1, 2, 3):
                        ins = v.scalar_tensor_tensor(out=cv[:], in0=xml_c[:, i:i + TC],
                                                     scalar=cw_sb[:, c * 4 + i:c * 4 + i + 1],
                                                     in1=cv[:], op0=ALU.mult, op1=ALU.add)
                    inc(ins)
                pp.phase(vector=conv)

                def act(s, inc, c=c):
                    ins = s.activation(out=xact_c[:], in_=cv[:], func=AF.Silu,
                                       bias=cb_sb[:, c:c + 1])
                    inc(ins)
                pp.phase(scalar=act)

                def st_xa(e, inc, c=c):
                    inc(e.dma_start(out=xab_d[c * 128:(c + 1) * 128, :], in_=xact_c[:]), 16)
                pp.phase(gpsimd=st_xa)

                qk_items = [(w, tt) for w in range(2) for tt in range(NT)]

                def mmqk(t, i, s, c=c):
                    w, tt = qk_items[i]
                    w_bf = bq_bf if w == 0 else bk_bf
                    ps = pa if s == 0 else pb
                    return t.matmul(ps[:, 0:TT], w_bf[:, c * 128:(c + 1) * 128],
                                    xact_c[:, tt * TT:(tt + 1) * TT],
                                    start=True, stop=True, skip_group_check=True)

                def evqk(v, i, s):
                    ps = pa if s == 0 else pb
                    return v.tensor_copy(out=ev_bf[:, s * TT:(s + 1) * TT], in_=ps[:, 0:TT])

                def stqk(e, i, s, c=c):
                    w, tt = qk_items[i]
                    dram_out = qb_d if w == 0 else kb_d
                    return e.dma_start(out=dram_out[c * 128:(c + 1) * 128,
                                                    tt * TT:(tt + 1) * TT],
                                       in_=ev_bf[:, s * TT:(s + 1) * TT])

                pp.pipe2(8, mmqk, ev=evqk, st=stqk)

                def mmg(t, inc, c=c):
                    for tt in range(NT):
                        t.matmul(pg[:, tt * TT:(tt + 1) * TT], a_act[:, c * 8:(c + 1) * 8],
                                 xact_c[:, tt * TT:(tt + 1) * TT],
                                 start=(c == 0), stop=False, skip_group_check=True)
                        ins = t.matmul(pg[:, tt * TT:(tt + 1) * TT], a_ml[:, c * 8:(c + 1) * 8],
                                       xml_c[:, 3 + tt * TT: 3 + (tt + 1) * TT],
                                       start=False, stop=(c == 15), skip_group_check=True)
                    inc(ins)
                pp.phase(tensor=mmg)

            def evg(v, inc):
                ins = v.tensor_scalar(out=g_f[:], in0=pg[:, 0:TC], scalar1=bg_sb[:],
                                      scalar2=None, op0=ALU.add)
                inc(ins)
            pp.phase(vector=evg)

            def st_g(e, inc):
                inc(e.dma_start(out=gb_d, in_=g_f[:]), 16)
            pp.phase(gpsimd=st_g)

            for tau in range(16):
                def ld_tau(e, inc, tau=tau):
                    for c in range(16):
                        inc(e.dma_start(out=xml_c[:, c * 128:(c + 1) * 128],
                                        in_=xmlb_d[c * 128:(c + 1) * 128,
                                                   3 + tau * 128: 3 + (tau + 1) * 128]), 16)
                pp.phase(sync=ld_tau)

                def mmv_(t, i, s, tau=tau):
                    for j in range(4):
                        c = 4 * i + j
                        ps = pa if s == 0 else pb
                        ins = t.matmul(ps[:, j * 128:(j + 1) * 128],
                                       xml_c[:, c * 128:(c + 1) * 128],
                                       bv_bf[:, c * 128:(c + 1) * 128],
                                       start=True, stop=True, skip_group_check=True)
                    return ins

                def evv_(v, i, s):
                    ps = pa if s == 0 else pb
                    return v.tensor_copy(out=ev_bf[:, s * TT:(s + 1) * TT], in_=ps[:, 0:TT])

                def stv_(e, i, s, tau=tau):
                    return e.dma_start(out=vb_d[tau * 128:(tau + 1) * 128,
                                                i * 512:(i + 1) * 512],
                                       in_=ev_bf[:, s * TT:(s + 1) * TT])

                pp.pipe2(4, mmv_, ev=evv_, st=stv_)

            gp_now_v = pp.cnt["gpsimd"]

            def cc_v(g, inc):
                g.wait_ge(pp.sems["gpsimd"], gp_now_v)
                g.collective_compute("AllGather", ALU.bypass,
                                     replica_groups=[[0, 1], [2, 3], [4, 5], [6, 7]],
                                     ins=[vb_d.opt()], outs=[vo_o.opt()]).then_inc(pp.h["cc1"], 1)
                inc(g.nop(), 1)
            pp.phase(gpsimd=cc_v)

            gp_now_kg = pp.cnt["gpsimd"]

            def cc_kg(g, inc):
                g.wait_ge(pp.sems["gpsimd"], gp_now_kg)
                g.collective_compute("AllGather", ALU.bypass,
                                     replica_groups=[[0, 1], [2, 3], [4, 5], [6, 7]],
                                     ins=[gb_d.opt()], outs=[g_o.opt()]).then_inc(pp.h["cc2"], 1)
                g.collective_compute("AllGather", ALU.bypass,
                                     replica_groups=[[0, 1], [2, 3], [4, 5], [6, 7]],
                                     ins=[kb_d.opt()], outs=[ko_o.opt()]).then_inc(pp.h["cc3"], 1)
                inc(g.nop(), 1)
            pp.phase(gpsimd=cc_kg)

        with contextlib.ExitStack() as s6:
            igf = s6.enter_context(nc.sbuf_tensor("s6_ig", [4, TF], F32))
            fgf = s6.enter_context(nc.sbuf_tensor("s6_fg", [4, TF], F32))
            Ff = s6.enter_context(nc.sbuf_tensor("s6_F", [4, TF], F32))
            pois = s6.enter_context(nc.sbuf_tensor("s6_po", [4, TF], F32))
            own_a = s6.enter_context(nc.sbuf_tensor("s6_oa", [4, TC], F32))
            own_b = s6.enter_context(nc.sbuf_tensor("s6_ob", [4, TC], F32))
            own_c = s6.enter_context(nc.sbuf_tensor("s6_oc", [4, TC], F32))

            def ld_g(e, inc):
                e.wait_ge(pp.h["cc2"], cc2v)
                inc(e.dma_start(out=igf[:, 0:TC], in_=g_o[0:4, :]), 16)
                inc(e.dma_start(out=igf[:, TC:TF], in_=gb_d[0:4, :]), 16)
                inc(e.dma_start(out=fgf[:, 0:TC], in_=g_o[4:8, :]), 16)
                inc(e.dma_start(out=fgf[:, TC:TF], in_=gb_d[4:8, :]), 16)
            pp.phase(sync=ld_g)

            def lf1(s, inc):
                inc(s.activation(out=Ff[:], in_=fgf[:], func=AF.Abs), wait=True)
                inc(s.activation(out=Ff[:], in_=Ff[:], func=AF.Exp, scale=-1.0), wait=True)
                inc(s.activation(out=Ff[:], in_=Ff[:], func=AF.Ln,
                                 bias=consts["one"][0:4, :]), wait=True)
                inc(s.activation(out=own_a[:], in_=fgf[:, 0:TC], func=AF.Relu), wait=True)
                inc(s.activation(out=own_b[:], in_=fgf[:, TC:TF], func=AF.Relu))
            pp.phase(scalar=lf1)

            def lf2(v, inc):
                inc(v.tensor_tensor(out=fgf[:, 0:TC], in0=fgf[:, 0:TC], in1=own_a[:],
                                    op=ALU.subtract), wait=True)
                inc(v.tensor_tensor(out=fgf[:, TC:TF], in0=fgf[:, TC:TF], in1=own_b[:],
                                    op=ALU.subtract), wait=True)
                inc(v.tensor_tensor(out=fgf[:], in0=fgf[:], in1=Ff[:], op=ALU.subtract), wait=True)
                inc(v.memset(pois[:], 1.0), wait=True)
                inc(v.tensor_tensor_scan(out=Ff[:], data0=pois[:], data1=fgf[:],
                                         initial=0.0, op0=ALU.mult, op1=ALU.add), wait=True)
                inc(v.tensor_tensor(out=igf[:], in0=igf[:], in1=Ff[:], op=ALU.subtract))
            pp.phase(vector=lf2)

            def ld_pois(e, inc):
                inc(e.dma_start(out=pois[:], in_=poison_d), 16)
            pp.phase(sync=ld_pois)

            def lf3(v, inc):
                inc(v.tensor_tensor(out=igf[:], in0=igf[:], in1=pois[:], op=ALU.add), wait=True)
                inc(v.tensor_tensor_scan(out=fgf[:], data0=igf[:], data1=igf[:],
                                         initial=NEG_BIG, op0=ALU.max, op1=ALU.max), wait=True)
                inc(v.tensor_scalar(out=own_a[:], in0=Ff[:, 0:TC], scalar1=hm4c[:],
                                    scalar2=None, op0=ALU.mult), wait=True)
                inc(v.scalar_tensor_tensor(out=own_a[:], in0=Ff[:, TC:TF], scalar=hm4[:],
                                           in1=own_a[:], op0=ALU.mult, op1=ALU.add), wait=True)
                inc(v.tensor_scalar(out=own_b[:], in0=fgf[:, 0:TC], scalar1=hm4c[:],
                                    scalar2=None, op0=ALU.mult), wait=True)
                inc(v.scalar_tensor_tensor(out=own_b[:], in0=fgf[:, TC:TF], scalar=hm4[:],
                                           in1=own_b[:], op0=ALU.mult, op1=ALU.add), wait=True)
                inc(v.tensor_tensor(out=own_c[:], in0=own_a[:], in1=own_b[:], op=ALU.add), wait=True)
                inc(v.tensor_scalar(out=own_c[:], in0=own_c[:], scalar1=-1.0,
                                    scalar2=None, op0=ALU.mult))
            pp.phase(vector=lf3)

            def st_rows(e, inc):
                inc(e.dma_start(out=grow_d, in_=igf[:]), 16)
                inc(e.dma_start(out=gown_d[0:4, :], in_=own_b[:]), 16)
                inc(e.dma_start(out=gown_d[4:8, :], in_=own_c[:]), 16)
            pp.phase(gpsimd=st_rows)

        with contextlib.ExitStack() as s7:
            class _Kit4:
                pass
            kit4 = _Kit4()
            kit4.nch, kit4.W = 4, QBF
            kit4.xbf = s7.enter_context(nc.sbuf_tensor("ln4_xbf", [128, 4 * QBF], BF16))
            kit4.d = s7.enter_context(nc.sbuf_tensor("ln4_d", [128, 4 * QBF], F32))
            kit4.rstd = s7.enter_context(nc.sbuf_tensor("ln4_rb", [128, QBF], F32))
            kit4.rstd_a = kit4.rstd
            khead = s7.enter_context(nc.sbuf_tensor("s7_kh", [128, 4 * TF], BF16))
            vhead = s7.enter_context(nc.sbuf_tensor("s7_vh", [128, 32 * 512], BF16))
            q_t = s7.enter_context(nc.sbuf_tensor("s7_q", [128, 4 * QBF], BF16))
            t32 = s7.enter_context(nc.sbuf_tensor("s7_t32", [32, 128], F32))
            igmF_tm = s7.enter_context(nc.sbuf_tensor("s7_itm", [128, 32], F32))
            rm_qb = s7.enter_context(nc.sbuf_tensor("s7_rm", [1, QBF], F32))
            en_qb = s7.enter_context(nc.sbuf_tensor("s7_en", [1, QBF], F32))
            cq_row = s7.enter_context(nc.sbuf_tensor("s7_cqr", [1, 4], F32))
            cq_b = s7.enter_context(nc.sbuf_tensor("s7_cqb", [128, 4], F32))
            ncq3 = s7.enter_context(nc.sbuf_tensor("s7_ncq", [128, 4], F32))
            colsc = s7.enter_context(nc.sbuf_tensor("s7_cs", [128, 36], F32))
            rf_row = s7.enter_context(nc.sbuf_tensor("s7_rfr", [1, QBF], F32))
            den_r = s7.enter_context(nc.sbuf_tensor("s7_den", [1, QBF], F32))
            nrm_r = den_r
            sc_row = s7.enter_context(nc.sbuf_tensor("s7_scr", [1, QBF], F32))
            screp = s7.enter_context(nc.sbuf_tensor("s7_srep", [128, QBF], F32))
            T_sb = s7.enter_context(nc.sbuf_tensor("s7_T", [128, QBF], BF16))
            T_sbB = s7.enter_context(nc.sbuf_tensor("s7_TB", [128, QBF], BF16))
            T2 = [T_sb, T_sbB]
            hfm = s7.enter_context(nc.sbuf_tensor("s7_hfm", [128, 4 * QBF], F32))
            hn4 = s7.enter_context(nc.sbuf_tensor("s7_hn", [128, 4 * QBF], BF16))
            mbigA = s7.enter_context(nc.sbuf_tensor("s7_mA", [128, 2 * QBF], BF16))
            mbigB = s7.enter_context(nc.sbuf_tensor("s7_mB", [128, 2 * QBF], BF16))
            ones_pv = s7.enter_context(nc.sbuf_tensor("s7_1pv", [128, 1], BF16))
            ps_den = s7.enter_context(nc.psum_tensor("s7_psd", [1, QBF], F32))
            ps_acc = [s7.enter_context(nc.psum_tensor(f"s7_pa{i}", [128, QBF], F32))
                      for i in range(4)]
            ps_s = pa
            ps_bc = pb
            PS2 = [pa, pb]

            def ld_masks(e, inc):
                inc(e.dma_start(out=mbigA[:], in_=mbigA_d), 16)
                inc(e.dma_start(out=mbigB[:], in_=mbigB_d), 16)
            pp.phase(sync=ld_masks)

            def mk1(v, inc):
                ins = v.memset(ones_pv[:], 1.0)
                inc(ins)
            pp.phase(vector=mk1)

            for n in range(4):
                def ld_head(e, inc, n=n):
                    e.wait_ge(pp.h["cc1"], cc1v)
                    e.wait_ge(pp.h["cc3"], cc3v)
                    for c in range(4):
                        r0 = n * 512 + c * 128
                        inc(e.dma_start(out=khead[:, c * TF: c * TF + TC],
                                        in_=ko_o[r0:r0 + 128, :]), 16)
                        inc(e.dma_start(out=khead[:, c * TF + TC: (c + 1) * TF],
                                        in_=kb_d[r0:r0 + 128, :]), 16)
                    for j in range(16):
                        inc(e.dma_start(out=vhead[:, j * 512:(j + 1) * 512],
                                        in_=vo_o[j * 128:(j + 1) * 128,
                                                 n * 512:(n + 1) * 512]), 16)
                    for j in range(16):
                        inc(e.dma_start(out=vhead[:, (16 + j) * 512:(17 + j) * 512],
                                        in_=vb_d[j * 128:(j + 1) * 128,
                                                 n * 512:(n + 1) * 512]), 16)
                    inc(e.dma_start(out=t32[:],
                                    in_=grow_d[n:n + 1, :].rearrange("one (a b) -> (one a) b", b=128)), 16)
                    inc(e.dma_start(out=cq_row[:],
                                    in_=gown_d[n:n + 1, QBF - 1:TC:QBF],
                                    allow_slow_non_contiguous=True), 16)
                pp.phase(sync=ld_head)

                def tp(t, inc):
                    ins = t.transpose(out=ps_bc[:, 0:32], in_=t32[:], identity=ident[0:32, 0:32])
                    inc(ins)
                pp.phase(tensor=tp)

                def cp_tm(v, inc):
                    ins = v.tensor_copy(out=igmF_tm[:], in_=ps_bc[:, 0:32])
                    inc(ins)
                pp.phase(vector=cp_tm)

                def cq2(t, inc):
                    ins = t.matmul(ps_bc[:, 0:4], consts["ones_row"], cq_row[:],
                                   start=True, stop=True)
                    inc(ins)
                pp.phase(tensor=cq2)

                def cq3(v, inc):
                    inc(v.tensor_copy(out=cq_b[:], in_=ps_bc[:, 0:4]), wait=True)
                    inc(v.tensor_scalar(out=ncq3[:], in0=cq_b[:], scalar1=CSC_SHIFT,
                                        scalar2=-1.0, op0=ALU.add, op1=ALU.mult))
                pp.phase(vector=cq3)

                for qb in range(4):
                    nk = 16 + 4 * qb + 4

                    def ld_qbrows(e, inc, n=n, qb=qb):
                        inc(e.dma_start(out=rm_qb[:],
                                        in_=gown_d[n:n + 1, qb * QBF:(qb + 1) * QBF]), 16)
                        inc(e.dma_start(out=en_qb[:],
                                        in_=gown_d[4 + n:5 + n, qb * QBF:(qb + 1) * QBF]), 16)
                    pp.phase(sync=ld_qbrows)

                    def gsc(s, inc, qb=qb, nk=nk):
                        s.activation(out=colsc[:, 0:nk], in_=igmF_tm[:, 0:nk], func=AF.Exp,
                                     bias=ncq3[:, qb:qb + 1])
                        inc(s.activation(out=en_qb[:], in_=en_qb[:], func=AF.Exp), wait=True)
                        ins = s.activation(out=rf_row[:], in_=rm_qb[:],
                                           func=AF.Exp, scale=-1.0,
                                           bias=cq_b[0:1, qb:qb + 1])
                        inc(ins)
                    pp.phase(scalar=gsc)

                    if 4 * qb + 4 < 16:
                        def gz(v, inc, qb=qb):
                            ins = v.tensor_scalar(out=colsc[:, 4 * qb + 4:16],
                                                  in0=colsc[:, 4 * qb + 4:16],
                                                  scalar1=cmask[:], scalar2=None,
                                                  op0=ALU.mult)
                            inc(ins)
                        pp.phase(vector=gz)

                    def ld_q(e, inc, n=n, qb=qb):
                        for c in range(4):
                            inc(e.dma_start(out=q_t[:, c * QBF:(c + 1) * QBF],
                                            in_=qb_d[n * 512 + c * 128: n * 512 + (c + 1) * 128,
                                                     qb * QBF:(qb + 1) * QBF]), 16)
                    pp.phase(sync=ld_q)

                    h, hc = pp.h, pp.hc
                    pre = dict(pp.cnt)
                    sc0, ev0, pv0 = hc["mm"], hc["ev"], hc["pv"]
                    jorder = [nk - 1 - k for k in range(nk)]

                    def body_pe(t, qb=qb, nk=nk, pre=pre, sc0=sc0, ev0=ev0, pv0=pv0,
                                jorder=jorder):
                        for o in ENGS:
                            if o != "tensor" and pre[o] > 0:
                                t.wait_ge(pp.sems[o], pre[o])

                        def pv(k, last):
                            j = jorder[k]
                            st_, sp_ = (k == 0), last
                            for fc in range(4):
                                t.matmul(ps_acc[fc][:, 0:QBF],
                                         vhead[:, j * 512 + fc * 128: j * 512 + (fc + 1) * 128],
                                         T2[k % 2][:], start=st_, stop=sp_,
                                         skip_group_check=True)
                            t.matmul(ps_den[:, 0:QBF], ones_pv[:], T2[k % 2][:],
                                     start=st_, stop=sp_,
                                     skip_group_check=True).then_inc(h["pv"], 1)

                        for k in range(nk):
                            j = jorder[k]
                            if k >= 2:
                                t.wait_ge(h["ev"], ev0 + k - 1)
                            for c in range(4):
                                ins = t.matmul(PS2[k % 2][:, 0:QBF],
                                               khead[:, c * TF + j * 128: c * TF + (j + 1) * 128],
                                               q_t[:, c * QBF:(c + 1) * QBF],
                                               start=(c == 0), stop=(c == 3),
                                               skip_group_check=True)
                            ins.then_inc(h["mm"], 1)
                            if k >= 1:
                                t.wait_ge(h["ev"], ev0 + k)
                                pv(k - 1, last=False)
                        t.wait_ge(h["ev"], ev0 + nk)
                        pv(nk - 1, last=True)
                    pp.block.tensor(body_pe)
                    hc["mm"] += nk
                    hc["pv"] += nk

                    def body_ve(v, qb=qb, nk=nk, pre=pre, sc0=sc0, pv0=pv0,
                                jorder=jorder):
                        for o in ENGS:
                            if o != "vector" and pre[o] > 0:
                                v.wait_ge(pp.sems[o], pre[o])
                        for k in range(nk):
                            j = jorder[k]
                            v.wait_ge(h["mm"], sc0 + k + 1)
                            if k >= 2:
                                v.wait_ge(h["pv"], pv0 + k - 1)
                            rA = j - 4 * qb
                            rB = j - 16 - 4 * qb
                            if 0 <= rA <= 3:
                                ins = v.scalar_tensor_tensor(
                                    out=T2[k % 2][:], in0=PS2[k % 2][:, 0:QBF],
                                    scalar=colsc[:, j:j + 1],
                                    in1=mbigA[:, QBF - 128 * rA: 2 * QBF - 128 * rA],
                                    op0=ALU.mult, op1=ALU.mult)
                            elif 0 <= rB <= 3:
                                ins = v.scalar_tensor_tensor(
                                    out=T2[k % 2][:], in0=PS2[k % 2][:, 0:QBF],
                                    scalar=colsc[:, j:j + 1],
                                    in1=mbigB[:, QBF - 128 * rB: 2 * QBF - 128 * rB],
                                    op0=ALU.mult, op1=ALU.mult)
                            else:
                                ins = v.tensor_scalar(out=T2[k % 2][:], in0=PS2[k % 2][:, 0:QBF],
                                                      scalar1=colsc[:, j:j + 1],
                                                      scalar2=None, op0=ALU.mult)
                            ins.then_inc(h["ev"], 1)
                    pp.block.vector(body_ve)
                    hc["ev"] += nk

                    def mark_t(t, hcpv=hc["pv"]):
                        t.wait_ge(h["pv"], hcpv)
                        t.nop().then_inc(pp.sems["tensor"], 1)
                    pp.block.tensor(mark_t)
                    pp.cnt["tensor"] += 1

                    def mark_v(v, hcev=hc["ev"]):
                        v.wait_ge(h["ev"], hcev)
                        v.nop().then_inc(pp.sems["vector"], 1)
                    pp.block.vector(mark_v)
                    pp.cnt["vector"] += 1

                    def ep1(v, inc, qb=qb):
                        inc(v.tensor_tensor(out=den_r[:], in0=ps_den[:, 0:QBF], in1=rf_row[:],
                                            op=ALU.mult))
                    pp.phase(vector=ep1)

                    def ep1b(s, inc):
                        inc(s.activation(out=nrm_r[:], in_=den_r[:], func=AF.Abs))
                    pp.phase(scalar=ep1b)

                    def ep1c(v, inc):
                        inc(v.tensor_tensor(out=nrm_r[:], in0=nrm_r[:],
                                            in1=en_qb[:], op=ALU.max))
                    pp.phase(vector=ep1c)

                    def ep2(s, inc):
                        inc(s.activation(out=nrm_r[:], in_=nrm_r[:], func=AF.Ln,
                                         bias=consts["m1e6"][0:1, :]), wait=True)
                        inc(s.activation(out=nrm_r[:], in_=nrm_r[:], func=AF.Exp, scale=-1.0))
                    pp.phase(scalar=ep2)

                    def ep3(v, inc):
                        inc(v.tensor_tensor(out=sc_row[:], in0=rf_row[:], in1=nrm_r[:],
                                            op=ALU.mult))
                    pp.phase(vector=ep3)

                    def ep4(t, inc):
                        ins = t.matmul(ps_bc[:, 0:QBF], consts["ones_row"], sc_row[:],
                                       start=True, stop=True)
                        inc(ins)
                    pp.phase(tensor=ep4)

                    def ep5(v, inc):
                        inc(v.tensor_copy(out=screp[:], in_=ps_bc[:, 0:QBF]), wait=True)
                        for fc in range(4):
                            inc(v.tensor_tensor(out=ch(hfm, fc, QBF), in0=ps_acc[fc][:, 0:QBF],
                                                in1=screp[:], op=ALU.mult), wait=True)
                        for fc in range(4):
                            ins = v.tensor_copy(out=ch(kit4.xbf, fc, QBF), in_=ch(hfm, fc, QBF))
                        inc(ins)
                    pp.phase(vector=ep5)

                    def mmu(t, inc):
                        for fc in range(4):
                            ins = t.matmul(ps_s[:, 0:QBF], consts["ones_bf"], ch(kit4.xbf, fc, QBF),
                                           start=(fc == 0), stop=(fc == 3))
                        inc(ins)
                    pp.phase(tensor=mmu)

                    def msub(v, inc):
                        for fc in range(4):
                            ins = v.scalar_tensor_tensor(out=ch(kit4.d, fc, QBF), in0=ps_s[:, 0:QBF],
                                                         scalar=-1.0 / DH, in1=ch(hfm, fc, QBF),
                                                         op0=ALU.mult, op1=ALU.add)
                        inc(ins)
                    pp.phase(vector=msub)

                    def msq(s, inc):
                        for fc in range(4):
                            ins = s.activation(out=ch(kit4.xbf, fc, QBF), in_=ch(kit4.d, fc, QBF),
                                               func=AF.Square)
                        inc(ins)
                    pp.phase(scalar=msq)

                    def mvar(t, inc):
                        for fc in range(4):
                            ins = t.matmul(ps_bc[:, 0:QBF], consts["ones_bf"], ch(kit4.xbf, fc, QBF),
                                           start=(fc == 0), stop=(fc == 3))
                        inc(ins)
                    pp.phase(tensor=mvar)

                    def mrstd(s, inc):
                        inc(s.activation(out=kit4.rstd[:], in_=ps_bc[:, 0:QBF], func=AF.Ln,
                                         scale=1.0 / DH, bias=consts["eps6"]), wait=True)
                        inc(s.activation(out=kit4.rstd[:], in_=kit4.rstd[:], func=AF.Exp,
                                         scale=-0.5))
                    pp.phase(scalar=mrstd)

                    def mout(v, inc, n=n):
                        for fc in range(4):
                            ins = v.scalar_tensor_tensor(out=hn4[:, fc * QBF:(fc + 1) * QBF],
                                                         in0=ch(kit4.d, fc, QBF),
                                                         scalar=nw_sb[:, n * 4 + fc:n * 4 + fc + 1],
                                                         in1=kit4.rstd[:],
                                                         op0=ALU.mult, op1=ALU.mult)
                        inc(ins)
                    pp.phase(vector=mout)

                    def mst(e, inc, n=n, qb=qb):
                        for fc in range(4):
                            inc(e.dma_start(out=hnb_d[n * 512 + fc * 128: n * 512 + (fc + 1) * 128,
                                                      qb * QBF:(qb + 1) * QBF],
                                            in_=hn4[:, fc * QBF:(fc + 1) * QBF]), 16)
                    pp.phase(gpsimd=mst)

        with contextlib.ExitStack() as s8:
            w_t = s8.enter_context(nc.sbuf_tensor("s8_wt", [128, 2 * 16 * 128], BF16))
            xa_t = s8.enter_context(nc.sbuf_tensor("s8_xa", [128, 16 * TT], BF16))
            z_t = s8.enter_context(nc.sbuf_tensor("s8_z", [128, 16 * TT], BF16))
            hn_t = s8.enter_context(nc.sbuf_tensor("s8_hn", [128, 16 * TT], BF16))
            ao_f = s8.enter_context(nc.sbuf_tensor("s8_ao", [128, 8 * TT], F32))
            sz = s8.enter_context(nc.sbuf_tensor("s8_sz", [128, 2 * TT], F32))
            tmf = s8.enter_context(nc.sbuf_tensor("s8_tmf", [128, TT], F32))

            for tt in range(NT):
                def load_t(e, inc, tt=tt):
                    load_fm(e, inc, xa_t, xab_d, 16, TT, tt * TT)
                    load_fm(e, inc, z_t, zb_d, 16, TT, tt * TT)
                    load_fm(e, inc, hn_t, hnb_d, 16, TT, tt * TT)
                pp.phase(sync=load_t)

                def hsilu(sc, i, s):
                    return sc.activation(out=sz[:, s * TT:(s + 1) * TT],
                                         in_=ch(z_t, i, TT), func=AF.Silu)

                def hsmix(v, i, s):
                    pp.chain(v, v.scalar_tensor_tensor(out=tmf[:], in0=ch(xa_t, i, TT),
                                                       scalar=skip_sb[:, i:i + 1],
                                                       in1=ch(hn_t, i, TT),
                                                       op0=ALU.mult, op1=ALU.add))
                    return v.tensor_tensor(out=ch(hn_t, i, TT), in0=tmf[:],
                                           in1=sz[:, s * TT:(s + 1) * TT], op=ALU.mult)

                pp.pipe2(16, hsilu, ev=hsmix, mm_eng="scalar", ev_eng="vector")

                def ldd(e, i, s):
                    for kc in range(16):
                        ins = e.dma_start(out=w_t[:, s * 2048 + kc * 128: s * 2048 + (kc + 1) * 128],
                                          in_=wdbf_d[kc * 128:(kc + 1) * 128,
                                                     i * 128:(i + 1) * 128])
                        if kc < 15:
                            ins.then_inc(pp.h["ld"], 16)
                    return ins

                def mmd(t, i, s):
                    ps = pa if s == 0 else pb
                    for kc in range(16):
                        ins = t.matmul(ps[:, 0:TT],
                                       w_t[:, s * 2048 + kc * 128: s * 2048 + (kc + 1) * 128],
                                       ch(hn_t, kc, TT), start=(kc == 0), stop=(kc == 15),
                                       skip_group_check=True)
                    return ins

                def evd(v, i, s):
                    ps = pa if s == 0 else pb
                    return v.tensor_copy(out=ch(ao_f, i, TT), in_=ps[:, 0:TT])

                pp.pipe2(8, mmd, ev=evd, ld=ldd, ld_units=16)

                emit_ln(pp, kit, consts, ao_f, pa, pb, out_f32=ao_f)

                def addres(v, inc, tt=tt):
                    for c in range(8):
                        ins = v.tensor_tensor(out=sl_t(c, tt), in0=sl_t(c, tt),
                                              in1=ch(ao_f, c, TT), op=ALU.add)
                    inc(ins)
                pp.phase(vector=addres)

        dump_stream("dbg_x3")
        if DBG:
            def dump_rows(e, inc):
                inc(e.dma_start(out=dbg["dbg_gown"], in_=gown_d), 16)
                inc(e.dma_start(out=dbg["dbg_gb"], in_=gb_d), 16)
                for r0 in range(0, 2048, 128):
                    inc(e.dma_start(out=dbg["dbg_xa"][r0:r0 + 128, :],
                                    in_=xab_d[r0:r0 + 128, :]), 16)
                    inc(e.dma_start(out=dbg["dbg_kb"][r0:r0 + 128, :],
                                    in_=kb_d[r0:r0 + 128, :]), 16)
                    inc(e.dma_start(out=dbg["dbg_vb"][r0:r0 + 128, :],
                                    in_=vb_d[r0:r0 + 128, :]), 16)
                    inc(e.dma_start(out=dbg["dbg_qb"][r0:r0 + 128, :],
                                    in_=qb_d[r0:r0 + 128, :]), 16)
                inc(e.dma_start(out=dbg["dbg_grow"], in_=grow_d), 16)
                for r0 in range(0, 2048, 128):
                    inc(e.dma_start(out=dbg["dbg_hn"][r0:r0 + 128, :],
                                    in_=hnb_d[r0:r0 + 128, :]), 16)
            pp.phase(gpsimd=dump_rows)

        with contextlib.ExitStack() as s9:
            w1t = s9.enter_context(nc.sbuf_tensor("s9_w1t", [128, 2 * 1024], BF16))
            w2t = s9.enter_context(nc.sbuf_tensor("s9_w2t", [128, 2 * C], BF16))
            b1_sb = s9.enter_context(nc.sbuf_tensor("s9_b1", [128, 32], F32))
            b2_sb = s9.enter_context(nc.sbuf_tensor("s9_b2", [128, 8], F32))
            fin_bf = s9.enter_context(nc.sbuf_tensor("s9_fin", [128, 8 * TT], BF16))
            hh_bf = s9.enter_context(nc.sbuf_tensor("s9_hh", [128, 32 * TT], BF16))
            ffn_f = s9.enter_context(nc.sbuf_tensor("s9_ffn", [128, 8 * TT], F32))
            pcs = [s9.enter_context(nc.psum_tensor(f"s9_pc{i}", [128, TT], F32))
                   for i in range(4)]
            pcs8 = pcs + [s9.enter_context(nc.psum_tensor(f"s9_pd{i}", [128, TT], F32))
                          for i in range(2)] + [pa, pb]

            def load_b(e, inc):
                inc(e.dma_start(out=b1_sb[:], in_=b1_d), 16)
                inc(e.dma_start(out=b2_sb[:], in_=b2_d), 16)
            pp.phase(sync=load_b)

            for tt in range(NT):
                emit_stream_ln(out_bf=fin_bf, tt=tt)

                def ld1(e, i, s):
                    for kc in range(8):
                        ins = e.dma_start(out=w1t[:, s * 1024 + kc * 128: s * 1024 + (kc + 1) * 128],
                                          in_=w1bf_d[kc * 128:(kc + 1) * 128,
                                                     i * 128:(i + 1) * 128])
                        if kc < 7:
                            ins.then_inc(pp.h["ld"], 16)
                    return ins

                def mm1(t, i, s):
                    ps = pa if s == 0 else pb
                    for kc in range(8):
                        ins = t.matmul(ps[:, 0:TT],
                                       w1t[:, s * 1024 + kc * 128: s * 1024 + (kc + 1) * 128],
                                       ch(fin_bf, kc, TT), start=(kc == 0), stop=(kc == 7),
                                       skip_group_check=True)
                    return ins

                def gl1(sc, i, s):
                    ps = pa if s == 0 else pb
                    return sc.activation(out=ch(hh_bf, i, TT), in_=ps[:, 0:TT],
                                         func=AF.Gelu, bias=b1_sb[:, i:i + 1])

                pp.pipe2(32, mm1, ev=gl1, ld=ld1, ev_eng="scalar", ld_units=8)

                def ld2(e, i, s):
                    return e.dma_start(out=w2t[:, s * C:(s + 1) * C],
                                       in_=w2bf_d[i * 128:(i + 1) * 128, :])

                def mm2(t, i, s):
                    for dc in range(8):
                        ins = t.matmul(pcs8[dc][:, 0:TT],
                                       w2t[:, s * C + dc * 128: s * C + (dc + 1) * 128],
                                       ch(hh_bf, i, TT),
                                       start=(i == 0), stop=(i == 31),
                                       skip_group_check=True)
                    return ins

                pp.pipe2(32, mm2, ld=ld2)

                def ev2(v, inc):
                    for dc in range(8):
                        ins = v.tensor_scalar(out=ch(ffn_f, dc, TT), in0=pcs8[dc][:, 0:TT],
                                              scalar1=b2_sb[:, dc:dc + 1],
                                              scalar2=None, op0=ALU.add)
                    inc(ins)
                pp.phase(vector=ev2)

                emit_ln(pp, kit, consts, ffn_f, pa, pb, out_f32=ffn_f)

                def addres(v, inc, tt=tt):
                    for c in range(8):
                        ins = v.tensor_tensor(out=sl_t(c, tt), in0=sl_t(c, tt),
                                              in1=ch(ffn_f, c, TT), op=ALU.add)
                    inc(ins)
                pp.phase(vector=addres)

                def store(e, inc, tt=tt):
                    for c in range(8):
                        inc(e.dma_start(out=y_d[c * 128:(c + 1) * 128, tt * TT:(tt + 1) * TT],
                                        in_=sl_t(c, tt)), 16)
                pp.phase(gpsimd=store)

        pp.finish()
    return nc



def _asc(a):
    return np.ascontiguousarray(a)


def _blockdiag(W):
    blocks = W.reshape(16, 32, 4, 4)
    Bd = np.zeros((16, 128, 128), np.float32)
    BdT = np.zeros((16, 128, 128), np.float32)
    for a in range(32):
        Bd[:, 4 * a:4 * a + 4, 4 * a:4 * a + 4] = blocks[:, a].transpose(0, 2, 1)
        BdT[:, 4 * a:4 * a + 4, 4 * a:4 * a + 4] = blocks[:, a]
    return Bd.reshape(16 * 128, 128), BdT.reshape(16 * 128, 128)


def _cols(vec, nch):
    return _asc(np.asarray(vec, np.float32).reshape(nch, 128).T)


_CACHE = {}


def _get(name, builder):
    if name not in _CACHE:
        _CACHE[name] = builder()
    return _CACHE[name]


def kernel_unfused(**inputs):
    f32 = np.float32
    x0 = np.asarray(inputs["L0_attn_x"], f32)
    cores = [(b, h) for b in range(B) for h in range(2)]

    wiT = _asc(np.asarray(inputs["L0_attn_Wi"], f32).T)
    wrT = _asc(np.asarray(inputs["L0_attn_Wr"], f32).T)
    bi = _cols(inputs["L0_attn_bi"], 8)
    br = _cols(inputs["L0_attn_br"], 8)
    lam = np.asarray(inputs["L0_attn_cap_lambda"], f32).reshape(1, 1)
    im1 = []
    for (b, h) in cores:
        xs = _asc(x0[b, h * TC:(h + 1) * TC, :].T)
        im1.append({"xs": xs, "wiT": wiT, "wrT": wrT, "bi": bi, "br": br, "lam": lam})
    r1 = run_bass_kernel_spmd(_get("l1", build_l1_gates), im1,
                              core_ids=list(range(NCORES))).results

    at_fm = [np.concatenate([r1[2 * b]["at_o"], r1[2 * b + 1]["at_o"]], axis=1)
             for b in range(B)]
    gx_fm = [np.concatenate([r1[2 * b]["gx_o"], r1[2 * b + 1]["gx_o"]], axis=1)
             for b in range(B)]

    im2 = []
    for b in range(B):
        for f in range(2):
            im2.append({"at_i": _asc(at_fm[b][f * 512:(f + 1) * 512]),
                        "gx_i": _asc(gx_fm[b][f * 512:(f + 1) * 512])})
    r2 = run_bass_kernel_spmd(_get("l2", build_l2_scan), im2,
                              core_ids=list(range(NCORES))).results
    ys_fm = [np.concatenate([r2[2 * b]["ys_o"], r2[2 * b + 1]["ys_o"]], axis=0)
             for b in range(B)]

    th = _asc(np.asarray(inputs["L0_ffn_theta"], f32))
    thn = _asc(np.asarray(inputs["L0_ffn_theta_noise"], f32))
    phm = _asc(np.asarray(inputs["L0_ffn_phi"], f32))
    phn = _asc(np.asarray(inputs["L0_ffn_phi_noise"], f32))
    im3 = []
    for (b, h) in cores:
        im3.append({"xs": _asc(x0[b, h * TC:(h + 1) * TC, :].T),
                    "ys": _asc(ys_fm[b][:, h * TC:(h + 1) * TC]),
                    "th": th, "thn": thn, "ph": phm, "phn": phn})
    r3 = run_bass_kernel_spmd(_get("l3", build_l3_ffn0), im3,
                              core_ids=list(range(NCORES))).results
    x2 = [r3[i]["x2_o"] for i in range(NCORES)]

    wup = _asc(np.asarray(inputs["L1_attn_W_up"], f32))
    im4a = []
    for i, (b, h) in enumerate(cores):
        x2h = np.zeros((C, TH), f32)
        x2h[:, 512:] = x2[i]
        if h == 1:
            x2h[:, 509:512] = x2[2 * b][:, TC - 3:]
        im4a.append({"x2h": x2h, "wup": wup})
    r4a = run_bass_kernel_spmd(_get("l4a", build_l4a_up), im4a,
                               core_ids=list(range(NCORES))).results

    bq, bqT = _blockdiag(np.asarray(inputs["L1_attn_Wq"], f32))
    bk, bkT = _blockdiag(np.asarray(inputs["L1_attn_Wk"], f32))
    bv, bvT = _blockdiag(np.asarray(inputs["L1_attn_Wv"], f32))
    W_ig = np.asarray(inputs["L1_attn_W_ig"], f32)
    W_fg = np.asarray(inputs["L1_attn_W_fg"], f32)
    wig_q = _asc(np.concatenate([W_ig[0:2048], W_fg[0:2048]], axis=1))
    wig_k = _asc(np.concatenate([W_ig[2048:4096], W_fg[2048:4096]], axis=1))
    wig_v = _asc(np.concatenate([W_ig[4096:6144], W_fg[4096:6144]], axis=1))
    bg = _asc(np.concatenate([np.asarray(inputs["L1_attn_b_ig"], f32),
                              np.asarray(inputs["L1_attn_b_fg"], f32)]).reshape(8, 1))
    cw = _asc(np.asarray(inputs["L1_attn_conv_w"], f32)
              .reshape(4, 16, 128).transpose(2, 1, 0).reshape(128, 64))
    cb = _cols(inputs["L1_attn_conv_b"], 16)
    im4b = []
    for i in range(NCORES):
        im4b.append({"xml": r4a[i]["xml_o"], "bq": bq, "bk": bk, "bv": bv,
                     "bqT": bqT, "bkT": bkT, "bvT": bvT,
                     "wig_q": wig_q, "wig_k": wig_k, "wig_v": wig_v,
                     "bg": bg, "cw": cw, "cb": cb})
    r4b = run_bass_kernel_spmd(_get("l4b", build_l4b_qkv), im4b,
                               core_ids=list(range(NCORES))).results

    norm_w = np.asarray(inputs["L1_attn_norm_w"], f32)
    ident = np.eye(128, dtype=f32)
    mbig = (np.arange(512)[None, :] >= (np.arange(128)[:, None] + 256)).astype(BF)
    im5 = []
    for i in range(NCORES):
        b, npair = i // 2, i % 2
        m = {"ident_in": ident, "mbig_in": mbig}
        for u in range(2):
            n = 2 * npair + u
            c0, c1 = 2 * b, 2 * b + 1
            m[f"qf{u}"] = _asc(np.concatenate(
                [r4b[c0]["q_o"][512 * n:512 * (n + 1)],
                 r4b[c1]["q_o"][512 * n:512 * (n + 1)]], axis=1))
            m[f"kf{u}"] = _asc(np.concatenate(
                [r4b[c0]["k_o"][512 * n:512 * (n + 1)],
                 r4b[c1]["k_o"][512 * n:512 * (n + 1)]], axis=1))
            vfull = np.concatenate([r4b[c0]["v_o"], r4b[c1]["v_o"]], axis=0)
            m[f"v1{u}"] = _asc(vfull[:, 512 * n:512 * (n + 1)])
            m[f"ig{u}"] = _asc(np.concatenate(
                [r4b[c0]["ig_o"][n:n + 1], r4b[c1]["ig_o"][n:n + 1]], axis=1))
            m[f"fg{u}"] = _asc(np.concatenate(
                [r4b[c0]["fg_o"][n:n + 1], r4b[c1]["fg_o"][n:n + 1]], axis=1))
            m[f"nw{u}"] = _asc(np.tile(norm_w[None, 512 * n:512 * (n + 1)], (128, 1)))
        im5.append(m)
    r5 = run_bass_kernel_spmd(_get("l5", build_l5_attn), im5,
                              core_ids=list(range(NCORES))).results

    skip_sb = _cols(inputs["L1_attn_skip"], 16)
    wdn = _asc(np.asarray(inputs["L1_attn_W_down"], f32))
    im6a = []
    for i, (b, h) in enumerate(cores):
        hn_parts = []
        for n in range(4):
            core5 = 2 * b + n // 2
            u = n % 2
            hn_parts.append(r5[core5][f"hn{u}"][h * TC:(h + 1) * TC, :])
        hn_tm = np.concatenate(hn_parts, axis=1)
        im6a.append({"x2": x2[i], "xact": r4b[i]["xact_o"], "z": r4a[i]["z_o"],
                     "hn": _asc(hn_tm.T), "skip": skip_sb, "wdn": wdn})
    r6a = run_bass_kernel_spmd(_get("l6a", build_l6a_post), im6a,
                               core_ids=list(range(NCORES))).results

    w1 = _asc(np.asarray(inputs["L1_ffn_w1"], f32))
    b1c = _cols(inputs["L1_ffn_b1"], 32)
    w2 = _asc(np.asarray(inputs["L1_ffn_w2"], f32))
    b2c = _cols(inputs["L1_ffn_b2"], 8)
    im6b = [{"x3": r6a[i]["x3_o"], "w1": w1, "b1": b1c, "w2": w2, "b2": b2c}
            for i in range(NCORES)]
    r6b = run_bass_kernel_spmd(_get("l6b", build_l6b_ffn1), im6b,
                               core_ids=list(range(NCORES))).results

    out = np.empty((B, TF, C), f32)
    for i, (b, h) in enumerate(cores):
        out[b, h * TC:(h + 1) * TC, :] = r6b[i]["x4_o"].T
    return out


def kernel(**inputs):
    f32 = np.float32
    x0 = np.asarray(inputs["L0_attn_x"], f32)
    cores = [(b, h) for b in range(B) for h in range(2)]

    wiT = _asc(np.asarray(inputs["L0_attn_Wi"], f32).T)
    wrT = _asc(np.asarray(inputs["L0_attn_Wr"], f32).T)
    bi = _cols(inputs["L0_attn_bi"], 8)
    br = _cols(inputs["L0_attn_br"], 8)
    lam = np.asarray(inputs["L0_attn_cap_lambda"], f32).reshape(1, 1)
    th = _asc(np.asarray(inputs["L0_ffn_theta"], f32))
    thn = _asc(np.asarray(inputs["L0_ffn_theta_noise"], f32))
    phm = _asc(np.asarray(inputs["L0_ffn_phi"], f32))
    phn = _asc(np.asarray(inputs["L0_ffn_phi_noise"], f32))
    wup = _asc(np.asarray(inputs["L1_attn_W_up"], f32))
    bq, bqT = _blockdiag(np.asarray(inputs["L1_attn_Wq"], f32))
    bk, bkT = _blockdiag(np.asarray(inputs["L1_attn_Wk"], f32))
    bv, bvT = _blockdiag(np.asarray(inputs["L1_attn_Wv"], f32))
    W_ig = np.asarray(inputs["L1_attn_W_ig"], f32)
    W_fg = np.asarray(inputs["L1_attn_W_fg"], f32)
    wig_q = _asc(np.concatenate([W_ig[0:2048], W_fg[0:2048]], axis=1))
    wig_k = _asc(np.concatenate([W_ig[2048:4096], W_fg[2048:4096]], axis=1))
    wig_v = _asc(np.concatenate([W_ig[4096:6144], W_fg[4096:6144]], axis=1))
    bg = _asc(np.concatenate([np.asarray(inputs["L1_attn_b_ig"], f32),
                              np.asarray(inputs["L1_attn_b_fg"], f32)]).reshape(8, 1))
    cw = _asc(np.asarray(inputs["L1_attn_conv_w"], f32)
              .reshape(4, 16, 128).transpose(2, 1, 0).reshape(128, 64))
    cb = _cols(inputs["L1_attn_conv_b"], 16)
    nw = _cols(inputs["L1_attn_norm_w"], 16)
    skip = _cols(inputs["L1_attn_skip"], 16)
    wdn = _asc(np.asarray(inputs["L1_attn_W_down"], f32))
    w1 = _asc(np.asarray(inputs["L1_ffn_w1"], f32))
    b1c = _cols(inputs["L1_ffn_b1"], 32)
    w2 = _asc(np.asarray(inputs["L1_ffn_w2"], f32))
    b2c = _cols(inputs["L1_ffn_b2"], 8)
    ident = np.eye(128, dtype=f32)
    mwin = (np.arange(2 * QBF)[None, :] >= (np.arange(128)[:, None] + QBF)).astype(BF)
    ones_w = np.ones((128, 2 * QBF), BF)
    pois_h0 = np.zeros((4, TF), f32); pois_h0[:, TC:] = -1.0e30
    pois_h1 = np.zeros((4, TF), f32)

    im = []
    for (b, h) in cores:
        m = dict(xs=_asc(x0[b, h * TC:(h + 1) * TC, :].T), wiT=wiT, wrT=wrT,
                 bi=bi, br=br, lam=lam, th=th, thn=thn, ph=phm, phn=phn, wup=wup,
                 bq=bq, bk=bk, bv=bv, bqT=bqT, bkT=bkT, bvT=bvT,
                 wig_q=wig_q, wig_k=wig_k, wig_v=wig_v, bg=bg, cw=cw, cb=cb,
                 nw=nw, skip=skip, wdn=wdn, w1=w1, b1=b1c, w2=w2, b2=b2c,
                 ident=ident)
        if h == 0:
            m["poison"] = pois_h0
            m["mbigA"] = mwin
            m["mbigB"] = ones_w
            m["cmask"] = np.zeros((128, 1), f32)
            m["hm4"] = np.zeros((4, 1), f32)
            m["hm4c"] = np.ones((4, 1), f32)
        else:
            m["poison"] = pois_h1
            m["mbigA"] = ones_w
            m["mbigB"] = mwin
            m["cmask"] = np.ones((128, 1), f32)
            m["hm4"] = np.ones((4, 1), f32)
            m["hm4c"] = np.zeros((4, 1), f32)
        im.append(m)

    res = run_bass_kernel_spmd(_get("fused", build_fused), im,
                               core_ids=list(range(NCORES))).results
    out = np.empty((B, TF, C), f32)
    for i, (b, h) in enumerate(cores):
        out[b, h * TC:(h + 1) * TC, :] = res[i]["y"].T
    return out
